# version 12
# baseline (speedup 1.0000x reference)
import contextlib
import math

import numpy as np

import concourse.bass as bass
import concourse.mybir as mybir
from concourse.bass_utils import run_bass_kernel_spmd

F32 = mybir.dt.float32
BF16 = mybir.dt.bfloat16
FP16 = mybir.dt.float16
FP8 = mybir.dt.float8e4
I32 = mybir.dt.int32
AF = mybir.ActivationFunctionType
ALU = mybir.AluOpType
AX = mybir.AxisListType

B, T, E = 4, 2048, 2048
H, D = 16, 128
HID = 8192
TC = 1024
NE = E // 128
NH = HID // 128
NPAIR = 4

_DEBUG = False
_PHASES = 10


class Sem:
    def __init__(self, stack, nc, name):
        self.h = stack.enter_context(nc.semaphore(name))
        self.v = 0


def build_program():
    nc = bass.Bass("TRN2", target_bir_lowering=False, detect_race_conditions=False)

    xT = nc.dram_tensor("xT", [E, T], F32, kind="ExternalInput")
    xTm = nc.dram_tensor("xTm", [E, TC], F32, kind="ExternalInput")
    wqT = nc.dram_tensor("wqT", [E, E], F32, kind="ExternalInput")
    wkT = nc.dram_tensor("wkT", [E, E], F32, kind="ExternalInput")
    wvT = nc.dram_tensor("wvT", [E, E], F32, kind="ExternalInput")
    woT = nc.dram_tensor("woT", [E, E], F32, kind="ExternalInput")
    w1T = nc.dram_tensor("w1T", [E, HID], F32, kind="ExternalInput")
    w2T = nc.dram_tensor("w2T", [HID, E], F32, kind="ExternalInput")
    tabs = {}
    for nm, cols in [("cosq", TC), ("ssq", TC), ("cosk", T), ("ssk", T)]:
        for p in ("hi", "lo"):
            tabs[nm + p] = nc.dram_tensor(nm + p, [D, cols], F32, kind="ExternalInput")
    maskp = nc.dram_tensor("maskp", [128, NPAIR * 4 * 256], FP16, kind="ExternalInput")
    identb = nc.dram_tensor("identb", [128, 128], FP16, kind="ExternalInput")
    identf = nc.dram_tensor("identf", [128, 128], F32, kind="ExternalInput")
    out_d = nc.dram_tensor("out", [E, TC], F32, kind="ExternalOutput")
    w2q_hbm = nc.dram_tensor("w2q_hbm", [NH * 128, E], FP8)
    ynorm_hbm = nc.dram_tensor("ynorm_hbm", [E, TC], F32)
    xq1_hbm = nc.dram_tensor("xq1_hbm", [E, TC], BF16)
    r1_hbm = nc.dram_tensor("r1_hbm", [128, TC], F32)

    dbg = {}
    if _DEBUG:
        for nm, sh, dt in [("d_k", [D, T], FP16), ("d_q", [D, TC], FP16),
                           ("d_vt", [128, 16 * 128], FP16), ("d_ctx", [128, H * TC], FP16),
                           ("d_y1", [128, NE * TC], F32), ("d_xq1", [128, NE * TC], BF16),
                           ("d_g", [128, 8 * TC], FP16), ("d_s1", [128, TC], F32),
                           ("d_rows", [1, 2 * TC], F32), ("d_mrun", [128, TC], F32)]:
            dbg[nm] = nc.dram_tensor(nm, sh, dt, kind="ExternalOutput")

    top = contextlib.ExitStack()
    s_in = Sem(top, nc, "s_in")
    s_g = Sem(top, nc, "s_g")
    s_t = Sem(top, nc, "s_t")
    s_v = Sem(top, nc, "s_v")
    s_s = Sem(top, nc, "s_s")
    s_o = Sem(top, nc, "s_o")
    s_p0 = Sem(top, nc, "s_p0")
    s_p1 = Sem(top, nc, "s_p1")
    s_q0 = Sem(top, nc, "s_q0")
    s_q1 = Sem(top, nc, "s_q1")

    ones_f = top.enter_context(nc.sbuf_tensor("ones_f", [128, 1], F32))
    ones_b = top.enter_context(nc.sbuf_tensor("ones_b", [128, 1], FP16))
    onesL_f = top.enter_context(nc.sbuf_tensor("onesL_f", [1, 128], F32))
    negb12 = top.enter_context(nc.sbuf_tensor("negb12", [128, 1], F32))
    idb = top.enter_context(nc.sbuf_tensor("idb", [128, 128], FP16))
    idf = top.enter_context(nc.sbuf_tensor("idf", [128, 128], F32))
    w1s_pp = top.enter_context(nc.sbuf_tensor("w1s_pp", [128, NH], F32))
    n1_pp = top.enter_context(nc.sbuf_tensor("n1_pp", [128, NH], F32))
    w2s_pp = top.enter_context(nc.sbuf_tensor("w2s_pp", [128, NE], F32))
    cstack0 = contextlib.ExitStack()
    ctx_bf = cstack0.enter_context(nc.sbuf_tensor("ctx_bf", [128, H * TC], FP16))

    heads_stack = contextlib.ExitStack()
    tabsb = {nm: heads_stack.enter_context(nc.sbuf_tensor("t_" + nm, [D, cols], F32))
             for nm, cols in [("cosq", TC), ("ssq", TC), ("cosk", T), ("ssk", T)]}
    x_bf = heads_stack.enter_context(nc.sbuf_tensor("x_bf", [128, NE * T], FP16))
    xm_bf = heads_stack.enter_context(nc.sbuf_tensor("xm_bf", [128, NE * TC], FP16))
    mask_sb = heads_stack.enter_context(nc.sbuf_tensor("mask_sb", [128, NPAIR * 4 * 256], FP16))


    def _early_out(tag):
        with contextlib.ExitStack() as p0:
            dummy = p0.enter_context(nc.sbuf_tensor("dummy" + tag, [128, 256], F32))
            with nc.Block() as blk:
                @blk.vector
                def _(vec):
                    vec.memset(dummy[:, :], 0.0).then_inc(s_v.h, 1)
                    s_v.v += 1

                @blk.sync
                def _(sync):
                    sync.wait_ge(s_v.h, s_v.v)
                    for fc in range(NE):
                        for tsx in range(4):
                            sync.dma_start(out=bass.AP(out_d, fc * 128 * TC + tsx * 256,
                                                       [[TC, 128], [1, 256]]),
                                           in_=dummy[:, :]).then_inc(s_o.h, 16)
                            s_o.v += 16
                    sync.wait_ge(s_o.h, s_o.v)
        return nc

    with contextlib.ExitStack() as ph:
        tlo = {nm: ph.enter_context(nc.sbuf_tensor("tl_" + nm, [D, cols], F32))
               for nm, cols in [("cosq", TC), ("ssq", TC), ("cosk", T), ("ssk", T)]}
        with nc.Block() as blk:
            @blk.sync
            def _(sync):
                for nm in ("cosq", "ssq", "cosk", "ssk"):
                    sync.dma_start(out=tabsb[nm][:], in_=tabs[nm + "hi"].ap()[:, :]).then_inc(s_in.h, 16)
                    sync.dma_start(out=tlo[nm][:], in_=tabs[nm + "lo"].ap()[:, :]).then_inc(s_in.h, 16)
                    s_in.v += 32
                sync.dma_start(out=mask_sb[:], in_=maskp.ap()[:, :]).then_inc(s_in.h, 16)
                sync.dma_start(out=idb[:], in_=identb.ap()[:, :]).then_inc(s_in.h, 16)
                sync.dma_start(out=idf[:], in_=identf.ap()[:, :]).then_inc(s_in.h, 16)
                s_in.v += 48
                sync.wait_ge(s_in.h, s_in.v)

            @blk.gpsimd
            def _(gp):
                gp.dma_start(out=x_bf[:, :],
                             in_=bass.AP(xT, 0, [[T, 128], [128 * T, NE], [1, T]])).then_inc(s_g.h, 16)
                gp.dma_start(out=xm_bf[:, :],
                             in_=bass.AP(xTm, 0, [[TC, 128], [128 * TC, NE], [1, TC]])).then_inc(s_g.h, 16)
                s_g.v += 32
                gp.wait_ge(s_g.h, s_g.v)

            @blk.vector
            def _(vec):
                vec.memset(ones_f[:], 1.0)
                vec.memset(ones_b[:], 1.0)
                vec.memset(onesL_f[:], 1.0)
                vec.memset(negb12[:], -12.0)
                vec.wait_ge(s_in.h, s_in.v)
                for nm in ("cosq", "ssq", "cosk", "ssk"):
                    vec.tensor_add(tabsb[nm][:], tabsb[nm][:], tlo[nm][:])

    if _PHASES <= 1:
        heads_stack.close()
        return _early_out("p1")
    cosq, ssq, cosk, ssk = (tabsb[n] for n in ("cosq", "ssq", "cosk", "ssk"))

    def rope_evac(vec, ps, dst, dc0, cos_t, ss_t, c0, width):
        vec.tensor_tensor(tmpA[:, 0:width], ps[:, 0:width], cos_t[:, c0:c0 + width], op=ALU.mult)
        vec.tensor_tensor(tmpB[0:64, 0:width], ps[64:128, 0:width], ss_t[0:64, c0:c0 + width], op=ALU.mult)
        vec.tensor_tensor(tmpB[64:128, 0:width], ps[0:64, 0:width], ss_t[64:128, c0:c0 + width], op=ALU.mult)
        return vec.tensor_add(dst[:, dc0:dc0 + width], tmpA[:, 0:width], tmpB[:, 0:width])

    for h in range(H):
        hp = contextlib.ExitStack()
        wq_bf = hp.enter_context(nc.sbuf_tensor(f"wq{h}", [128, NE * 128], FP16))
        wk_bf = hp.enter_context(nc.sbuf_tensor(f"wk{h}", [128, NE * 128], FP16))
        wv_bf = hp.enter_context(nc.sbuf_tensor(f"wv{h}", [128, NE * 128], FP16))
        k_bf = hp.enter_context(nc.sbuf_tensor(f"k{h}", [D, T], FP16))
        q_bf = hp.enter_context(nc.sbuf_tensor(f"q{h}", [D, TC], FP16))
        v_fm = hp.enter_context(nc.sbuf_tensor(f"vf{h}", [D, T], FP16))
        v_tm = hp.enter_context(nc.sbuf_tensor(f"vt{h}", [128, 16 * 128], FP16))
        tmpA = hp.enter_context(nc.sbuf_tensor(f"ta{h}", [128, 512], F32))
        tmpB = hp.enter_context(nc.sbuf_tensor(f"tb{h}", [128, 512], F32))
        pt_bf = hp.enter_context(nc.sbuf_tensor(f"pt{h}", [128, 2 * 256], FP16))
        rcpb = hp.enter_context(nc.sbuf_tensor(f"rc{h}", [128, TC], F32))

        with contextlib.ExitStack() as pp:
            ps_a = pp.enter_context(nc.psum_tensor(f"pa{h}", [128, 512], F32))
            ps_b = pp.enter_context(nc.psum_tensor(f"pb{h}", [128, 512], F32))
            ps_tr = pp.enter_context(nc.psum_tensor(f"ptr{h}", [128, 128], FP16))
            g0, t0, v0 = s_g.v, s_t.v, s_v.v
            with nc.Block() as blk:
                @blk.gpsimd
                def _(gp):
                    for wsb, wdr in ((wq_bf, wqT), (wk_bf, wkT), (wv_bf, wvT)):
                        gp.dma_start(out=wsb[:],
                                     in_=bass.AP(wdr, h * 128, [[E, 128], [128 * E, NE], [1, 128]])
                                     ).then_inc(s_g.h, 16)
                    s_g.v += 48

                @blk.tensor
                def _(tn):
                    tn.wait_ge(s_g.h, g0 + 48)
                    for i in range(10):
                        ps = ps_a if i % 2 == 0 else ps_b
                        if i >= 2:
                            tn.wait_ge(s_v.h, v0 + i - 1)
                        if i < 4:
                            w, rhs_t, cbase, wd = wk_bf, x_bf, i * 512, T
                        elif i < 8:
                            w, rhs_t, cbase, wd = wv_bf, x_bf, (i - 4) * 512, T
                        else:
                            w, rhs_t, cbase, wd = wq_bf, xm_bf, (i - 8) * 512, TC
                        for ec in range(NE):
                            ins = tn.matmul(ps[:, :], w[:, ec * 128:(ec + 1) * 128],
                                            rhs_t[:, ec * wd + cbase: ec * wd + cbase + 512],
                                            start=(ec == 0), stop=(ec == NE - 1))
                        ins.then_inc(s_t.h, 1)
                    s_t.v += 10
                    tn.wait_ge(s_v.h, v0 + 8)
                    for kc in range(16):
                        if kc >= 1:
                            tn.wait_ge(s_v.h, v0 + 10 + kc)
                        tn.transpose(ps_tr[:, :], v_fm[:, kc * 128:(kc + 1) * 128],
                                     idb[:, :]).then_inc(s_t.h, 1)
                    s_t.v += 16

                @blk.vector
                def _(vec):
                    for i in range(10):
                        ps = ps_a if i % 2 == 0 else ps_b
                        vec.wait_ge(s_t.h, t0 + i + 1)
                        if i < 4:
                            ins = rope_evac(vec, ps, k_bf, i * 512, cosk, ssk, i * 512, 512)
                        elif i < 8:
                            ins = vec.tensor_copy(v_fm[:, (i - 4) * 512:(i - 3) * 512], ps[:, :])
                        else:
                            ins = rope_evac(vec, ps, q_bf, (i - 8) * 512, cosq, ssq, (i - 8) * 512, 512)
                        ins.then_inc(s_v.h, 1)
                    for kc in range(16):
                        vec.wait_ge(s_t.h, t0 + 10 + kc + 1)
                        vec.tensor_copy(v_tm[:, kc * 128:(kc + 1) * 128], ps_tr[:, :]).then_inc(s_v.h, 1)
                    s_v.v += 26

        with contextlib.ExitStack() as pp:
            ps_s0 = pp.enter_context(nc.psum_tensor(f"pss0{h}", [128, 256], F32))
            ps_s1 = pp.enter_context(nc.psum_tensor(f"pss1{h}", [128, 256], F32))
            ps_ctx = pp.enter_context(nc.psum_tensor(f"pctx{h}", [128, TC], F32))
            ps_row = pp.enter_context(nc.psum_tensor(f"prow{h}", [1, TC], F32))
            ps_bc = pp.enter_context(nc.psum_tensor(f"pbc{h}", [128, TC], F32))
            t0, v0, s0 = s_t.v, s_v.v, s_s.v
            sched = []
            for j in range(NPAIR):
                nk = 4 * j + 4
                for kc in range(nk):
                    sched.append((j, kc, nk))
            masked_idx = {}
            for n, (j, kc, nk) in enumerate(sched):
                if kc >= nk - 4:
                    masked_idx[n] = len(masked_idx) + 1
            NM = len(masked_idx)

            with nc.Block() as blk:
                @blk.tensor
                def _(tn):
                    for n, (j, kc, nk) in enumerate(sched):
                        sl = (n % 2) * 256
                        pss = ps_s0 if n % 2 == 0 else ps_s1
                        if n >= 2:
                            tn.wait_ge(s_s.h, s0 + n - 1)
                        tn.matmul(pss[:, :], k_bf[:, kc * 128:(kc + 1) * 128],
                                  q_bf[:, j * 256:(j + 1) * 256], start=True, stop=True
                                  ).then_inc(s_t.h, 1)
                        if n in masked_idx:
                            tn.wait_ge(s_v.h, v0 + masked_idx[n])
                        else:
                            tn.wait_ge(s_s.h, s0 + n + 1)
                        tn.matmul(ps_ctx[:, j * 256:(j + 1) * 256],
                                  v_tm[:, kc * 128:(kc + 1) * 128],
                                  pt_bf[:, sl:sl + 256], start=(kc == 0), stop=(kc == nk - 1))
                        tn.matmul(ps_row[0:1, j * 256:(j + 1) * 256], ones_b[:, 0:1],
                                  pt_bf[:, sl:sl + 256], start=(kc == 0), stop=(kc == nk - 1)
                                  ).then_inc(s_t.h, 1)
                    s_t.v += 80
                    tn.wait_ge(s_v.h, v0 + NM + 1)
                    for tb in range(2):
                        ins = tn.matmul(ps_bc[:, tb * 512:(tb + 1) * 512], onesL_f[0:1, :],
                                        rcpb[0:1, tb * 512:(tb + 1) * 512], start=True, stop=True)
                    ins.then_inc(s_t.h, 1)
                    s_t.v += 1

                @blk.scalar
                def _(sc):
                    for n, (j, kc, nk) in enumerate(sched):
                        sl = (n % 2) * 256
                        pss = ps_s0 if n % 2 == 0 else ps_s1
                        sc.wait_ge(s_t.h, t0 + 2 * n + 1)
                        sc.activation(pt_bf[:, sl:sl + 256], pss[:, :], AF.Exp,
                                      bias=negb12[:, 0:1]).then_inc(s_s.h, 1)
                    s_s.v += 80 if False else len(sched)

                @blk.vector
                def _(vec):
                    for n, (j, kc, nk) in enumerate(sched):
                        if n not in masked_idx:
                            continue
                        sl = (n % 2) * 256
                        mo = (j * 4 + (kc - (nk - 4))) * 256
                        vec.wait_ge(s_s.h, s0 + n + 1)
                        vec.tensor_mul(pt_bf[:, sl:sl + 256], pt_bf[:, sl:sl + 256],
                                       mask_sb[:, mo:mo + 256]).then_inc(s_v.h, 1)
                    vec.wait_ge(s_t.h, t0 + 80)
                    vec.reciprocal(rcpb[0:1, :], ps_row[0:1, :]).then_inc(s_v.h, 1)
                    s_v.v += NM + 1
                    vec.wait_ge(s_t.h, t0 + 81)
                    vec.tensor_copy(rcpb[:, 0:512], ps_bc[:, 0:512])
                    vec.tensor_copy(rcpb[:, 512:1024], ps_bc[:, 512:1024])
                    for j in range(NPAIR):
                        vec.tensor_tensor(ctx_bf[:, h * TC + j * 256: h * TC + (j + 1) * 256],
                                          ps_ctx[:, j * 256:(j + 1) * 256],
                                          rcpb[:, j * 256:(j + 1) * 256], op=ALU.mult)
                    vec.engine_nop().then_inc(s_v.h, 1)
                    s_v.v += 1
        if _DEBUG and h == 0:
            with nc.Block() as blk:
                @blk.sync
                def _(sync):
                    sync.dma_start(out=dbg["d_k"].ap()[:, :], in_=k_bf[:]).then_inc(s_o.h, 16)
                    sync.dma_start(out=dbg["d_q"].ap()[:, :], in_=q_bf[:]).then_inc(s_o.h, 16)
                    sync.dma_start(out=dbg["d_vt"].ap()[:, :], in_=v_tm[:]).then_inc(s_o.h, 16)
                    s_o.v += 48
                    sync.wait_ge(s_o.h, s_o.v)
        hp.close()

    if _DEBUG:
        with nc.Block() as blk:
            @blk.sync
            def _(sync):
                sync.dma_start(out=dbg["d_ctx"].ap()[:, :], in_=ctx_bf[:]).then_inc(s_o.h, 16)
                s_o.v += 16
                sync.wait_ge(s_o.h, s_o.v)

    heads_stack.close()

    if _PHASES <= 2:
        return _early_out("p2")
    y1 = cstack0.enter_context(nc.sbuf_tensor("y1", [128, NE * TC], F32))
    mrun = cstack0.enter_context(nc.sbuf_tensor("mrun", [128, TC], F32))
    mrun_mn = cstack0.enter_context(nc.sbuf_tensor("mrun_mn", [128, TC], F32))
    rowsA = cstack0.enter_context(nc.sbuf_tensor("rowsA", [1, TC], F32))
    rowsB = cstack0.enter_context(nc.sbuf_tensor("rowsB", [1, TC], F32))
    Ab = cstack0.enter_context(nc.sbuf_tensor("Ab", [128, TC], F32))
    Bb = cstack0.enter_context(nc.sbuf_tensor("Bb", [128, TC], F32))

    with contextlib.ExitStack() as ph:
        wo_bf = ph.enter_context(nc.sbuf_tensor("wo_bf", [128, NE * E], FP16))
        xres = ph.enter_context(nc.sbuf_tensor("xres", [128, 2 * TC], F32))
        ps_a = ph.enter_context(nc.psum_tensor("pc_a", [128, 512], F32))
        ps_b = ph.enter_context(nc.psum_tensor("pc_b", [128, 512], F32))
        g0, t0, v0, i0 = s_g.v, s_t.v, s_v.v, s_in.v
        p00, p10 = s_p0.v, s_p1.v
        with nc.Block() as blk:
            @blk.gpsimd
            def _(gp):
                for ec in range(NE):
                    gp.dma_start(out=wo_bf[:, ec * E:(ec + 1) * E],
                                 in_=bass.AP(woT, ec * 128 * E, [[E, 128], [1, E]])
                                 ).then_inc(s_g.h, 16)
                s_g.v += 16 * NE

            @blk.sync
            def _(sync):
                for fc in range(NE):
                    sp = s_p0 if fc % 2 == 0 else s_p1
                    sync.dma_start(out=xres[:, (fc % 2) * TC:(fc % 2) * TC + TC],
                                   in_=bass.AP(xTm, fc * 128 * TC, [[TC, 128], [1, TC]])
                                   ).then_inc(sp.h, 16)
                    sp.v += 16
                    if fc < NE - 1:
                        sync.wait_ge(s_v.h, v0 + 2 * fc)

            @blk.tensor
            def _(tn):
                tn.wait_ge(s_g.h, g0 + 16 * NE)
                for fc in range(NE):
                    for th in range(2):
                        i = 2 * fc + th
                        ps = ps_a if i % 2 == 0 else ps_b
                        if i >= 2:
                            tn.wait_ge(s_v.h, v0 + i - 1)
                        for ec in range(NE):
                            ins = tn.matmul(ps[:, :], wo_bf[:, ec * E + fc * 128: ec * E + fc * 128 + 128],
                                            ctx_bf[:, ec * TC + th * 512: ec * TC + th * 512 + 512],
                                            start=(ec == 0), stop=(ec == NE - 1))
                        ins.then_inc(s_t.h, 1)
                s_t.v += 32

            @blk.vector
            def _(vec):
                for fc in range(NE):
                    for th in range(2):
                        i = 2 * fc + th
                        ps = ps_a if i % 2 == 0 else ps_b
                        vec.wait_ge(s_t.h, t0 + i + 1)
                        sp = s_p0 if fc % 2 == 0 else s_p1
                        vec.wait_ge(sp.h, (p00 if fc % 2 == 0 else p10) + 16 * (fc // 2 + 1))
                        vec.tensor_add(y1[:, fc * TC + th * 512: fc * TC + th * 512 + 512],
                                       ps[:, :], xres[:, (fc % 2) * TC + th * 512:(fc % 2) * TC + th * 512 + 512]
                                       ).then_inc(s_v.h, 1)
                s_v.v += 32

    if _PHASES <= 3:
        cstack0.close()
        return _early_out("p3")
    with contextlib.ExitStack() as ph:
        sqt = ph.enter_context(nc.sbuf_tensor("sqt", [128, 2 * TC], F32))
        ps_sum = ph.enter_context(nc.psum_tensor("pc_sum", [1, TC], F32))
        ps_sq = ph.enter_context(nc.psum_tensor("pc_sq", [1, TC], F32))
        ps_bc = ph.enter_context(nc.psum_tensor("pc_bc", [128, TC], F32))
        t0, v0, s0 = s_t.v, s_v.v, s_s.v
        with nc.Block() as blk:
            @blk.vector
            def _(vec):
                for fc in range(NE):
                    if fc >= 2:
                        vec.wait_ge(s_t.h, t0 + fc - 1)
                    vec.tensor_mul(sqt[:, (fc % 2) * TC:(fc % 2) * TC + TC],
                                   y1[:, fc * TC:(fc + 1) * TC],
                                   y1[:, fc * TC:(fc + 1) * TC]).then_inc(s_v.h, 1)
                s_v.v += NE
                vec.wait_ge(s_t.h, t0 + NE)
                vec.tensor_scalar(rowsA[0:1, :], ps_sum[0:1, :], 1.0 / E, None, op0=ALU.mult)
                vec.tensor_scalar(rowsB[0:1, :], ps_sq[0:1, :], 1.0 / E, None, op0=ALU.mult)
                vec.tensor_mul(Ab[0:1, 0:TC], rowsA[0:1, :], rowsA[0:1, :])
                vec.tensor_sub(rowsB[0:1, :], rowsB[0:1, :], Ab[0:1, 0:TC])
                vec.tensor_scalar(rowsB[0:1, :], rowsB[0:1, :], 1e-5, None, op0=ALU.add
                                  ).then_inc(s_v.h, 1)
                s_v.v += 1
                vec.wait_ge(s_s.h, s0 + 1)
                vec.reciprocal(Ab[0:1, 0:TC], Bb[0:1, 0:TC])
                vec.tensor_mul(Bb[0:1, 0:TC], Ab[0:1, 0:TC], Ab[0:1, 0:TC])
                vec.tensor_mul(Bb[0:1, 0:TC], Bb[0:1, 0:TC], rowsB[0:1, :])
                vec.tensor_scalar(Bb[0:1, 0:TC], Bb[0:1, 0:TC], -0.5, 1.5, op0=ALU.mult, op1=ALU.add)
                vec.tensor_mul(rowsB[0:1, :], Ab[0:1, 0:TC], Bb[0:1, 0:TC])
                vec.tensor_mul(rowsA[0:1, :], rowsA[0:1, :], rowsB[0:1, :])
                vec.tensor_scalar(rowsA[0:1, :], rowsA[0:1, :], -1.0, None, op0=ALU.mult
                                  ).then_inc(s_v.h, 1)
                s_v.v += 1
                vec.wait_ge(s_t.h, t0 + NE + 1)
                vec.tensor_copy(Ab[:, :], ps_bc[:, :]).then_inc(s_v.h, 1)
                s_v.v += 1
                vec.wait_ge(s_t.h, t0 + NE + 2)
                vec.tensor_copy(Bb[:, :], ps_bc[:, :]).then_inc(s_v.h, 1)
                s_v.v += 1
                vec.memset(mrun[:, :], 0.0)
                vec.memset(mrun_mn[:, :], 0.0)
                for fc in range(NE):
                    vec.tensor_mul(y1[:, fc * TC:(fc + 1) * TC], y1[:, fc * TC:(fc + 1) * TC], Ab[:, :])
                    vec.tensor_add(y1[:, fc * TC:(fc + 1) * TC], y1[:, fc * TC:(fc + 1) * TC], Bb[:, :])
                    vec.tensor_max(mrun[:, :], mrun[:, :], y1[:, fc * TC:(fc + 1) * TC])
                    vec.tensor_tensor(mrun_mn[:, :], mrun_mn[:, :], y1[:, fc * TC:(fc + 1) * TC],
                                      op=ALU.min)
                    vec.engine_nop().then_inc(s_v.h, 1)
                vec.tensor_scalar(mrun_mn[:, :], mrun_mn[:, :], -1.0, None, op0=ALU.mult)
                vec.tensor_max(mrun[:, :], mrun[:, :], mrun_mn[:, :])
                s_v.v += NE

            @blk.scalar
            def _(sc):
                sc.wait_ge(s_v.h, v0 + NE + 1)
                sc.activation(Bb[0:1, 0:TC], rowsB[0:1, :], AF.Sqrt).then_inc(s_s.h, 1)
                s_s.v += 1

            @blk.tensor
            def _(tn):
                for fc in range(NE):
                    tn.wait_ge(s_v.h, v0 + fc + 1)
                    for th in range(2):
                        tn.matmul(ps_sum[0:1, th * 512:(th + 1) * 512], ones_f[:, 0:1],
                                  y1[:, fc * TC + th * 512: fc * TC + th * 512 + 512],
                                  start=(fc == 0), stop=(fc == NE - 1))
                        ins = tn.matmul(ps_sq[0:1, th * 512:(th + 1) * 512], ones_f[:, 0:1],
                                        sqt[:, (fc % 2) * TC + th * 512:(fc % 2) * TC + th * 512 + 512],
                                        start=(fc == 0), stop=(fc == NE - 1))
                    ins.then_inc(s_t.h, 1)
                s_t.v += NE
                tn.wait_ge(s_v.h, v0 + NE + 2)
                for tb in range(2):
                    ins = tn.matmul(ps_bc[:, tb * 512:(tb + 1) * 512], onesL_f[0:1, :],
                                    rowsB[0:1, tb * 512:(tb + 1) * 512], start=True, stop=True)
                ins.then_inc(s_t.h, 1)
                s_t.v += 1
                tn.wait_ge(s_v.h, v0 + NE + 3)
                for tb in range(2):
                    ins = tn.matmul(ps_bc[:, tb * 512:(tb + 1) * 512], onesL_f[0:1, :],
                                    rowsA[0:1, tb * 512:(tb + 1) * 512], start=True, stop=True)
                ins.then_inc(s_t.h, 1)
                s_t.v += 1

            @blk.sync
            def _(sync):
                for fc in range(NE):
                    sync.wait_ge(s_v.h, v0 + NE + 4 + fc + 1)
                    sync.dma_start(out=bass.AP(ynorm_hbm, fc * 128 * TC, [[TC, 128], [1, TC]]),
                                   in_=y1[:, fc * TC:(fc + 1) * TC]).then_inc(s_o.h, 16)
                    s_o.v += 16
                sync.wait_ge(s_o.h, s_o.v)

    if _PHASES <= 4:
        cstack0.close()
        return _early_out("p4")
    with contextlib.ExitStack() as ph:
        xq1 = ph.enter_context(nc.sbuf_tensor("xq1c", [128, NE * TC], BF16))
        R1b = ph.enter_context(nc.sbuf_tensor("R1bc", [128, TC], F32))
        mca = ph.enter_context(nc.sbuf_tensor("mca", [128, 8], F32))
        v_t = ph.enter_context(nc.sbuf_tensor("qv", [128, TC], F32))
        ci_t = ph.enter_context(nc.sbuf_tensor("qc", [128, TC], I32))
        r_t = ph.enter_context(nc.sbuf_tensor("qr", [128, TC], F32))
        an_t = ph.enter_context(nc.sbuf_tensor("qa", [128, TC], F32))
        S1b = ph.enter_context(nc.sbuf_tensor("S1b", [128, TC], F32))
        ps_tr = ph.enter_context(nc.psum_tensor("pq_tr", [128, 128], F32))
        ps_tc0 = ph.enter_context(nc.psum_tensor("pq_tc0", [1, 128], F32))
        ps_tc1 = ph.enter_context(nc.psum_tensor("pq_tc1", [1, 128], F32))
        ps_bc = ph.enter_context(nc.psum_tensor("pq_bc", [128, TC], F32))
        t0, v0, s0 = s_t.v, s_v.v, s_s.v
        with nc.Block() as blk:
            @blk.tensor
            def _(tn):
                for jt in range(8):
                    if jt >= 1:
                        tn.wait_ge(s_v.h, v0 + jt)
                    tn.transpose(ps_tr[:, :], mrun[:, jt * 128:(jt + 1) * 128],
                                 idf[:, :]).then_inc(s_t.h, 1)
                s_t.v += 8
                for i in range(8):
                    tn.wait_ge(s_v.h, v0 + (i + 1 if i < 2 else 8 + i))
                    tn.transpose((ps_tc0 if i % 2 == 0 else ps_tc1)[0:1, :],
                                 mca[:, i:i + 1], idf[:, :]).then_inc(s_t.h, 1)
                s_t.v += 8
                tn.wait_ge(s_v.h, v0 + 17)
                for tb in range(2):
                    ins = tn.matmul(ps_bc[:, tb * 512:(tb + 1) * 512], onesL_f[0:1, :],
                                    rowsA[0:1, tb * 512:(tb + 1) * 512], start=True, stop=True)
                ins.then_inc(s_t.h, 1)
                s_t.v += 1
                tn.wait_ge(s_v.h, v0 + 18)
                for tb in range(2):
                    ins = tn.matmul(ps_bc[:, tb * 512:(tb + 1) * 512], onesL_f[0:1, :],
                                    rowsB[0:1, tb * 512:(tb + 1) * 512], start=True, stop=True)
                ins.then_inc(s_t.h, 1)
                s_t.v += 1

            @blk.vector
            def _(vec):
                for jt in range(8):
                    vec.wait_ge(s_t.h, t0 + jt + 1)
                    vec.reduce_max(mca[:, jt:jt + 1], ps_tr[:, :], axis=AX.X).then_inc(s_v.h, 1)
                for i in range(8):
                    vec.wait_ge(s_t.h, t0 + 8 + i + 1)
                    vec.tensor_copy(rowsA[0:1, i * 128:(i + 1) * 128],
                                    (ps_tc0 if i % 2 == 0 else ps_tc1)[0:1, :]).then_inc(s_v.h, 1)
                s_v.v += 16
                vec.tensor_scalar(rowsB[0:1, :], rowsA[0:1, :], 1e-5, 1.0 / 127.0,
                                  op0=ALU.max, op1=ALU.mult)
                vec.reciprocal(rowsA[0:1, :], rowsB[0:1, :])
                vec.engine_nop().then_inc(s_v.h, 1)
                s_v.v += 1
                vec.wait_ge(s_t.h, t0 + 17)
                vec.tensor_copy(S1b[:, :], ps_bc[:, :]).then_inc(s_v.h, 1)
                s_v.v += 1
                vec.wait_ge(s_t.h, t0 + 18)
                vec.tensor_copy(R1b[:, :], ps_bc[:, :]).then_inc(s_v.h, 1)
                s_v.v += 1
                for fc in range(NE):
                    vec.tensor_mul(v_t[:, :], y1[:, fc * TC:(fc + 1) * TC], S1b[:, :])
                    vec.engine_nop().then_inc(s_v.h, 1)
                    vec.wait_ge(s_s.h, s0 + fc + 1)
                    vec.tensor_sub(r_t[:, :], v_t[:, :], ci_t[:, :])
                    vec.tensor_scalar(an_t[:, :], r_t[:, :], 0.5, -1.0, op0=ALU.is_gt, op1=ALU.mult)
                    vec.scalar_tensor_tensor(r_t[:, :], r_t[:, :], -0.5, an_t[:, :],
                                             op0=ALU.is_lt, op1=ALU.add)
                    vec.tensor_sub(xq1[:, fc * TC:(fc + 1) * TC], ci_t[:, :], r_t[:, :])
                    vec.engine_nop().then_inc(s_v.h, 1)
                s_v.v += 2 * NE

            @blk.scalar
            def _(sc):
                for fc in range(NE):
                    sc.wait_ge(s_v.h, v0 + 19 + 2 * fc + 1)
                    sc.activation(ci_t[:, :], v_t[:, :], AF.Identity).then_inc(s_s.h, 1)
                s_s.v += NE

            @blk.sync
            def _(sync):
                for fc in range(NE):
                    sync.wait_ge(s_v.h, v0 + 19 + 2 * fc + 2)
                    sync.dma_start(out=bass.AP(xq1_hbm, fc * 128 * TC, [[TC, 128], [1, TC]]),
                                   in_=xq1[:, fc * TC:(fc + 1) * TC]).then_inc(s_o.h, 16)
                    s_o.v += 16
                sync.wait_ge(s_v.h, v0 + 19)
                sync.dma_start(out=r1_hbm.ap()[:, :], in_=R1b[:, :]).then_inc(s_o.h, 16)
                s_o.v += 16
                if _DEBUG:
                    sync.dma_start(out=dbg["d_y1"].ap()[:, :], in_=y1[:, :]).then_inc(s_o.h, 16)
                    sync.dma_start(out=dbg["d_xq1"].ap()[:, :], in_=xq1[:, :]).then_inc(s_o.h, 16)
                    sync.dma_start(out=dbg["d_s1"].ap()[:, :], in_=S1b[:, :]).then_inc(s_o.h, 16)
                    sync.dma_start(out=dbg["d_rows"].ap()[0:1, 0:TC], in_=rowsA[0:1, :]).then_inc(s_o.h, 16)
                    sync.dma_start(out=dbg["d_rows"].ap()[0:1, TC:2 * TC], in_=rowsB[0:1, :]).then_inc(s_o.h, 16)
                    sync.dma_start(out=dbg["d_mrun"].ap()[:, :], in_=mrun[:, :]).then_inc(s_o.h, 16)
                    s_o.v += 96
                sync.wait_ge(s_o.h, s_o.v)

    cstack0.close()

    if _PHASES <= 5:
        return _early_out("p5")

    with contextlib.ExitStack() as ph:
        stg = ph.enter_context(nc.sbuf_tensor("d0stg", [128, 2 * 8192], F32))
        absb = ph.enter_context(nc.sbuf_tensor("d0abs", [128, 8192], F32))
        s1row = ph.enter_context(nc.sbuf_tensor("s1row", [1, HID], F32))
        n1row = ph.enter_context(nc.sbuf_tensor("n1row", [1, HID], F32))
        ps_r = ph.enter_context(nc.psum_tensor("d0pr", [1, 512], F32))
        g0 = s_in.v
        p00, p10 = s_p0.v, s_p1.v
        t0, v0, s0 = s_t.v, s_v.v, s_s.v
        with nc.Block() as blk:
            @blk.sync
            def _(sync):
                for hb in range(16):
                    if hb >= 2:
                        sync.wait_ge(s_s.h, s0 + hb - 1)
                    sp = s_p0 if hb % 2 == 0 else s_p1
                    sync.dma_start(out=stg[:, (hb % 2) * 8192:(hb % 2) * 8192 + 8192],
                                   in_=bass.AP(w1T, hb * 512, [[HID, 128], [128 * HID, NE], [1, 512]])
                                   ).then_inc(sp.h, 16)
                    sp.v += 16

            @blk.scalar
            def _(sc):
                for hb in range(16):
                    sp0 = p00 if hb % 2 == 0 else p10
                    sc.wait_ge((s_p0 if hb % 2 == 0 else s_p1).h, sp0 + 16 * (hb // 2 + 1))
                    if hb >= 1:
                        sc.wait_ge(s_t.h, t0 + hb)
                    sc.activation(absb[:, :], stg[:, (hb % 2) * 8192:(hb % 2) * 8192 + 8192],
                                  AF.Abs).then_inc(s_s.h, 1)
                s_s.v += 16

            @blk.tensor
            def _(tn):
                for hb in range(16):
                    tn.wait_ge(s_s.h, s0 + hb + 1)
                    if hb >= 1:
                        tn.wait_ge(s_v.h, v0 + hb)
                    for ec in range(NE):
                        ins = tn.matmul(ps_r[0:1, :], ones_f[:, 0:1],
                                        absb[:, ec * 512:(ec + 1) * 512],
                                        start=(ec == 0), stop=(ec == NE - 1))
                    ins.then_inc(s_t.h, 1)
                s_t.v += 16

            @blk.vector
            def _(vec):
                for hb in range(16):
                    vec.wait_ge(s_t.h, t0 + hb + 1)
                    vec.tensor_scalar(s1row[0:1, hb * 512:(hb + 1) * 512], ps_r[0:1, :],
                                      1.0 / E, 1e-5, op0=ALU.mult, op1=ALU.max).then_inc(s_v.h, 1)
                s_v.v += 16
                vec.reciprocal(n1row[0:1, :], s1row[0:1, :])
                vec.tensor_scalar(n1row[0:1, :], n1row[0:1, :], -2.0, None, op0=ALU.mult)

        with contextlib.ExitStack() as p2:
            ps_tp = p2.enter_context(nc.psum_tensor("d0tp", [128, 1], F32))
            t0, v0 = s_t.v, s_v.v
            with nc.Block() as blk:
                @blk.tensor
                def _(tn):
                    for i in range(2 * NH):
                        row = s1row if i < NH else n1row
                        off = i % NH
                        if i >= 1:
                            tn.wait_ge(s_v.h, v0 + i)
                        tn.transpose(ps_tp[:, :], row[0:1, off * 128:(off + 1) * 128],
                                     idf[0:1, 0:1]).then_inc(s_t.h, 1)
                    s_t.v += 2 * NH

                @blk.vector
                def _(vec):
                    for i in range(2 * NH):
                        dst = w1s_pp if i < NH else n1_pp
                        off = i % NH
                        vec.wait_ge(s_t.h, t0 + i + 1)
                        vec.tensor_copy(dst[:, off:off + 1], ps_tp[:, :]).then_inc(s_v.h, 1)
                    s_v.v += 2 * NH

    with contextlib.ExitStack() as ph2:
        stg = ph2.enter_context(nc.sbuf_tensor("d0s2", [128, 2 * 2048], F32))
        absb = ph2.enter_context(nc.sbuf_tensor("d0a2", [128, 2048], F32))
        s2row = ph2.enter_context(nc.sbuf_tensor("s2row", [1, E], F32))
        n2row = ph2.enter_context(nc.sbuf_tensor("n2row", [1, E], F32))
        ps_r = ph2.enter_context(nc.psum_tensor("d0p2", [1, 2048], F32))
        g0 = s_in.v
        p00, p10 = s_p0.v, s_p1.v
        t0, v0, s0 = s_t.v, s_v.v, s_s.v
        with nc.Block() as blk:
            @blk.sync
            def _(sync):
                for hc in range(NH):
                    if hc >= 2:
                        sync.wait_ge(s_s.h, s0 + hc - 1)
                    sp = s_p0 if hc % 2 == 0 else s_p1
                    sync.dma_start(out=stg[:, (hc % 2) * 2048:(hc % 2) * 2048 + 2048],
                                   in_=bass.AP(w2T, hc * 128 * E, [[E, 128], [1, E]])
                                   ).then_inc(sp.h, 16)
                    sp.v += 16

            @blk.scalar
            def _(sc):
                for hc in range(NH):
                    sp0 = p00 if hc % 2 == 0 else p10
                    sc.wait_ge((s_p0 if hc % 2 == 0 else s_p1).h, sp0 + 16 * (hc // 2 + 1))
                    if hc >= 1:
                        sc.wait_ge(s_t.h, t0 + hc)
                    sc.activation(absb[:, :], stg[:, (hc % 2) * 2048:(hc % 2) * 2048 + 2048],
                                  AF.Abs).then_inc(s_s.h, 1)
                s_s.v += NH

            @blk.tensor
            def _(tn):
                for hc in range(NH):
                    tn.wait_ge(s_s.h, s0 + hc + 1)
                    for fg in range(4):
                        ins = tn.matmul(ps_r[0:1, fg * 512:(fg + 1) * 512], ones_f[:, 0:1],
                                        absb[:, fg * 512:(fg + 1) * 512],
                                        start=(hc == 0), stop=(hc == NH - 1))
                    ins.then_inc(s_t.h, 1)
                s_t.v += NH

            @blk.vector
            def _(vec):
                vec.wait_ge(s_t.h, t0 + NH)
                vec.tensor_scalar(s2row[0:1, :], ps_r[0:1, :], 1.0 / HID, 1e-5,
                                  op0=ALU.mult, op1=ALU.max)
                vec.reciprocal(n2row[0:1, :], s2row[0:1, :])
                vec.tensor_scalar(n2row[0:1, :], n2row[0:1, :], -2.0, None, op0=ALU.mult)

        with contextlib.ExitStack() as p2:
            ps_tp = p2.enter_context(nc.psum_tensor("dspp", [128, 1], F32))
            t0, v0 = s_t.v, s_v.v
            with nc.Block() as blk:
                @blk.tensor
                def _(tn):
                    for i in range(NE):
                        if i >= 1:
                            tn.wait_ge(s_v.h, v0 + i)
                        tn.transpose(ps_tp[:, :], s2row[0:1, i * 128:(i + 1) * 128],
                                     idf[0:1, 0:1]).then_inc(s_t.h, 1)
                    s_t.v += NE

                @blk.vector
                def _(vec):
                    for i in range(NE):
                        vec.wait_ge(s_t.h, t0 + i + 1)
                        vec.tensor_copy(w2s_pp[:, i:i + 1], ps_tp[:, :]).then_inc(s_v.h, 1)
                    s_v.v += NE

        ph = ph2
        stg = ph.enter_context(nc.sbuf_tensor("d1s", [128, 2 * 2048], F32))
        ub = ph.enter_context(nc.sbuf_tensor("d1u", [128, 2048], F32))
        bn = ph.enter_context(nc.sbuf_tensor("d1b", [128, 2048], BF16))
        n2b = ph.enter_context(nc.sbuf_tensor("n2b", [128, 2048], F32))
        w2q_t = ph.enter_context(nc.sbuf_tensor("w2q_t", [128, 2 * 2048], FP8))
        ps_bc = ph.enter_context(nc.psum_tensor("d1p", [128, 2048], F32))
        g0 = s_in.v
        t0, v0, o0 = s_t.v, s_v.v, s_o.v
        p00, p10, q00, q10 = s_p0.v, s_p1.v, s_q0.v, s_q1.v
        with nc.Block() as blk:
            @blk.tensor
            def _(tn):
                for fg in range(4):
                    ins = tn.matmul(ps_bc[:, fg * 512:(fg + 1) * 512], onesL_f[0:1, :],
                                    n2row[0:1, fg * 512:(fg + 1) * 512], start=True, stop=True)
                ins.then_inc(s_t.h, 1)
                s_t.v += 1

            @blk.sync
            def _(sync):
                for hc in range(NH):
                    if hc >= 2:
                        sync.wait_ge(s_v.h, v0 + hc - 1)
                    sp = s_p0 if hc % 2 == 0 else s_p1
                    sync.dma_start(out=stg[:, (hc % 2) * 2048:(hc % 2) * 2048 + 2048],
                                   in_=bass.AP(w2T, hc * 128 * E, [[E, 128], [1, E]])
                                   ).then_inc(sp.h, 16)
                    sp.v += 16

            @blk.vector
            def _(vec):
                vec.wait_ge(s_t.h, t0 + 1)
                vec.tensor_copy(n2b[:, :], ps_bc[:, :])
                for hc in range(NH):
                    vec.wait_ge((s_p0 if hc % 2 == 0 else s_p1).h,
                                (p00 if hc % 2 == 0 else p10) + 16 * (hc // 2 + 1))
                    if hc >= 2:
                        vec.wait_ge((s_q0 if hc % 2 == 0 else s_q1).h,
                                    (q00 if hc % 2 == 0 else q10) + 16 * ((hc - 2) // 2 + 1))
                    sl = (hc % 2) * 2048
                    vec.tensor_mul(ub[:, :], stg[:, sl:sl + 2048], n2b[:, :])
                    vec.tensor_scalar(bn[:, :], ub[:, :], 1.0, -1.0, op0=ALU.is_gt, op1=ALU.mult)
                    vec.scalar_tensor_tensor(w2q_t[:, sl:sl + 2048], ub[:, :], -1.0, bn[:, :],
                                             op0=ALU.is_lt, op1=ALU.add).then_inc(s_v.h, 1)
                s_v.v += NH

            @blk.gpsimd
            def _(gp):
                for hc in range(NH):
                    gp.wait_ge(s_v.h, v0 + hc + 1)
                    sl = (hc % 2) * 2048
                    sq = s_q0 if hc % 2 == 0 else s_q1
                    gp.dma_start(out=bass.AP(w2q_hbm, hc * 128 * E, [[E, 128], [1, E]]),
                                 in_=w2q_t[:, sl:sl + 2048]).then_inc(sq.h, 16)
                    sq.v += 16
                gp.wait_ge(s_q0.h, s_q0.v)
                gp.wait_ge(s_q1.h, s_q1.v)

    if _PHASES <= 6:
        return _early_out("p6")
    e_stack = contextlib.ExitStack()
    g_sb = e_stack.enter_context(nc.sbuf_tensor("g_sb", [128, NH * TC], FP16))
    mg = e_stack.enter_context(nc.sbuf_tensor("mg", [128, TC], F32))
    mg_mn = e_stack.enter_context(nc.sbuf_tensor("mg_mn", [128, TC], F32))

    with contextlib.ExitStack() as ph:
        stg = ph.enter_context(nc.sbuf_tensor("d2s", [128, 2 * 2048], F32))
        ub = ph.enter_context(nc.sbuf_tensor("d2u", [128, 1024], F32))
        bn = ph.enter_context(nc.sbuf_tensor("d2b", [128, 1024], BF16))
        n1bt = ph.enter_context(nc.sbuf_tensor("n1bt", [128, 128], F32))
        w1q_t = ph.enter_context(nc.sbuf_tensor("w1q_t", [128, 2048], FP8))
        t0t = ph.enter_context(nc.sbuf_tensor("d2t0", [128, 2 * 512], F32))
        ps_m = ph.enter_context(nc.psum_tensor("d2pm", [128, 1024], F32))
        ps_bc = ph.enter_context(nc.psum_tensor("d2pb", [128, 128], F32))

        n1r_sb = ph.enter_context(nc.sbuf_tensor("n1r_sb", [1, 128], F32))
        xq1 = ph.enter_context(nc.sbuf_tensor("xq1d", [128, NE * TC], BF16))
        R1b = ph.enter_context(nc.sbuf_tensor("R1bd", [128, TC], F32))
        ps_n1 = ph.enter_context(nc.psum_tensor("d2pn", [1, 128], F32))
        with nc.Block() as blk:
            @blk.sync
            def _(sync):
                sync.dma_start(out=xq1[:, :],
                               in_=bass.AP(xq1_hbm, 0, [[TC, 128], [128 * TC, NE], [1, TC]])
                               ).then_inc(s_in.h, 16)
                sync.dma_start(out=R1b[:, :], in_=r1_hbm.ap()[:, :]).then_inc(s_in.h, 16)
                s_in.v += 32
                sync.wait_ge(s_in.h, s_in.v)
        first = True
        for hc in range(NH):
            g0 = s_in.v
            t0, v0, s0 = s_t.v, s_v.v, s_s.v
            with nc.Block() as blk:
                @blk.sync
                def _(sync):
                    sync.dma_start(out=stg[:, (hc % 2) * 2048:(hc % 2) * 2048 + 2048],
                                   in_=bass.AP(w1T, hc * 128, [[HID, 128], [128 * HID, NE], [1, 128]])
                                   ).then_inc(s_in.h, 16)
                    s_in.v += 16

                @blk.tensor
                def _(tn):
                    tn.transpose(ps_n1[:, :], n1_pp[:, hc:hc + 1], idf[:, :]).then_inc(s_t.h, 1)
                    tn.wait_ge(s_v.h, v0 + 1)
                    tn.matmul(ps_bc[:, :], onesL_f[0:1, :], n1r_sb[0:1, :],
                              start=True, stop=True).then_inc(s_t.h, 1)
                    s_t.v += 2
                    tn.wait_ge(s_v.h, v0 + 3)
                    for th in range(2):
                        for ec in range(NE):
                            ins = tn.matmul(ps_m[:, th * 512:(th + 1) * 512],
                                            w1q_t[:, ec * 128:(ec + 1) * 128],
                                            xq1[:, ec * TC + th * 512: ec * TC + th * 512 + 512],
                                            start=(ec == 0), stop=(ec == NE - 1))
                        ins.then_inc(s_t.h, 1)
                    s_t.v += 2

                @blk.vector
                def _(vec):
                    if first:
                        vec.memset(mg[:, :], 0.0)
                        vec.memset(mg_mn[:, :], 0.0)
                    vec.wait_ge(s_t.h, t0 + 1)
                    vec.tensor_copy(n1r_sb[0:1, :], ps_n1[0:1, :]).then_inc(s_v.h, 1)
                    vec.wait_ge(s_t.h, t0 + 2)
                    vec.tensor_copy(n1bt[:, :], ps_bc[:, :]).then_inc(s_v.h, 1)
                    vec.wait_ge(s_in.h, g0 + 16)
                    sl = (hc % 2) * 2048
                    for hh in range(2):
                        vec.tensor_tensor(ub[:, :], stg[:, sl + hh * 1024: sl + hh * 1024 + 1024],
                                          bass.AP(n1bt, 0, [[128, 128], [0, NE // 2], [1, 128]]),
                                          op=ALU.mult)
                        vec.tensor_scalar(bn[:, :], ub[:, :], 1.0, -1.0, op0=ALU.is_gt, op1=ALU.mult)
                        vec.scalar_tensor_tensor(w1q_t[:, hh * 1024:(hh + 1) * 1024], ub[:, :],
                                                 -1.0, bn[:, :],
                                                 op0=ALU.is_lt, op1=ALU.add)
                    vec.engine_nop().then_inc(s_v.h, 1)
                    s_v.v += 3
                    for th in range(2):
                        vec.wait_ge(s_t.h, t0 + 3 + th)
                        vec.tensor_mul(t0t[:, th * 512:(th + 1) * 512],
                                       ps_m[:, th * 512:(th + 1) * 512],
                                       R1b[:, th * 512:(th + 1) * 512])
                        vec.engine_nop().then_inc(s_v.h, 1)
                        vec.wait_ge(s_s.h, s0 + th + 1)
                        vec.tensor_max(mg[:, th * 512:(th + 1) * 512],
                                       mg[:, th * 512:(th + 1) * 512],
                                       g_sb[:, hc * TC + th * 512: hc * TC + th * 512 + 512])
                        vec.tensor_tensor(mg_mn[:, th * 512:(th + 1) * 512],
                                          mg_mn[:, th * 512:(th + 1) * 512],
                                          g_sb[:, hc * TC + th * 512: hc * TC + th * 512 + 512],
                                          op=ALU.min).then_inc(s_v.h, 1)
                    s_v.v += 4

                @blk.scalar
                def _(sc):
                    for th in range(2):
                        sc.wait_ge(s_v.h, v0 + 3 + 2 * th + 1)
                        sc.activation(g_sb[:, hc * TC + th * 512: hc * TC + th * 512 + 512],
                                      t0t[:, th * 512:(th + 1) * 512], AF.Gelu_apprx_tanh,
                                      scale=w1s_pp[:, hc:hc + 1]).then_inc(s_s.h, 1)
                    s_s.v += 2
            first = False

    if _PHASES <= 7:
        e_stack.close()
        return _early_out("p7")
    if _DEBUG:
        with nc.Block() as blk:
            @blk.sync
            def _(sync):
                sync.dma_start(out=dbg["d_g"].ap()[:, :], in_=g_sb[:, 0:8 * TC]).then_inc(s_o.h, 16)
                s_o.v += 16
                sync.wait_ge(s_o.h, s_o.v)

    R2b = e_stack.enter_context(nc.sbuf_tensor("R2b", [128, TC], F32))
    with contextlib.ExitStack() as ph:
        mca = ph.enter_context(nc.sbuf_tensor("e0ma", [128, 8], F32))
        v_t = ph.enter_context(nc.sbuf_tensor("e0v", [128, TC], F32))
        ci_t = ph.enter_context(nc.sbuf_tensor("e0c", [128, TC], I32))
        r_t = ph.enter_context(nc.sbuf_tensor("e0r", [128, TC], F32))
        an_t = ph.enter_context(nc.sbuf_tensor("e0a", [128, TC], F32))
        S2b = ph.enter_context(nc.sbuf_tensor("S2b", [128, TC], F32))
        rA = ph.enter_context(nc.sbuf_tensor("eq_ra", [1, TC], F32))
        rB = ph.enter_context(nc.sbuf_tensor("eq_rb", [1, TC], F32))
        ps_tr = ph.enter_context(nc.psum_tensor("e0pt", [128, 128], F32))
        ps_tc0 = ph.enter_context(nc.psum_tensor("e0pc0", [1, 128], F32))
        ps_tc1 = ph.enter_context(nc.psum_tensor("e0pc1", [1, 128], F32))
        ps_bc = ph.enter_context(nc.psum_tensor("e0pb", [128, TC], F32))
        t0, v0, s0 = s_t.v, s_v.v, s_s.v
        with nc.Block() as blk:
            @blk.tensor
            def _(tn):
                for jt in range(8):
                    tn.wait_ge(s_v.h, v0 + 1 + max(0, jt))
                    tn.transpose(ps_tr[:, :], mg[:, jt * 128:(jt + 1) * 128],
                                 idf[:, :]).then_inc(s_t.h, 1)
                s_t.v += 8
                for i in range(8):
                    tn.wait_ge(s_v.h, v0 + 1 + (i + 1 if i < 2 else 8 + i))
                    tn.transpose((ps_tc0 if i % 2 == 0 else ps_tc1)[0:1, :],
                                 mca[:, i:i + 1], idf[:, :]).then_inc(s_t.h, 1)
                s_t.v += 8
                tn.wait_ge(s_v.h, v0 + 18)
                for tb in range(2):
                    ins = tn.matmul(ps_bc[:, tb * 512:(tb + 1) * 512], onesL_f[0:1, :],
                                    rA[0:1, tb * 512:(tb + 1) * 512], start=True, stop=True)
                ins.then_inc(s_t.h, 1)
                s_t.v += 1
                tn.wait_ge(s_v.h, v0 + 19)
                for tb in range(2):
                    ins = tn.matmul(ps_bc[:, tb * 512:(tb + 1) * 512], onesL_f[0:1, :],
                                    rB[0:1, tb * 512:(tb + 1) * 512], start=True, stop=True)
                ins.then_inc(s_t.h, 1)
                s_t.v += 1

            @blk.vector
            def _(vec):
                vec.tensor_scalar(mg_mn[:, :], mg_mn[:, :], -1.0, None, op0=ALU.mult)
                vec.tensor_max(mg[:, :], mg[:, :], mg_mn[:, :]).then_inc(s_v.h, 1)
                s_v.v += 1
                for jt in range(8):
                    vec.wait_ge(s_t.h, t0 + jt + 1)
                    vec.reduce_max(mca[:, jt:jt + 1], ps_tr[:, :], axis=AX.X).then_inc(s_v.h, 1)
                for i in range(8):
                    vec.wait_ge(s_t.h, t0 + 8 + i + 1)
                    vec.tensor_copy(rA[0:1, i * 128:(i + 1) * 128],
                                    (ps_tc0 if i % 2 == 0 else ps_tc1)[0:1, :]).then_inc(s_v.h, 1)
                s_v.v += 16
                vec.tensor_scalar(rB[0:1, :], rA[0:1, :], 1e-5, 1.0 / 127.0,
                                  op0=ALU.max, op1=ALU.mult)
                vec.reciprocal(rA[0:1, :], rB[0:1, :])
                vec.engine_nop().then_inc(s_v.h, 1)
                s_v.v += 1
                vec.wait_ge(s_t.h, t0 + 17)
                vec.tensor_copy(S2b[:, :], ps_bc[:, :]).then_inc(s_v.h, 1)
                s_v.v += 1
                vec.wait_ge(s_t.h, t0 + 18)
                vec.tensor_copy(R2b[:, :], ps_bc[:, :]).then_inc(s_v.h, 1)
                s_v.v += 1
                for hc in range(NH):
                    vec.tensor_mul(v_t[:, :], g_sb[:, hc * TC:(hc + 1) * TC], S2b[:, :])
                    vec.engine_nop().then_inc(s_v.h, 1)
                    vec.wait_ge(s_s.h, s0 + hc + 1)
                    vec.tensor_sub(r_t[:, :], v_t[:, :], ci_t[:, :])
                    vec.tensor_scalar(an_t[:, :], r_t[:, :], 0.5, -1.0, op0=ALU.is_gt, op1=ALU.mult)
                    vec.scalar_tensor_tensor(r_t[:, :], r_t[:, :], -0.5, an_t[:, :],
                                             op0=ALU.is_lt, op1=ALU.add)
                    vec.tensor_sub(g_sb[:, hc * TC:(hc + 1) * TC], ci_t[:, :], r_t[:, :])
                    vec.engine_nop().then_inc(s_v.h, 1)
                s_v.v += 2 * NH

            @blk.scalar
            def _(sc):
                for hc in range(NH):
                    sc.wait_ge(s_v.h, v0 + 20 + 2 * hc + 1)
                    sc.activation(ci_t[:, :], v_t[:, :], AF.Identity).then_inc(s_s.h, 1)
                s_s.v += NH

    if _PHASES <= 8:
        e_stack.close()
        return _early_out("p8")
    for ts_i in range(4):
        with contextlib.ExitStack() as ph:
            w2q_t = ph.enter_context(nc.sbuf_tensor(f"e{ts_i}w", [128, 2 * 2048], FP8))
            yn_t = ph.enter_context(nc.sbuf_tensor(f"e{ts_i}y", [128, NE * 256], F32))
            y2 = ph.enter_context(nc.sbuf_tensor(f"e{ts_i}z", [128, NE * 256], F32))
            sqt = ph.enter_context(nc.sbuf_tensor(f"e{ts_i}q", [128, 2 * 256], F32))
            rA = ph.enter_context(nc.sbuf_tensor(f"e{ts_i}ra", [1, 256], F32))
            rB = ph.enter_context(nc.sbuf_tensor(f"e{ts_i}rb", [1, 256], F32))
            A2 = ph.enter_context(nc.sbuf_tensor(f"e{ts_i}A", [128, 256], F32))
            B2 = ph.enter_context(nc.sbuf_tensor(f"e{ts_i}B", [128, 256], F32))
            g0, i0 = s_g.v, s_in.v
            t0, v0, s0 = s_t.v, s_v.v, s_s.v
            p00, p10 = s_p0.v, s_p1.v
            with nc.Block() as blk:
                ps2 = ph.enter_context(nc.psum_tensor(f"e{ts_i}p", [128, 4096], F32))

                @blk.sync
                def _(sync):
                    for hc in range(NH):
                        if hc >= 2:
                            sync.wait_ge(s_t.h, t0 + hc - 1)
                        sp = s_p0 if hc % 2 == 0 else s_p1
                        sync.dma_start(out=w2q_t[:, (hc % 2) * 2048:(hc % 2) * 2048 + 2048],
                                       in_=bass.AP(w2q_hbm, hc * 128 * E, [[E, 128], [1, E]])
                                       ).then_inc(sp.h, 16)
                        sp.v += 16

                @blk.gpsimd
                def _(gp):
                    gp.dma_start(out=yn_t[:, :],
                                 in_=bass.AP(ynorm_hbm, ts_i * 256,
                                             [[TC, 128], [128 * TC, NE], [1, 256]])
                                 ).then_inc(s_g.h, 16)
                    s_g.v += 16

                @blk.tensor
                def _(tn):
                    for hc in range(NH):
                        tn.wait_ge((s_p0 if hc % 2 == 0 else s_p1).h,
                                   (p00 if hc % 2 == 0 else p10) + 16 * (hc // 2 + 1))
                        sl = (hc % 2) * 2048
                        for fc in range(NE):
                            ins = tn.matmul(ps2[:, fc * 256:(fc + 1) * 256],
                                            w2q_t[:, sl + fc * 128: sl + fc * 128 + 128],
                                            g_sb[:, hc * TC + ts_i * 256: hc * TC + ts_i * 256 + 256],
                                            start=(hc == 0 and fc % 2 == 0),
                                            stop=(hc == NH - 1 and fc % 2 == 1))
                        ins.then_inc(s_t.h, 1)
                    s_t.v += NH
                    for fc in range(NE):
                        tn.wait_ge(s_v.h, v0 + NE + fc + 1)
                        tn.matmul(ps2[0:1, 0:256], ones_f[:, 0:1], y2[:, fc * 256:(fc + 1) * 256],
                                  start=(fc == 0), stop=(fc == NE - 1))
                        tn.matmul(ps2[0:1, 512:768], ones_f[:, 0:1],
                                  sqt[:, (fc % 2) * 256:(fc % 2) * 256 + 256],
                                  start=(fc == 0), stop=(fc == NE - 1)).then_inc(s_t.h, 1)
                    s_t.v += NE
                    tn.wait_ge(s_v.h, v0 + 2 * NE + 2)
                    tn.matmul(ps2[:, 1024:1280], onesL_f[0:1, :], rB[0:1, :],
                              start=True, stop=True).then_inc(s_t.h, 1)
                    tn.wait_ge(s_v.h, v0 + 2 * NE + 3)
                    tn.matmul(ps2[:, 1536:1792], onesL_f[0:1, :], rA[0:1, :],
                              start=True, stop=True).then_inc(s_t.h, 1)
                    s_t.v += 2

                @blk.vector
                def _(vec):
                    vec.wait_ge(s_t.h, t0 + NH)
                    vec.wait_ge(s_g.h, g0 + 16)
                    for fc in range(NE):
                        vec.scalar_tensor_tensor(y2[:, fc * 256:(fc + 1) * 256],
                                                 ps2[:, fc * 256:(fc + 1) * 256],
                                                 w2s_pp[:, fc:fc + 1],
                                                 R2b[:, ts_i * 256:(ts_i + 1) * 256],
                                                 op0=ALU.mult, op1=ALU.mult)
                        vec.tensor_add(y2[:, fc * 256:(fc + 1) * 256],
                                       y2[:, fc * 256:(fc + 1) * 256],
                                       yn_t[:, fc * 256:(fc + 1) * 256]).then_inc(s_v.h, 1)
                    for fc in range(NE):
                        if fc >= 2:
                            vec.wait_ge(s_t.h, t0 + NH + fc - 1)
                        vec.tensor_mul(sqt[:, (fc % 2) * 256:(fc % 2) * 256 + 256],
                                       y2[:, fc * 256:(fc + 1) * 256],
                                       y2[:, fc * 256:(fc + 1) * 256]).then_inc(s_v.h, 1)
                    s_v.v += 2 * NE
                    vec.wait_ge(s_t.h, t0 + NH + NE)
                    vec.tensor_scalar(rA[0:1, :], ps2[0:1, 0:256], 1.0 / E, None, op0=ALU.mult)
                    vec.tensor_scalar(rB[0:1, :], ps2[0:1, 512:768], 1.0 / E, None, op0=ALU.mult)
                    vec.tensor_mul(A2[0:1, 0:256], rA[0:1, :], rA[0:1, :])
                    vec.tensor_sub(rB[0:1, :], rB[0:1, :], A2[0:1, 0:256])
                    vec.tensor_scalar(rB[0:1, :], rB[0:1, :], 1e-5, None, op0=ALU.add
                                      ).then_inc(s_v.h, 1)
                    s_v.v += 1
                    vec.wait_ge(s_s.h, s0 + 1)
                    vec.reciprocal(A2[0:1, 0:256], B2[0:1, 0:256])
                    vec.tensor_mul(B2[0:1, 0:256], A2[0:1, 0:256], A2[0:1, 0:256])
                    vec.tensor_mul(B2[0:1, 0:256], B2[0:1, 0:256], rB[0:1, :])
                    vec.tensor_scalar(B2[0:1, 0:256], B2[0:1, 0:256], -0.5, 1.5,
                                      op0=ALU.mult, op1=ALU.add)
                    vec.tensor_mul(rB[0:1, :], A2[0:1, 0:256], B2[0:1, 0:256])
                    vec.tensor_mul(rA[0:1, :], rA[0:1, :], rB[0:1, :])
                    vec.tensor_scalar(rA[0:1, :], rA[0:1, :], -1.0, None, op0=ALU.mult
                                      ).then_inc(s_v.h, 1)
                    s_v.v += 1
                    vec.wait_ge(s_t.h, t0 + NH + NE + 1)
                    vec.tensor_copy(A2[:, :], ps2[:, 1024:1280]).then_inc(s_v.h, 1)
                    s_v.v += 1
                    vec.wait_ge(s_t.h, t0 + NH + NE + 2)
                    vec.tensor_copy(B2[:, :], ps2[:, 1536:1792]).then_inc(s_v.h, 1)
                    s_v.v += 1
                    for fc in range(NE):
                        vec.tensor_mul(y2[:, fc * 256:(fc + 1) * 256],
                                       y2[:, fc * 256:(fc + 1) * 256], A2[:, :])
                        vec.tensor_add(y2[:, fc * 256:(fc + 1) * 256],
                                       y2[:, fc * 256:(fc + 1) * 256], B2[:, :]
                                       ).then_inc(s_v.h, 1)
                    s_v.v += NE

                @blk.scalar
                def _(sc):
                    sc.wait_ge(s_v.h, v0 + 2 * NE + 1)
                    sc.activation(B2[0:1, 0:256], rB[0:1, :], AF.Sqrt).then_inc(s_s.h, 1)
                    s_s.v += 1

                @blk.sync
                def _(sync2):
                    for fc in range(NE):
                        sync2.wait_ge(s_v.h, v0 + 2 * NE + 4 + fc + 1)
                        sync2.dma_start(out=bass.AP(out_d, fc * 128 * TC + ts_i * 256,
                                                    [[TC, 128], [1, 256]]),
                                        in_=y2[:, fc * 256:(fc + 1) * 256]).then_inc(s_o.h, 16)
                        s_o.v += 16
                    sync2.wait_ge(s_o.h, s_o.v)
    e_stack.close()

    top.close()
    return nc


def _host_prep(x, wq, wk, wv, wo, w1, w2):
    ins = {}
    ins["wqT"] = np.ascontiguousarray(wq.T)
    ins["wkT"] = np.ascontiguousarray(wk.T)
    ins["wvT"] = np.ascontiguousarray(wv.T)
    ins["woT"] = np.ascontiguousarray(wo.T)
    ins["w1T"] = np.ascontiguousarray(w1.T)
    ins["w2T"] = np.ascontiguousarray(w2.T)

    import jax
    import jax.numpy as jnp
    cpu = jax.devices("cpu")[0]
    with jax.default_device(cpu):
        inv_j = 1.0 / (1000000.0 ** (jnp.arange(0, D, 2, dtype=jnp.float32) / D))
        fang = jnp.arange(T, dtype=jnp.float32)[:, None] * inv_j[None, :]
        cos32 = np.asarray(jnp.concatenate([jnp.cos(fang), jnp.cos(fang)], axis=-1))
        sin32 = np.asarray(jnp.concatenate([jnp.sin(fang), jnp.sin(fang)], axis=-1))
        sf = np.float64(np.asarray(jnp.sqrt(jnp.float32(D))))
    cos = cos32.astype(np.float64).T
    sin = sin32.astype(np.float64).T
    ss = sin.copy()
    ss[0:64] = -sin[0:64]
    sc = 1.0 / sf

    def hl(a):
        hi = a.astype(np.float32)
        lo = (a - hi.astype(np.float64)).astype(np.float32)
        return np.ascontiguousarray(hi), np.ascontiguousarray(lo)

    import ml_dtypes
    ins["identb"] = np.eye(128, dtype=np.float16)
    ins["identf"] = np.eye(128, dtype=np.float32)
    percore = []
    for c in range(8):
        b, half = c // 2, c % 2
        toks = np.concatenate([np.arange((2 * i + half) * 128, (2 * i + half + 1) * 128)
                               for i in range(8)])
        d = dict(ins)
        d["xT"] = np.ascontiguousarray(x[b].T)
        d["xTm"] = np.ascontiguousarray(x[b].T[:, toks])
        d["coskhi"], d["cosklo"] = hl(cos)
        d["sskhi"], d["ssklo"] = hl(ss)
        d["cosqhi"], d["cosqlo"] = hl(cos[:, toks] * sc)
        d["ssqhi"], d["ssqlo"] = hl(ss[:, toks] * sc)
        mk = np.zeros((128, NPAIR * 4 * 256), dtype=np.float32)
        for j in range(NPAIR):
            nk = 4 * j + 4
            for ci_, kc in enumerate(range(nk - 4, nk)):
                for pp in range(2):
                    pos = 2 * j + pp
                    qt = (2 * pos + half) * 128 + np.arange(128)
                    kt = kc * 128 + np.arange(128)
                    blkm = (qt[None, :] >= kt[:, None]).astype(np.float32)
                    mk[:, (j * 4 + ci_) * 256 + pp * 128:(j * 4 + ci_) * 256 + (pp + 1) * 128] = blkm
        d["maskp"] = mk.astype(np.float16)
        percore.append(d)
    return percore


def kernel(L0_attn_x, L0_attn_wq, L0_attn_wk, L0_attn_wv, L0_attn_wo, L0_ffn_w1, L0_ffn_w2):
    x = np.ascontiguousarray(np.asarray(L0_attn_x, dtype=np.float32))
    args = [np.asarray(a, dtype=np.float32) for a in
            (L0_attn_wq, L0_attn_wk, L0_attn_wv, L0_attn_wo, L0_ffn_w1, L0_ffn_w2)]
    in_maps = _host_prep(x, *args)
    nc = build_program()
    res = run_bass_kernel_spmd(nc, in_maps, core_ids=list(range(8)))
    out = np.empty((B, T, E), np.float32)
    for c in range(8):
        o = res.results[c]["out"]
        half = c % 2
        for i in range(8):
            t0_ = (2 * i + half) * 128
            out[c // 2, t0_:t0_ + 128, :] = o[:, i * 128:(i + 1) * 128].T
    return out


# version 13
# speedup vs baseline: 1.0767x; 1.0767x over previous
import contextlib
import math

import numpy as np

import concourse.bass as bass
import concourse.mybir as mybir
from concourse.bass_utils import run_bass_kernel_spmd

F32 = mybir.dt.float32
BF16 = mybir.dt.bfloat16
FP16 = mybir.dt.float16
FP8 = mybir.dt.float8e4
I32 = mybir.dt.int32
AF = mybir.ActivationFunctionType
ALU = mybir.AluOpType
AX = mybir.AxisListType

B, T, E = 4, 2048, 2048
H, D = 16, 128
HID = 8192
TC = 1024
NE = E // 128
NH = HID // 128
NPAIR = 4

_DEBUG = False
_PHASES = 10


class Sem:
    def __init__(self, stack, nc, name):
        self.h = stack.enter_context(nc.semaphore(name))
        self.v = 0


def build_program():
    nc = bass.Bass("TRN2", target_bir_lowering=False, detect_race_conditions=False)

    xT = nc.dram_tensor("xT", [E, T], F32, kind="ExternalInput")
    xTm = nc.dram_tensor("xTm", [E, TC], F32, kind="ExternalInput")
    wqT = nc.dram_tensor("wqT", [E, E], F32, kind="ExternalInput")
    wkT = nc.dram_tensor("wkT", [E, E], F32, kind="ExternalInput")
    wvT = nc.dram_tensor("wvT", [E, E], F32, kind="ExternalInput")
    woT = nc.dram_tensor("woT", [E, E], F32, kind="ExternalInput")
    w1T = nc.dram_tensor("w1T", [E, HID], F32, kind="ExternalInput")
    w2T = nc.dram_tensor("w2T", [HID, E], F32, kind="ExternalInput")
    tabs = {}
    for nm, cols in [("cosq", TC), ("ssq", TC), ("cosk", T), ("ssk", T)]:
        for p in ("hi", "lo"):
            tabs[nm + p] = nc.dram_tensor(nm + p, [D, cols], F32, kind="ExternalInput")
    maskp = nc.dram_tensor("maskp", [128, NPAIR * 4 * 256], FP16, kind="ExternalInput")
    identb = nc.dram_tensor("identb", [128, 128], FP16, kind="ExternalInput")
    identf = nc.dram_tensor("identf", [128, 128], F32, kind="ExternalInput")
    out_d = nc.dram_tensor("out", [E, TC], F32, kind="ExternalOutput")
    w2q_hbm = nc.dram_tensor("w2q_hbm", [NH * 128, E], FP8)
    ynorm_hbm = nc.dram_tensor("ynorm_hbm", [E, TC], F32)
    xq1_hbm = nc.dram_tensor("xq1_hbm", [E, TC], BF16)
    r1_hbm = nc.dram_tensor("r1_hbm", [128, TC], F32)

    dbg = {}
    if _DEBUG:
        for nm, sh, dt in [("d_k", [D, T], FP16), ("d_q", [D, TC], FP16),
                           ("d_vt", [128, 16 * 128], FP16), ("d_ctx", [128, H * TC], FP16),
                           ("d_y1", [128, NE * TC], F32), ("d_xq1", [128, NE * TC], BF16),
                           ("d_g", [128, 8 * TC], FP16), ("d_s1", [128, TC], F32),
                           ("d_rows", [1, 2 * TC], F32), ("d_mrun", [128, TC], F32)]:
            dbg[nm] = nc.dram_tensor(nm, sh, dt, kind="ExternalOutput")

    top = contextlib.ExitStack()
    s_in = Sem(top, nc, "s_in")
    s_g = Sem(top, nc, "s_g")
    s_t = Sem(top, nc, "s_t")
    s_v = Sem(top, nc, "s_v")
    s_s = Sem(top, nc, "s_s")
    s_o = Sem(top, nc, "s_o")
    s_p0 = Sem(top, nc, "s_p0")
    s_p1 = Sem(top, nc, "s_p1")
    s_q0 = Sem(top, nc, "s_q0")
    s_q1 = Sem(top, nc, "s_q1")

    ones_f = top.enter_context(nc.sbuf_tensor("ones_f", [128, 1], F32))
    ones_b = top.enter_context(nc.sbuf_tensor("ones_b", [128, 1], FP16))
    onesL_f = top.enter_context(nc.sbuf_tensor("onesL_f", [1, 128], F32))
    negb12 = top.enter_context(nc.sbuf_tensor("negb12", [128, 1], F32))
    idb = top.enter_context(nc.sbuf_tensor("idb", [128, 128], FP16))
    idf = top.enter_context(nc.sbuf_tensor("idf", [128, 128], F32))
    w1s_pp = top.enter_context(nc.sbuf_tensor("w1s_pp", [128, NH], F32))
    n1_pp = top.enter_context(nc.sbuf_tensor("n1_pp", [128, NH], F32))
    w2s_pp = top.enter_context(nc.sbuf_tensor("w2s_pp", [128, NE], F32))
    cstack0 = contextlib.ExitStack()
    ctx_bf = cstack0.enter_context(nc.sbuf_tensor("ctx_bf", [128, H * TC], FP16))

    heads_stack = contextlib.ExitStack()
    tabsb = {nm: heads_stack.enter_context(nc.sbuf_tensor("t_" + nm, [D, cols], F32))
             for nm, cols in [("cosq", TC), ("ssq", TC), ("cosk", T), ("ssk", T)]}
    x_bf = heads_stack.enter_context(nc.sbuf_tensor("x_bf", [128, NE * T], FP16))
    xm_bf = heads_stack.enter_context(nc.sbuf_tensor("xm_bf", [128, NE * TC], FP16))
    mask_sb = heads_stack.enter_context(nc.sbuf_tensor("mask_sb", [128, NPAIR * 4 * 256], FP16))


    def _early_out(tag):
        with contextlib.ExitStack() as p0:
            dummy = p0.enter_context(nc.sbuf_tensor("dummy" + tag, [128, 256], F32))
            with nc.Block() as blk:
                @blk.vector
                def _(vec):
                    vec.memset(dummy[:, :], 0.0).then_inc(s_v.h, 1)
                    s_v.v += 1

                @blk.sync
                def _(sync):
                    sync.wait_ge(s_v.h, s_v.v)
                    for fc in range(NE):
                        for tsx in range(4):
                            sync.dma_start(out=bass.AP(out_d, fc * 128 * TC + tsx * 256,
                                                       [[TC, 128], [1, 256]]),
                                           in_=dummy[:, :]).then_inc(s_o.h, 16)
                            s_o.v += 16
                    sync.wait_ge(s_o.h, s_o.v)
        return nc

    with contextlib.ExitStack() as ph:
        tlo = {nm: ph.enter_context(nc.sbuf_tensor("tl_" + nm, [D, cols], F32))
               for nm, cols in [("cosq", TC), ("ssq", TC), ("cosk", T), ("ssk", T)]}
        with nc.Block() as blk:
            @blk.sync
            def _(sync):
                for nm in ("cosq", "ssq", "cosk", "ssk"):
                    sync.dma_start(out=tabsb[nm][:], in_=tabs[nm + "hi"].ap()[:, :]).then_inc(s_in.h, 16)
                    sync.dma_start(out=tlo[nm][:], in_=tabs[nm + "lo"].ap()[:, :]).then_inc(s_in.h, 16)
                    s_in.v += 32
                sync.dma_start(out=mask_sb[:], in_=maskp.ap()[:, :]).then_inc(s_in.h, 16)
                sync.dma_start(out=idb[:], in_=identb.ap()[:, :]).then_inc(s_in.h, 16)
                sync.dma_start(out=idf[:], in_=identf.ap()[:, :]).then_inc(s_in.h, 16)
                s_in.v += 48
                sync.wait_ge(s_in.h, s_in.v)

            @blk.gpsimd
            def _(gp):
                gp.dma_start(out=x_bf[:, :],
                             in_=bass.AP(xT, 0, [[T, 128], [128 * T, NE], [1, T]])).then_inc(s_g.h, 16)
                gp.dma_start(out=xm_bf[:, :],
                             in_=bass.AP(xTm, 0, [[TC, 128], [128 * TC, NE], [1, TC]])).then_inc(s_g.h, 16)
                s_g.v += 32
                gp.wait_ge(s_g.h, s_g.v)

            @blk.vector
            def _(vec):
                vec.memset(ones_f[:], 1.0)
                vec.memset(ones_b[:], 1.0)
                vec.memset(onesL_f[:], 1.0)
                vec.memset(negb12[:], -12.0)
                vec.wait_ge(s_in.h, s_in.v)
                for nm in ("cosq", "ssq", "cosk", "ssk"):
                    vec.tensor_add(tabsb[nm][:], tabsb[nm][:], tlo[nm][:])

    if _PHASES <= 1:
        heads_stack.close()
        return _early_out("p1")
    cosq, ssq, cosk, ssk = (tabsb[n] for n in ("cosq", "ssq", "cosk", "ssk"))

    def rope_evac(vec, ps, dst, dc0, cos_t, ss_t, c0, width):
        vec.tensor_tensor(tmpA[:, 0:width], ps[:, 0:width], cos_t[:, c0:c0 + width], op=ALU.mult)
        vec.tensor_tensor(tmpB[0:64, 0:width], ps[64:128, 0:width], ss_t[0:64, c0:c0 + width], op=ALU.mult)
        vec.tensor_tensor(tmpB[64:128, 0:width], ps[0:64, 0:width], ss_t[64:128, c0:c0 + width], op=ALU.mult)
        return vec.tensor_add(dst[:, dc0:dc0 + width], tmpA[:, 0:width], tmpB[:, 0:width])

    for h in range(H):
        hp = contextlib.ExitStack()
        wq_bf = hp.enter_context(nc.sbuf_tensor(f"wq{h}", [128, NE * 128], FP16))
        wk_bf = hp.enter_context(nc.sbuf_tensor(f"wk{h}", [128, NE * 128], FP16))
        wv_bf = hp.enter_context(nc.sbuf_tensor(f"wv{h}", [128, NE * 128], FP16))
        k_bf = hp.enter_context(nc.sbuf_tensor(f"k{h}", [D, T], FP16))
        q_bf = hp.enter_context(nc.sbuf_tensor(f"q{h}", [D, TC], FP16))
        v_fm = hp.enter_context(nc.sbuf_tensor(f"vf{h}", [D, T], FP16))
        v_tm = hp.enter_context(nc.sbuf_tensor(f"vt{h}", [128, 16 * 128], FP16))
        tmpA = hp.enter_context(nc.sbuf_tensor(f"ta{h}", [128, 512], F32))
        tmpB = hp.enter_context(nc.sbuf_tensor(f"tb{h}", [128, 512], F32))
        pt_bf = hp.enter_context(nc.sbuf_tensor(f"pt{h}", [128, 2 * 256], FP16))
        rcpb = hp.enter_context(nc.sbuf_tensor(f"rc{h}", [128, TC], F32))

        with contextlib.ExitStack() as pp:
            ps_a = pp.enter_context(nc.psum_tensor(f"pa{h}", [128, 512], F32))
            ps_b = pp.enter_context(nc.psum_tensor(f"pb{h}", [128, 512], F32))
            ps_tr = pp.enter_context(nc.psum_tensor(f"ptr{h}", [128, 128], FP16))
            g0, t0, v0 = s_g.v, s_t.v, s_v.v
            with nc.Block() as blk:
                @blk.gpsimd
                def _(gp):
                    for wsb, wdr in ((wq_bf, wqT), (wk_bf, wkT), (wv_bf, wvT)):
                        gp.dma_start(out=wsb[:],
                                     in_=bass.AP(wdr, h * 128, [[E, 128], [128 * E, NE], [1, 128]])
                                     ).then_inc(s_g.h, 16)
                    s_g.v += 48

                @blk.tensor
                def _(tn):
                    tn.wait_ge(s_g.h, g0 + 48)
                    for i in range(10):
                        ps = ps_a if i % 2 == 0 else ps_b
                        if i >= 2:
                            tn.wait_ge(s_v.h, v0 + i - 1)
                        if i < 4:
                            w, rhs_t, cbase, wd = wk_bf, x_bf, i * 512, T
                        elif i < 8:
                            w, rhs_t, cbase, wd = wv_bf, x_bf, (i - 4) * 512, T
                        else:
                            w, rhs_t, cbase, wd = wq_bf, xm_bf, (i - 8) * 512, TC
                        for ec in range(NE):
                            ins = tn.matmul(ps[:, :], w[:, ec * 128:(ec + 1) * 128],
                                            rhs_t[:, ec * wd + cbase: ec * wd + cbase + 512],
                                            start=(ec == 0), stop=(ec == NE - 1))
                        ins.then_inc(s_t.h, 1)
                    s_t.v += 10
                    tn.wait_ge(s_v.h, v0 + 8)
                    for kc in range(16):
                        if kc >= 1:
                            tn.wait_ge(s_v.h, v0 + 10 + kc)
                        tn.transpose(ps_tr[:, :], v_fm[:, kc * 128:(kc + 1) * 128],
                                     idb[:, :]).then_inc(s_t.h, 1)
                    s_t.v += 16

                @blk.vector
                def _(vec):
                    for i in range(10):
                        ps = ps_a if i % 2 == 0 else ps_b
                        vec.wait_ge(s_t.h, t0 + i + 1)
                        if i < 4:
                            ins = rope_evac(vec, ps, k_bf, i * 512, cosk, ssk, i * 512, 512)
                        elif i < 8:
                            ins = vec.tensor_copy(v_fm[:, (i - 4) * 512:(i - 3) * 512], ps[:, :])
                        else:
                            ins = rope_evac(vec, ps, q_bf, (i - 8) * 512, cosq, ssq, (i - 8) * 512, 512)
                        ins.then_inc(s_v.h, 1)
                    for kc in range(16):
                        vec.wait_ge(s_t.h, t0 + 10 + kc + 1)
                        vec.tensor_copy(v_tm[:, kc * 128:(kc + 1) * 128], ps_tr[:, :]).then_inc(s_v.h, 1)
                    s_v.v += 26

        with contextlib.ExitStack() as pp:
            ps_s0 = pp.enter_context(nc.psum_tensor(f"pss0{h}", [128, 256], F32))
            ps_s1 = pp.enter_context(nc.psum_tensor(f"pss1{h}", [128, 256], F32))
            ps_ctx = pp.enter_context(nc.psum_tensor(f"pctx{h}", [128, TC], F32))
            ps_row = pp.enter_context(nc.psum_tensor(f"prow{h}", [1, TC], F32))
            ps_bc = pp.enter_context(nc.psum_tensor(f"pbc{h}", [128, TC], F32))
            t0, v0, s0 = s_t.v, s_v.v, s_s.v
            sched = []
            for j in range(NPAIR):
                nk = 4 * j + 4
                for kc in range(nk):
                    sched.append((j, kc, nk))
            masked_idx = {}
            for n, (j, kc, nk) in enumerate(sched):
                if kc >= nk - 4:
                    masked_idx[n] = len(masked_idx) + 1
            NM = len(masked_idx)

            with nc.Block() as blk:
                @blk.tensor
                def _(tn):
                    for n, (j, kc, nk) in enumerate(sched):
                        sl = (n % 2) * 256
                        pss = ps_s0 if n % 2 == 0 else ps_s1
                        if n >= 2:
                            tn.wait_ge(s_s.h, s0 + n - 1)
                        tn.matmul(pss[:, :], k_bf[:, kc * 128:(kc + 1) * 128],
                                  q_bf[:, j * 256:(j + 1) * 256], start=True, stop=True
                                  ).then_inc(s_t.h, 1)
                        if n in masked_idx:
                            tn.wait_ge(s_v.h, v0 + masked_idx[n])
                        else:
                            tn.wait_ge(s_s.h, s0 + n + 1)
                        tn.matmul(ps_ctx[:, j * 256:(j + 1) * 256],
                                  v_tm[:, kc * 128:(kc + 1) * 128],
                                  pt_bf[:, sl:sl + 256], start=(kc == 0), stop=(kc == nk - 1))
                        tn.matmul(ps_row[0:1, j * 256:(j + 1) * 256], ones_b[:, 0:1],
                                  pt_bf[:, sl:sl + 256], start=(kc == 0), stop=(kc == nk - 1)
                                  ).then_inc(s_t.h, 1)
                    s_t.v += 80
                    tn.wait_ge(s_v.h, v0 + NM + 1)
                    for tb in range(2):
                        ins = tn.matmul(ps_bc[:, tb * 512:(tb + 1) * 512], onesL_f[0:1, :],
                                        rcpb[0:1, tb * 512:(tb + 1) * 512], start=True, stop=True)
                    ins.then_inc(s_t.h, 1)
                    s_t.v += 1

                @blk.scalar
                def _(sc):
                    for n, (j, kc, nk) in enumerate(sched):
                        sl = (n % 2) * 256
                        pss = ps_s0 if n % 2 == 0 else ps_s1
                        sc.wait_ge(s_t.h, t0 + 2 * n + 1)
                        sc.activation(pt_bf[:, sl:sl + 256], pss[:, :], AF.Exp,
                                      bias=negb12[:, 0:1]).then_inc(s_s.h, 1)
                    s_s.v += 80 if False else len(sched)

                @blk.vector
                def _(vec):
                    for n, (j, kc, nk) in enumerate(sched):
                        if n not in masked_idx:
                            continue
                        sl = (n % 2) * 256
                        mo = (j * 4 + (kc - (nk - 4))) * 256
                        vec.wait_ge(s_s.h, s0 + n + 1)
                        vec.tensor_mul(pt_bf[:, sl:sl + 256], pt_bf[:, sl:sl + 256],
                                       mask_sb[:, mo:mo + 256]).then_inc(s_v.h, 1)
                    vec.wait_ge(s_t.h, t0 + 80)
                    vec.reciprocal(rcpb[0:1, :], ps_row[0:1, :]).then_inc(s_v.h, 1)
                    s_v.v += NM + 1
                    vec.wait_ge(s_t.h, t0 + 81)
                    vec.tensor_copy(rcpb[:, 0:512], ps_bc[:, 0:512])
                    vec.tensor_copy(rcpb[:, 512:1024], ps_bc[:, 512:1024])
                    for j in range(NPAIR):
                        vec.tensor_tensor(ctx_bf[:, h * TC + j * 256: h * TC + (j + 1) * 256],
                                          ps_ctx[:, j * 256:(j + 1) * 256],
                                          rcpb[:, j * 256:(j + 1) * 256], op=ALU.mult)
                    vec.engine_nop().then_inc(s_v.h, 1)
                    s_v.v += 1
        if _DEBUG and h == 0:
            with nc.Block() as blk:
                @blk.sync
                def _(sync):
                    sync.dma_start(out=dbg["d_k"].ap()[:, :], in_=k_bf[:]).then_inc(s_o.h, 16)
                    sync.dma_start(out=dbg["d_q"].ap()[:, :], in_=q_bf[:]).then_inc(s_o.h, 16)
                    sync.dma_start(out=dbg["d_vt"].ap()[:, :], in_=v_tm[:]).then_inc(s_o.h, 16)
                    s_o.v += 48
                    sync.wait_ge(s_o.h, s_o.v)
        hp.close()

    if _DEBUG:
        with nc.Block() as blk:
            @blk.sync
            def _(sync):
                sync.dma_start(out=dbg["d_ctx"].ap()[:, :], in_=ctx_bf[:]).then_inc(s_o.h, 16)
                s_o.v += 16
                sync.wait_ge(s_o.h, s_o.v)

    heads_stack.close()

    if _PHASES <= 2:
        return _early_out("p2")
    y1 = cstack0.enter_context(nc.sbuf_tensor("y1", [128, NE * TC], F32))
    mrun = cstack0.enter_context(nc.sbuf_tensor("mrun", [128, TC], F32))
    mrun_mn = cstack0.enter_context(nc.sbuf_tensor("mrun_mn", [128, TC], F32))
    rowsA = cstack0.enter_context(nc.sbuf_tensor("rowsA", [1, TC], F32))
    rowsB = cstack0.enter_context(nc.sbuf_tensor("rowsB", [1, TC], F32))
    Ab = cstack0.enter_context(nc.sbuf_tensor("Ab", [128, TC], F32))
    Bb = cstack0.enter_context(nc.sbuf_tensor("Bb", [128, TC], F32))

    with contextlib.ExitStack() as ph:
        wo_bf = ph.enter_context(nc.sbuf_tensor("wo_bf", [128, NE * E], FP16))
        xres = ph.enter_context(nc.sbuf_tensor("xres", [128, 2 * TC], F32))
        ps_a = ph.enter_context(nc.psum_tensor("pc_a", [128, 512], F32))
        ps_b = ph.enter_context(nc.psum_tensor("pc_b", [128, 512], F32))
        g0, t0, v0, i0 = s_g.v, s_t.v, s_v.v, s_in.v
        p00, p10 = s_p0.v, s_p1.v
        with nc.Block() as blk:
            @blk.gpsimd
            def _(gp):
                for ec in range(NE):
                    gp.dma_start(out=wo_bf[:, ec * E:(ec + 1) * E],
                                 in_=bass.AP(woT, ec * 128 * E, [[E, 128], [1, E]])
                                 ).then_inc(s_g.h, 16)
                s_g.v += 16 * NE

            @blk.sync
            def _(sync):
                for fc in range(NE):
                    sp = s_p0 if fc % 2 == 0 else s_p1
                    sync.dma_start(out=xres[:, (fc % 2) * TC:(fc % 2) * TC + TC],
                                   in_=bass.AP(xTm, fc * 128 * TC, [[TC, 128], [1, TC]])
                                   ).then_inc(sp.h, 16)
                    sp.v += 16
                    if fc < NE - 1:
                        sync.wait_ge(s_v.h, v0 + 2 * fc)

            @blk.tensor
            def _(tn):
                tn.wait_ge(s_g.h, g0 + 16 * NE)
                for fc in range(NE):
                    for th in range(2):
                        i = 2 * fc + th
                        ps = ps_a if i % 2 == 0 else ps_b
                        if i >= 2:
                            tn.wait_ge(s_v.h, v0 + i - 1)
                        for ec in range(NE):
                            ins = tn.matmul(ps[:, :], wo_bf[:, ec * E + fc * 128: ec * E + fc * 128 + 128],
                                            ctx_bf[:, ec * TC + th * 512: ec * TC + th * 512 + 512],
                                            start=(ec == 0), stop=(ec == NE - 1))
                        ins.then_inc(s_t.h, 1)
                s_t.v += 32

            @blk.vector
            def _(vec):
                for fc in range(NE):
                    for th in range(2):
                        i = 2 * fc + th
                        ps = ps_a if i % 2 == 0 else ps_b
                        vec.wait_ge(s_t.h, t0 + i + 1)
                        sp = s_p0 if fc % 2 == 0 else s_p1
                        vec.wait_ge(sp.h, (p00 if fc % 2 == 0 else p10) + 16 * (fc // 2 + 1))
                        vec.tensor_add(y1[:, fc * TC + th * 512: fc * TC + th * 512 + 512],
                                       ps[:, :], xres[:, (fc % 2) * TC + th * 512:(fc % 2) * TC + th * 512 + 512]
                                       ).then_inc(s_v.h, 1)
                s_v.v += 32

    if _PHASES <= 3:
        cstack0.close()
        return _early_out("p3")
    with contextlib.ExitStack() as ph:
        sqt = ph.enter_context(nc.sbuf_tensor("sqt", [128, 2 * TC], F32))
        ps_sum = ph.enter_context(nc.psum_tensor("pc_sum", [1, TC], F32))
        ps_sq = ph.enter_context(nc.psum_tensor("pc_sq", [1, TC], F32))
        ps_bc = ph.enter_context(nc.psum_tensor("pc_bc", [128, TC], F32))
        t0, v0, s0 = s_t.v, s_v.v, s_s.v
        with nc.Block() as blk:
            @blk.vector
            def _(vec):
                for fc in range(NE):
                    if fc >= 2:
                        vec.wait_ge(s_t.h, t0 + fc - 1)
                    vec.tensor_mul(sqt[:, (fc % 2) * TC:(fc % 2) * TC + TC],
                                   y1[:, fc * TC:(fc + 1) * TC],
                                   y1[:, fc * TC:(fc + 1) * TC]).then_inc(s_v.h, 1)
                s_v.v += NE
                vec.wait_ge(s_t.h, t0 + NE)
                vec.tensor_scalar(rowsA[0:1, :], ps_sum[0:1, :], 1.0 / E, None, op0=ALU.mult)
                vec.tensor_scalar(rowsB[0:1, :], ps_sq[0:1, :], 1.0 / E, None, op0=ALU.mult)
                vec.tensor_mul(Ab[0:1, 0:TC], rowsA[0:1, :], rowsA[0:1, :])
                vec.tensor_sub(rowsB[0:1, :], rowsB[0:1, :], Ab[0:1, 0:TC])
                vec.tensor_scalar(rowsB[0:1, :], rowsB[0:1, :], 1e-5, None, op0=ALU.add
                                  ).then_inc(s_v.h, 1)
                s_v.v += 1
                vec.wait_ge(s_s.h, s0 + 1)
                vec.reciprocal(Ab[0:1, 0:TC], Bb[0:1, 0:TC])
                vec.tensor_mul(Bb[0:1, 0:TC], Ab[0:1, 0:TC], Ab[0:1, 0:TC])
                vec.tensor_mul(Bb[0:1, 0:TC], Bb[0:1, 0:TC], rowsB[0:1, :])
                vec.tensor_scalar(Bb[0:1, 0:TC], Bb[0:1, 0:TC], -0.5, 1.5, op0=ALU.mult, op1=ALU.add)
                vec.tensor_mul(rowsB[0:1, :], Ab[0:1, 0:TC], Bb[0:1, 0:TC])
                vec.tensor_mul(rowsA[0:1, :], rowsA[0:1, :], rowsB[0:1, :])
                vec.tensor_scalar(rowsA[0:1, :], rowsA[0:1, :], -1.0, None, op0=ALU.mult
                                  ).then_inc(s_v.h, 1)
                s_v.v += 1
                vec.wait_ge(s_t.h, t0 + NE + 1)
                vec.tensor_copy(Ab[:, :], ps_bc[:, :]).then_inc(s_v.h, 1)
                s_v.v += 1
                vec.wait_ge(s_t.h, t0 + NE + 2)
                vec.tensor_copy(Bb[:, :], ps_bc[:, :]).then_inc(s_v.h, 1)
                s_v.v += 1
                vec.memset(mrun[:, :], 0.0)
                vec.memset(mrun_mn[:, :], 0.0)
                for fc in range(NE):
                    vec.tensor_mul(y1[:, fc * TC:(fc + 1) * TC], y1[:, fc * TC:(fc + 1) * TC], Ab[:, :])
                    vec.tensor_add(y1[:, fc * TC:(fc + 1) * TC], y1[:, fc * TC:(fc + 1) * TC], Bb[:, :])
                    vec.tensor_max(mrun[:, :], mrun[:, :], y1[:, fc * TC:(fc + 1) * TC])
                    vec.tensor_tensor(mrun_mn[:, :], mrun_mn[:, :], y1[:, fc * TC:(fc + 1) * TC],
                                      op=ALU.min)
                    vec.engine_nop().then_inc(s_v.h, 1)
                vec.tensor_scalar(mrun_mn[:, :], mrun_mn[:, :], -1.0, None, op0=ALU.mult)
                vec.tensor_max(mrun[:, :], mrun[:, :], mrun_mn[:, :])
                s_v.v += NE

            @blk.scalar
            def _(sc):
                sc.wait_ge(s_v.h, v0 + NE + 1)
                sc.activation(Bb[0:1, 0:TC], rowsB[0:1, :], AF.Sqrt).then_inc(s_s.h, 1)
                s_s.v += 1

            @blk.tensor
            def _(tn):
                for fc in range(NE):
                    tn.wait_ge(s_v.h, v0 + fc + 1)
                    for th in range(2):
                        tn.matmul(ps_sum[0:1, th * 512:(th + 1) * 512], ones_f[:, 0:1],
                                  y1[:, fc * TC + th * 512: fc * TC + th * 512 + 512],
                                  start=(fc == 0), stop=(fc == NE - 1))
                        ins = tn.matmul(ps_sq[0:1, th * 512:(th + 1) * 512], ones_f[:, 0:1],
                                        sqt[:, (fc % 2) * TC + th * 512:(fc % 2) * TC + th * 512 + 512],
                                        start=(fc == 0), stop=(fc == NE - 1))
                    ins.then_inc(s_t.h, 1)
                s_t.v += NE
                tn.wait_ge(s_v.h, v0 + NE + 2)
                for tb in range(2):
                    ins = tn.matmul(ps_bc[:, tb * 512:(tb + 1) * 512], onesL_f[0:1, :],
                                    rowsB[0:1, tb * 512:(tb + 1) * 512], start=True, stop=True)
                ins.then_inc(s_t.h, 1)
                s_t.v += 1
                tn.wait_ge(s_v.h, v0 + NE + 3)
                for tb in range(2):
                    ins = tn.matmul(ps_bc[:, tb * 512:(tb + 1) * 512], onesL_f[0:1, :],
                                    rowsA[0:1, tb * 512:(tb + 1) * 512], start=True, stop=True)
                ins.then_inc(s_t.h, 1)
                s_t.v += 1

            @blk.sync
            def _(sync):
                for fc in range(NE):
                    sync.wait_ge(s_v.h, v0 + NE + 4 + fc + 1)
                    sync.dma_start(out=bass.AP(ynorm_hbm, fc * 128 * TC, [[TC, 128], [1, TC]]),
                                   in_=y1[:, fc * TC:(fc + 1) * TC]).then_inc(s_o.h, 16)
                    s_o.v += 16
                sync.wait_ge(s_o.h, s_o.v)

    if _PHASES <= 4:
        cstack0.close()
        return _early_out("p4")
    with contextlib.ExitStack() as ph:
        xq1 = ph.enter_context(nc.sbuf_tensor("xq1c", [128, NE * TC], BF16))
        R1b = ph.enter_context(nc.sbuf_tensor("R1bc", [128, TC], F32))
        mca = ph.enter_context(nc.sbuf_tensor("mca", [128, 8], F32))
        v_t = ph.enter_context(nc.sbuf_tensor("qv", [128, TC], F32))
        ci_t = ph.enter_context(nc.sbuf_tensor("qc", [128, TC], I32))
        r_t = ph.enter_context(nc.sbuf_tensor("qr", [128, TC], F32))
        an_t = ph.enter_context(nc.sbuf_tensor("qa", [128, TC], F32))
        S1b = ph.enter_context(nc.sbuf_tensor("S1b", [128, TC], F32))
        ps_tr = ph.enter_context(nc.psum_tensor("pq_tr", [128, 128], F32))
        ps_tc0 = ph.enter_context(nc.psum_tensor("pq_tc0", [1, 128], F32))
        ps_tc1 = ph.enter_context(nc.psum_tensor("pq_tc1", [1, 128], F32))
        ps_bc = ph.enter_context(nc.psum_tensor("pq_bc", [128, TC], F32))
        t0, v0, s0 = s_t.v, s_v.v, s_s.v
        with nc.Block() as blk:
            @blk.tensor
            def _(tn):
                for jt in range(8):
                    if jt >= 1:
                        tn.wait_ge(s_v.h, v0 + jt)
                    tn.transpose(ps_tr[:, :], mrun[:, jt * 128:(jt + 1) * 128],
                                 idf[:, :]).then_inc(s_t.h, 1)
                s_t.v += 8
                for i in range(8):
                    tn.wait_ge(s_v.h, v0 + (i + 1 if i < 2 else 8 + i))
                    tn.transpose((ps_tc0 if i % 2 == 0 else ps_tc1)[0:1, :],
                                 mca[:, i:i + 1], idf[:, :]).then_inc(s_t.h, 1)
                s_t.v += 8
                tn.wait_ge(s_v.h, v0 + 17)
                for tb in range(2):
                    ins = tn.matmul(ps_bc[:, tb * 512:(tb + 1) * 512], onesL_f[0:1, :],
                                    rowsA[0:1, tb * 512:(tb + 1) * 512], start=True, stop=True)
                ins.then_inc(s_t.h, 1)
                s_t.v += 1
                tn.wait_ge(s_v.h, v0 + 18)
                for tb in range(2):
                    ins = tn.matmul(ps_bc[:, tb * 512:(tb + 1) * 512], onesL_f[0:1, :],
                                    rowsB[0:1, tb * 512:(tb + 1) * 512], start=True, stop=True)
                ins.then_inc(s_t.h, 1)
                s_t.v += 1

            @blk.vector
            def _(vec):
                for jt in range(8):
                    vec.wait_ge(s_t.h, t0 + jt + 1)
                    vec.reduce_max(mca[:, jt:jt + 1], ps_tr[:, :], axis=AX.X).then_inc(s_v.h, 1)
                for i in range(8):
                    vec.wait_ge(s_t.h, t0 + 8 + i + 1)
                    vec.tensor_copy(rowsA[0:1, i * 128:(i + 1) * 128],
                                    (ps_tc0 if i % 2 == 0 else ps_tc1)[0:1, :]).then_inc(s_v.h, 1)
                s_v.v += 16
                vec.tensor_scalar(rowsB[0:1, :], rowsA[0:1, :], 1e-5, 1.0 / 127.0,
                                  op0=ALU.max, op1=ALU.mult)
                vec.reciprocal(rowsA[0:1, :], rowsB[0:1, :])
                vec.engine_nop().then_inc(s_v.h, 1)
                s_v.v += 1
                vec.wait_ge(s_t.h, t0 + 17)
                vec.tensor_copy(S1b[:, :], ps_bc[:, :]).then_inc(s_v.h, 1)
                s_v.v += 1
                vec.wait_ge(s_t.h, t0 + 18)
                vec.tensor_copy(R1b[:, :], ps_bc[:, :]).then_inc(s_v.h, 1)
                s_v.v += 1
                for fc in range(NE):
                    vec.tensor_mul(v_t[:, :], y1[:, fc * TC:(fc + 1) * TC], S1b[:, :])
                    vec.engine_nop().then_inc(s_v.h, 1)
                    vec.wait_ge(s_s.h, s0 + fc + 1)
                    vec.tensor_sub(r_t[:, :], v_t[:, :], ci_t[:, :])
                    vec.tensor_scalar(an_t[:, :], r_t[:, :], 0.5, -1.0, op0=ALU.is_gt, op1=ALU.mult)
                    vec.scalar_tensor_tensor(r_t[:, :], r_t[:, :], -0.5, an_t[:, :],
                                             op0=ALU.is_lt, op1=ALU.add)
                    vec.tensor_sub(xq1[:, fc * TC:(fc + 1) * TC], ci_t[:, :], r_t[:, :])
                    vec.engine_nop().then_inc(s_v.h, 1)
                s_v.v += 2 * NE

            @blk.scalar
            def _(sc):
                for fc in range(NE):
                    sc.wait_ge(s_v.h, v0 + 19 + 2 * fc + 1)
                    sc.activation(ci_t[:, :], v_t[:, :], AF.Identity).then_inc(s_s.h, 1)
                s_s.v += NE

            @blk.sync
            def _(sync):
                for fc in range(NE):
                    sync.wait_ge(s_v.h, v0 + 19 + 2 * fc + 2)
                    sync.dma_start(out=bass.AP(xq1_hbm, fc * 128 * TC, [[TC, 128], [1, TC]]),
                                   in_=xq1[:, fc * TC:(fc + 1) * TC]).then_inc(s_o.h, 16)
                    s_o.v += 16
                sync.wait_ge(s_v.h, v0 + 19)
                sync.dma_start(out=r1_hbm.ap()[:, :], in_=R1b[:, :]).then_inc(s_o.h, 16)
                s_o.v += 16
                if _DEBUG:
                    sync.dma_start(out=dbg["d_y1"].ap()[:, :], in_=y1[:, :]).then_inc(s_o.h, 16)
                    sync.dma_start(out=dbg["d_xq1"].ap()[:, :], in_=xq1[:, :]).then_inc(s_o.h, 16)
                    sync.dma_start(out=dbg["d_s1"].ap()[:, :], in_=S1b[:, :]).then_inc(s_o.h, 16)
                    sync.dma_start(out=dbg["d_rows"].ap()[0:1, 0:TC], in_=rowsA[0:1, :]).then_inc(s_o.h, 16)
                    sync.dma_start(out=dbg["d_rows"].ap()[0:1, TC:2 * TC], in_=rowsB[0:1, :]).then_inc(s_o.h, 16)
                    sync.dma_start(out=dbg["d_mrun"].ap()[:, :], in_=mrun[:, :]).then_inc(s_o.h, 16)
                    s_o.v += 96
                sync.wait_ge(s_o.h, s_o.v)

    cstack0.close()

    if _PHASES <= 5:
        return _early_out("p5")

    with contextlib.ExitStack() as ph:
        stg = ph.enter_context(nc.sbuf_tensor("d0stg", [128, 2 * 8192], F32))
        absb = ph.enter_context(nc.sbuf_tensor("d0abs", [128, 8192], F32))
        s1row = ph.enter_context(nc.sbuf_tensor("s1row", [1, HID], F32))
        n1row = ph.enter_context(nc.sbuf_tensor("n1row", [1, HID], F32))
        ps_r = ph.enter_context(nc.psum_tensor("d0pr", [1, 512], F32))
        g0 = s_in.v
        p00, p10 = s_p0.v, s_p1.v
        t0, v0, s0 = s_t.v, s_v.v, s_s.v
        with nc.Block() as blk:
            @blk.sync
            def _(sync):
                for hb in range(16):
                    if hb >= 2:
                        sync.wait_ge(s_s.h, s0 + hb - 1)
                    sp = s_p0 if hb % 2 == 0 else s_p1
                    sync.dma_start(out=stg[:, (hb % 2) * 8192:(hb % 2) * 8192 + 8192],
                                   in_=bass.AP(w1T, hb * 512, [[HID, 128], [128 * HID, NE], [1, 512]])
                                   ).then_inc(sp.h, 16)
                    sp.v += 16

            @blk.scalar
            def _(sc):
                for hb in range(16):
                    sp0 = p00 if hb % 2 == 0 else p10
                    sc.wait_ge((s_p0 if hb % 2 == 0 else s_p1).h, sp0 + 16 * (hb // 2 + 1))
                    if hb >= 1:
                        sc.wait_ge(s_t.h, t0 + hb)
                    sc.activation(absb[:, :], stg[:, (hb % 2) * 8192:(hb % 2) * 8192 + 8192],
                                  AF.Abs).then_inc(s_s.h, 1)
                s_s.v += 16

            @blk.tensor
            def _(tn):
                for hb in range(16):
                    tn.wait_ge(s_s.h, s0 + hb + 1)
                    if hb >= 1:
                        tn.wait_ge(s_v.h, v0 + hb)
                    for ec in range(NE):
                        ins = tn.matmul(ps_r[0:1, :], ones_f[:, 0:1],
                                        absb[:, ec * 512:(ec + 1) * 512],
                                        start=(ec == 0), stop=(ec == NE - 1))
                    ins.then_inc(s_t.h, 1)
                s_t.v += 16

            @blk.vector
            def _(vec):
                for hb in range(16):
                    vec.wait_ge(s_t.h, t0 + hb + 1)
                    vec.tensor_scalar(s1row[0:1, hb * 512:(hb + 1) * 512], ps_r[0:1, :],
                                      1.0 / E, 1e-5, op0=ALU.mult, op1=ALU.max).then_inc(s_v.h, 1)
                s_v.v += 16
                vec.reciprocal(n1row[0:1, :], s1row[0:1, :])
                vec.tensor_scalar(n1row[0:1, :], n1row[0:1, :], -2.0, None, op0=ALU.mult)

        with contextlib.ExitStack() as p2:
            ps_tp = p2.enter_context(nc.psum_tensor("d0tp", [128, 1], F32))
            t0, v0 = s_t.v, s_v.v
            with nc.Block() as blk:
                @blk.tensor
                def _(tn):
                    for i in range(2 * NH):
                        row = s1row if i < NH else n1row
                        off = i % NH
                        if i >= 1:
                            tn.wait_ge(s_v.h, v0 + i)
                        tn.transpose(ps_tp[:, :], row[0:1, off * 128:(off + 1) * 128],
                                     idf[0:1, 0:1]).then_inc(s_t.h, 1)
                    s_t.v += 2 * NH

                @blk.vector
                def _(vec):
                    for i in range(2 * NH):
                        dst = w1s_pp if i < NH else n1_pp
                        off = i % NH
                        vec.wait_ge(s_t.h, t0 + i + 1)
                        vec.tensor_copy(dst[:, off:off + 1], ps_tp[:, :]).then_inc(s_v.h, 1)
                    s_v.v += 2 * NH

    with contextlib.ExitStack() as ph2:
        stg = ph2.enter_context(nc.sbuf_tensor("d0s2", [128, 2 * 2048], F32))
        absb = ph2.enter_context(nc.sbuf_tensor("d0a2", [128, 2048], F32))
        s2row = ph2.enter_context(nc.sbuf_tensor("s2row", [1, E], F32))
        n2row = ph2.enter_context(nc.sbuf_tensor("n2row", [1, E], F32))
        ps_r = ph2.enter_context(nc.psum_tensor("d0p2", [1, 2048], F32))
        g0 = s_in.v
        p00, p10 = s_p0.v, s_p1.v
        t0, v0, s0 = s_t.v, s_v.v, s_s.v
        with nc.Block() as blk:
            @blk.sync
            def _(sync):
                for hc in range(NH):
                    if hc >= 2:
                        sync.wait_ge(s_s.h, s0 + hc - 1)
                    sp = s_p0 if hc % 2 == 0 else s_p1
                    sync.dma_start(out=stg[:, (hc % 2) * 2048:(hc % 2) * 2048 + 2048],
                                   in_=bass.AP(w2T, hc * 128 * E, [[E, 128], [1, E]])
                                   ).then_inc(sp.h, 16)
                    sp.v += 16

            @blk.scalar
            def _(sc):
                for hc in range(NH):
                    sp0 = p00 if hc % 2 == 0 else p10
                    sc.wait_ge((s_p0 if hc % 2 == 0 else s_p1).h, sp0 + 16 * (hc // 2 + 1))
                    if hc >= 1:
                        sc.wait_ge(s_t.h, t0 + hc)
                    sc.activation(absb[:, :], stg[:, (hc % 2) * 2048:(hc % 2) * 2048 + 2048],
                                  AF.Abs).then_inc(s_s.h, 1)
                s_s.v += NH

            @blk.tensor
            def _(tn):
                for hc in range(NH):
                    tn.wait_ge(s_s.h, s0 + hc + 1)
                    for fg in range(4):
                        ins = tn.matmul(ps_r[0:1, fg * 512:(fg + 1) * 512], ones_f[:, 0:1],
                                        absb[:, fg * 512:(fg + 1) * 512],
                                        start=(hc == 0), stop=(hc == NH - 1))
                    ins.then_inc(s_t.h, 1)
                s_t.v += NH

            @blk.vector
            def _(vec):
                vec.wait_ge(s_t.h, t0 + NH)
                vec.tensor_scalar(s2row[0:1, :], ps_r[0:1, :], 1.0 / HID, 1e-5,
                                  op0=ALU.mult, op1=ALU.max)
                vec.reciprocal(n2row[0:1, :], s2row[0:1, :])
                vec.tensor_scalar(n2row[0:1, :], n2row[0:1, :], -2.0, None, op0=ALU.mult)

        with contextlib.ExitStack() as p2:
            ps_tp = p2.enter_context(nc.psum_tensor("dspp", [128, 1], F32))
            t0, v0 = s_t.v, s_v.v
            with nc.Block() as blk:
                @blk.tensor
                def _(tn):
                    for i in range(NE):
                        if i >= 1:
                            tn.wait_ge(s_v.h, v0 + i)
                        tn.transpose(ps_tp[:, :], s2row[0:1, i * 128:(i + 1) * 128],
                                     idf[0:1, 0:1]).then_inc(s_t.h, 1)
                    s_t.v += NE

                @blk.vector
                def _(vec):
                    for i in range(NE):
                        vec.wait_ge(s_t.h, t0 + i + 1)
                        vec.tensor_copy(w2s_pp[:, i:i + 1], ps_tp[:, :]).then_inc(s_v.h, 1)
                    s_v.v += NE

        ph = ph2
        stg = ph.enter_context(nc.sbuf_tensor("d1s", [128, 2 * 2048], F32))
        ub = ph.enter_context(nc.sbuf_tensor("d1u", [128, 2048], F32))
        bn = ph.enter_context(nc.sbuf_tensor("d1b", [128, 2048], BF16))
        n2b = ph.enter_context(nc.sbuf_tensor("n2b", [128, 2048], F32))
        w2q_t = ph.enter_context(nc.sbuf_tensor("w2q_t", [128, 2 * 2048], FP8))
        ps_bc = ph.enter_context(nc.psum_tensor("d1p", [128, 2048], F32))
        g0 = s_in.v
        t0, v0, o0 = s_t.v, s_v.v, s_o.v
        p00, p10, q00, q10 = s_p0.v, s_p1.v, s_q0.v, s_q1.v
        with nc.Block() as blk:
            @blk.tensor
            def _(tn):
                for fg in range(4):
                    ins = tn.matmul(ps_bc[:, fg * 512:(fg + 1) * 512], onesL_f[0:1, :],
                                    n2row[0:1, fg * 512:(fg + 1) * 512], start=True, stop=True)
                ins.then_inc(s_t.h, 1)
                s_t.v += 1

            @blk.sync
            def _(sync):
                for hc in range(NH):
                    if hc >= 2:
                        sync.wait_ge(s_v.h, v0 + hc - 1)
                    sp = s_p0 if hc % 2 == 0 else s_p1
                    sync.dma_start(out=stg[:, (hc % 2) * 2048:(hc % 2) * 2048 + 2048],
                                   in_=bass.AP(w2T, hc * 128 * E, [[E, 128], [1, E]])
                                   ).then_inc(sp.h, 16)
                    sp.v += 16

            @blk.vector
            def _(vec):
                vec.wait_ge(s_t.h, t0 + 1)
                vec.tensor_copy(n2b[:, :], ps_bc[:, :])
                for hc in range(NH):
                    vec.wait_ge((s_p0 if hc % 2 == 0 else s_p1).h,
                                (p00 if hc % 2 == 0 else p10) + 16 * (hc // 2 + 1))
                    if hc >= 2:
                        vec.wait_ge((s_q0 if hc % 2 == 0 else s_q1).h,
                                    (q00 if hc % 2 == 0 else q10) + 16 * ((hc - 2) // 2 + 1))
                    sl = (hc % 2) * 2048
                    vec.tensor_mul(ub[:, :], stg[:, sl:sl + 2048], n2b[:, :])
                    vec.tensor_scalar(bn[:, :], ub[:, :], 1.0, -1.0, op0=ALU.is_gt, op1=ALU.mult)
                    vec.scalar_tensor_tensor(w2q_t[:, sl:sl + 2048], ub[:, :], -1.0, bn[:, :],
                                             op0=ALU.is_lt, op1=ALU.add).then_inc(s_v.h, 1)
                s_v.v += NH

            @blk.gpsimd
            def _(gp):
                for hc in range(NH):
                    gp.wait_ge(s_v.h, v0 + hc + 1)
                    sl = (hc % 2) * 2048
                    sq = s_q0 if hc % 2 == 0 else s_q1
                    gp.dma_start(out=bass.AP(w2q_hbm, hc * 128 * E, [[E, 128], [1, E]]),
                                 in_=w2q_t[:, sl:sl + 2048]).then_inc(sq.h, 16)
                    sq.v += 16
                gp.wait_ge(s_q0.h, s_q0.v)
                gp.wait_ge(s_q1.h, s_q1.v)

    if _PHASES <= 6:
        return _early_out("p6")
    e_stack = contextlib.ExitStack()
    g_sb = e_stack.enter_context(nc.sbuf_tensor("g_sb", [128, NH * TC], FP16))
    mg = e_stack.enter_context(nc.sbuf_tensor("mg", [128, TC], F32))
    mg_mn = e_stack.enter_context(nc.sbuf_tensor("mg_mn", [128, TC], F32))

    with contextlib.ExitStack() as ph:
        stg = ph.enter_context(nc.sbuf_tensor("d2s", [128, 2 * 2048], F32))
        ub = ph.enter_context(nc.sbuf_tensor("d2u", [128, 1024], F32))
        bn = ph.enter_context(nc.sbuf_tensor("d2b", [128, 1024], BF16))
        n1bt = ph.enter_context(nc.sbuf_tensor("n1bt", [128, 2 * 128], F32))
        w1q_t = ph.enter_context(nc.sbuf_tensor("w1q_t", [128, 2 * 2048], FP8))
        t0t = ph.enter_context(nc.sbuf_tensor("d2t0", [128, 4 * 512], FP16))
        n1r_sb = ph.enter_context(nc.sbuf_tensor("n1r_sb", [1, 2 * 128], F32))
        xq1 = ph.enter_context(nc.sbuf_tensor("xq1d", [128, NE * TC], BF16))
        R1b = ph.enter_context(nc.sbuf_tensor("R1bd", [128, TC], F32))
        ps_m = ph.enter_context(nc.psum_tensor("d2pm", [128, 2048], F32))
        ps_bc0 = ph.enter_context(nc.psum_tensor("d2pb0", [128, 128], F32))
        ps_bc1 = ph.enter_context(nc.psum_tensor("d2pb1", [128, 128], F32))
        ps_n10 = ph.enter_context(nc.psum_tensor("d2pn0", [1, 128], F32))
        ps_n11 = ph.enter_context(nc.psum_tensor("d2pn1", [1, 128], F32))

        T0, V0, S0 = s_t.v, s_v.v, s_s.v
        P00, P10 = s_p0.v, s_p1.v
        I0 = s_in.v
        with nc.Block() as blk:
            @blk.sync
            def _(sync):
                sync.dma_start(out=xq1[:, :],
                               in_=bass.AP(xq1_hbm, 0, [[TC, 128], [128 * TC, NE], [1, TC]])
                               ).then_inc(s_in.h, 16)
                sync.dma_start(out=R1b[:, :], in_=r1_hbm.ap()[:, :]).then_inc(s_in.h, 16)
                s_in.v += 32
                for hc in range(NH):
                    if hc >= 2:
                        sync.wait_ge(s_v.h, V0 + 7 * (hc - 2) + 3)
                    sp = s_p0 if hc % 2 == 0 else s_p1
                    sync.dma_start(out=stg[:, (hc % 2) * 2048:(hc % 2) * 2048 + 2048],
                                   in_=bass.AP(w1T, hc * 128, [[HID, 128], [128 * HID, NE], [1, 128]])
                                   ).then_inc(sp.h, 16)
                    sp.v += 16

            @blk.tensor
            def _(tn):
                tn.wait_ge(s_in.h, I0 + 32)
                for hc in range(NH):
                    psn = ps_n10 if hc % 2 == 0 else ps_n11
                    psb = ps_bc0 if hc % 2 == 0 else ps_bc1
                    if hc >= 2:
                        tn.wait_ge(s_v.h, V0 + 7 * (hc - 2) + 1)
                    tn.transpose(psn[0:1, :], n1_pp[:, hc:hc + 1], idf[:, :]).then_inc(s_t.h, 1)
                    tn.wait_ge(s_v.h, V0 + 7 * hc + 1)
                    tn.matmul(psb[:, :], onesL_f[0:1, :],
                              n1r_sb[0:1, (hc % 2) * 128:(hc % 2) * 128 + 128],
                              start=True, stop=True).then_inc(s_t.h, 1)
                    tn.wait_ge(s_v.h, V0 + 7 * hc + 3)
                    for th in range(2):
                        sl = (hc % 2) * 1024
                        for ec in range(NE):
                            ins = tn.matmul(ps_m[:, sl + th * 512: sl + th * 512 + 512],
                                            w1q_t[:, (hc % 2) * 2048 + ec * 128:(hc % 2) * 2048 + ec * 128 + 128],
                                            xq1[:, ec * TC + th * 512: ec * TC + th * 512 + 512],
                                            start=(ec == 0), stop=(ec == NE - 1))
                        ins.then_inc(s_t.h, 1)
                s_t.v += 4 * NH

            @blk.vector
            def _(vec):
                vec.memset(mg[:, :], 0.0)
                vec.memset(mg_mn[:, :], 0.0)
                for hc in range(NH):
                    psn = ps_n10 if hc % 2 == 0 else ps_n11
                    psb = ps_bc0 if hc % 2 == 0 else ps_bc1
                    nsl = (hc % 2) * 128
                    vec.wait_ge(s_t.h, T0 + 4 * hc + 1)
                    vec.tensor_copy(n1r_sb[0:1, nsl:nsl + 128], psn[0:1, :]).then_inc(s_v.h, 1)
                    vec.wait_ge(s_t.h, T0 + 4 * hc + 2)
                    vec.tensor_copy(n1bt[:, nsl:nsl + 128], psb[:, :]).then_inc(s_v.h, 1)
                    sp = s_p0 if hc % 2 == 0 else s_p1
                    vec.wait_ge(sp.h, (P00 if hc % 2 == 0 else P10) + 16 * (hc // 2 + 1))
                    sl = (hc % 2) * 2048
                    for hh in range(2):
                        vec.tensor_tensor(ub[:, :], stg[:, sl + hh * 1024: sl + hh * 1024 + 1024],
                                          bass.AP(n1bt, nsl, [[256, 128], [0, NE // 2], [1, 128]]),
                                          op=ALU.mult)
                        vec.tensor_scalar(bn[:, :], ub[:, :], 1.0, -1.0, op0=ALU.is_gt, op1=ALU.mult)
                        vec.scalar_tensor_tensor(w1q_t[:, sl + hh * 1024: sl + hh * 1024 + 1024],
                                                 ub[:, :], -1.0, bn[:, :],
                                                 op0=ALU.is_lt, op1=ALU.add)
                    vec.engine_nop().then_inc(s_v.h, 1)
                    tsl = (hc % 2) * 1024
                    for th in range(2):
                        vec.wait_ge(s_t.h, T0 + 4 * hc + 3 + th)
                        if hc >= 2:
                            vec.wait_ge(s_s.h, S0 + 2 * (hc - 2) + th + 1)
                        vec.tensor_mul(t0t[:, tsl + th * 512: tsl + th * 512 + 512],
                                       ps_m[:, tsl + th * 512: tsl + th * 512 + 512],
                                       R1b[:, th * 512:(th + 1) * 512])
                        vec.engine_nop().then_inc(s_v.h, 1)
                    for th in range(2):
                        vec.wait_ge(s_s.h, S0 + 2 * hc + th + 1)
                        vec.tensor_max(mg[:, th * 512:(th + 1) * 512],
                                       mg[:, th * 512:(th + 1) * 512],
                                       g_sb[:, hc * TC + th * 512: hc * TC + th * 512 + 512])
                        vec.tensor_tensor(mg_mn[:, th * 512:(th + 1) * 512],
                                          mg_mn[:, th * 512:(th + 1) * 512],
                                          g_sb[:, hc * TC + th * 512: hc * TC + th * 512 + 512],
                                          op=ALU.min).then_inc(s_v.h, 1)
                s_v.v += 7 * NH

            @blk.scalar
            def _(sc):
                for hc in range(NH):
                    tsl = (hc % 2) * 1024
                    for th in range(2):
                        sc.wait_ge(s_v.h, V0 + 7 * hc + 4 + th)
                        sc.activation(g_sb[:, hc * TC + th * 512: hc * TC + th * 512 + 512],
                                      t0t[:, tsl + th * 512: tsl + th * 512 + 512],
                                      AF.Gelu_apprx_tanh,
                                      scale=w1s_pp[:, hc:hc + 1]).then_inc(s_s.h, 1)
                s_s.v += 2 * NH

    R2b = e_stack.enter_context(nc.sbuf_tensor("R2b", [128, TC], F32))
    with contextlib.ExitStack() as ph:
        mca = ph.enter_context(nc.sbuf_tensor("e0ma", [128, 8], F32))
        v_t = ph.enter_context(nc.sbuf_tensor("e0v", [128, TC], F32))
        ci_t = ph.enter_context(nc.sbuf_tensor("e0c", [128, TC], I32))
        r_t = ph.enter_context(nc.sbuf_tensor("e0r", [128, TC], F32))
        an_t = ph.enter_context(nc.sbuf_tensor("e0a", [128, TC], F32))
        S2b = ph.enter_context(nc.sbuf_tensor("S2b", [128, TC], F32))
        rA = ph.enter_context(nc.sbuf_tensor("eq_ra", [1, TC], F32))
        rB = ph.enter_context(nc.sbuf_tensor("eq_rb", [1, TC], F32))
        ps_tr = ph.enter_context(nc.psum_tensor("e0pt", [128, 128], F32))
        ps_tc0 = ph.enter_context(nc.psum_tensor("e0pc0", [1, 128], F32))
        ps_tc1 = ph.enter_context(nc.psum_tensor("e0pc1", [1, 128], F32))
        ps_bc = ph.enter_context(nc.psum_tensor("e0pb", [128, TC], F32))
        t0, v0, s0 = s_t.v, s_v.v, s_s.v
        with nc.Block() as blk:
            @blk.tensor
            def _(tn):
                for jt in range(8):
                    tn.wait_ge(s_v.h, v0 + 1 + max(0, jt))
                    tn.transpose(ps_tr[:, :], mg[:, jt * 128:(jt + 1) * 128],
                                 idf[:, :]).then_inc(s_t.h, 1)
                s_t.v += 8
                for i in range(8):
                    tn.wait_ge(s_v.h, v0 + 1 + (i + 1 if i < 2 else 8 + i))
                    tn.transpose((ps_tc0 if i % 2 == 0 else ps_tc1)[0:1, :],
                                 mca[:, i:i + 1], idf[:, :]).then_inc(s_t.h, 1)
                s_t.v += 8
                tn.wait_ge(s_v.h, v0 + 18)
                for tb in range(2):
                    ins = tn.matmul(ps_bc[:, tb * 512:(tb + 1) * 512], onesL_f[0:1, :],
                                    rA[0:1, tb * 512:(tb + 1) * 512], start=True, stop=True)
                ins.then_inc(s_t.h, 1)
                s_t.v += 1
                tn.wait_ge(s_v.h, v0 + 19)
                for tb in range(2):
                    ins = tn.matmul(ps_bc[:, tb * 512:(tb + 1) * 512], onesL_f[0:1, :],
                                    rB[0:1, tb * 512:(tb + 1) * 512], start=True, stop=True)
                ins.then_inc(s_t.h, 1)
                s_t.v += 1

            @blk.vector
            def _(vec):
                vec.tensor_scalar(mg_mn[:, :], mg_mn[:, :], -1.0, None, op0=ALU.mult)
                vec.tensor_max(mg[:, :], mg[:, :], mg_mn[:, :]).then_inc(s_v.h, 1)
                s_v.v += 1
                for jt in range(8):
                    vec.wait_ge(s_t.h, t0 + jt + 1)
                    vec.reduce_max(mca[:, jt:jt + 1], ps_tr[:, :], axis=AX.X).then_inc(s_v.h, 1)
                for i in range(8):
                    vec.wait_ge(s_t.h, t0 + 8 + i + 1)
                    vec.tensor_copy(rA[0:1, i * 128:(i + 1) * 128],
                                    (ps_tc0 if i % 2 == 0 else ps_tc1)[0:1, :]).then_inc(s_v.h, 1)
                s_v.v += 16
                vec.tensor_scalar(rB[0:1, :], rA[0:1, :], 1e-5, 1.0 / 127.0,
                                  op0=ALU.max, op1=ALU.mult)
                vec.reciprocal(rA[0:1, :], rB[0:1, :])
                vec.engine_nop().then_inc(s_v.h, 1)
                s_v.v += 1
                vec.wait_ge(s_t.h, t0 + 17)
                vec.tensor_copy(S2b[:, :], ps_bc[:, :]).then_inc(s_v.h, 1)
                s_v.v += 1
                vec.wait_ge(s_t.h, t0 + 18)
                vec.tensor_copy(R2b[:, :], ps_bc[:, :]).then_inc(s_v.h, 1)
                s_v.v += 1
                for hc in range(NH):
                    vec.tensor_mul(v_t[:, :], g_sb[:, hc * TC:(hc + 1) * TC], S2b[:, :])
                    vec.engine_nop().then_inc(s_v.h, 1)
                    vec.wait_ge(s_s.h, s0 + hc + 1)
                    vec.tensor_sub(r_t[:, :], v_t[:, :], ci_t[:, :])
                    vec.tensor_scalar(an_t[:, :], r_t[:, :], 0.5, -1.0, op0=ALU.is_gt, op1=ALU.mult)
                    vec.scalar_tensor_tensor(r_t[:, :], r_t[:, :], -0.5, an_t[:, :],
                                             op0=ALU.is_lt, op1=ALU.add)
                    vec.tensor_sub(g_sb[:, hc * TC:(hc + 1) * TC], ci_t[:, :], r_t[:, :])
                    vec.engine_nop().then_inc(s_v.h, 1)
                s_v.v += 2 * NH

            @blk.scalar
            def _(sc):
                for hc in range(NH):
                    sc.wait_ge(s_v.h, v0 + 20 + 2 * hc + 1)
                    sc.activation(ci_t[:, :], v_t[:, :], AF.Identity).then_inc(s_s.h, 1)
                s_s.v += NH

    if _PHASES <= 8:
        e_stack.close()
        return _early_out("p8")
    for ts_i in range(4):
        with contextlib.ExitStack() as ph:
            w2q_t = ph.enter_context(nc.sbuf_tensor(f"e{ts_i}w", [128, 2 * 2048], FP8))
            yn_t = ph.enter_context(nc.sbuf_tensor(f"e{ts_i}y", [128, NE * 256], F32))
            y2 = ph.enter_context(nc.sbuf_tensor(f"e{ts_i}z", [128, NE * 256], F32))
            sqt = ph.enter_context(nc.sbuf_tensor(f"e{ts_i}q", [128, 2 * 256], F32))
            rA = ph.enter_context(nc.sbuf_tensor(f"e{ts_i}ra", [1, 256], F32))
            rB = ph.enter_context(nc.sbuf_tensor(f"e{ts_i}rb", [1, 256], F32))
            A2 = ph.enter_context(nc.sbuf_tensor(f"e{ts_i}A", [128, 256], F32))
            B2 = ph.enter_context(nc.sbuf_tensor(f"e{ts_i}B", [128, 256], F32))
            g0, i0 = s_g.v, s_in.v
            t0, v0, s0 = s_t.v, s_v.v, s_s.v
            p00, p10 = s_p0.v, s_p1.v
            with nc.Block() as blk:
                ps2 = ph.enter_context(nc.psum_tensor(f"e{ts_i}p", [128, 4096], F32))

                @blk.sync
                def _(sync):
                    for hc in range(NH):
                        if hc >= 2:
                            sync.wait_ge(s_t.h, t0 + hc - 1)
                        sp = s_p0 if hc % 2 == 0 else s_p1
                        sync.dma_start(out=w2q_t[:, (hc % 2) * 2048:(hc % 2) * 2048 + 2048],
                                       in_=bass.AP(w2q_hbm, hc * 128 * E, [[E, 128], [1, E]])
                                       ).then_inc(sp.h, 16)
                        sp.v += 16

                @blk.gpsimd
                def _(gp):
                    gp.dma_start(out=yn_t[:, :],
                                 in_=bass.AP(ynorm_hbm, ts_i * 256,
                                             [[TC, 128], [128 * TC, NE], [1, 256]])
                                 ).then_inc(s_g.h, 16)
                    s_g.v += 16

                @blk.tensor
                def _(tn):
                    for hc in range(NH):
                        tn.wait_ge((s_p0 if hc % 2 == 0 else s_p1).h,
                                   (p00 if hc % 2 == 0 else p10) + 16 * (hc // 2 + 1))
                        sl = (hc % 2) * 2048
                        for fc in range(NE):
                            ins = tn.matmul(ps2[:, fc * 256:(fc + 1) * 256],
                                            w2q_t[:, sl + fc * 128: sl + fc * 128 + 128],
                                            g_sb[:, hc * TC + ts_i * 256: hc * TC + ts_i * 256 + 256],
                                            start=(hc == 0 and fc % 2 == 0),
                                            stop=(hc == NH - 1 and fc % 2 == 1))
                        ins.then_inc(s_t.h, 1)
                    s_t.v += NH
                    for fc in range(NE):
                        tn.wait_ge(s_v.h, v0 + NE + fc + 1)
                        tn.matmul(ps2[0:1, 0:256], ones_f[:, 0:1], y2[:, fc * 256:(fc + 1) * 256],
                                  start=(fc == 0), stop=(fc == NE - 1))
                        tn.matmul(ps2[0:1, 512:768], ones_f[:, 0:1],
                                  sqt[:, (fc % 2) * 256:(fc % 2) * 256 + 256],
                                  start=(fc == 0), stop=(fc == NE - 1)).then_inc(s_t.h, 1)
                    s_t.v += NE
                    tn.wait_ge(s_v.h, v0 + 2 * NE + 2)
                    tn.matmul(ps2[:, 1024:1280], onesL_f[0:1, :], rB[0:1, :],
                              start=True, stop=True).then_inc(s_t.h, 1)
                    tn.wait_ge(s_v.h, v0 + 2 * NE + 3)
                    tn.matmul(ps2[:, 1536:1792], onesL_f[0:1, :], rA[0:1, :],
                              start=True, stop=True).then_inc(s_t.h, 1)
                    s_t.v += 2

                @blk.vector
                def _(vec):
                    vec.wait_ge(s_t.h, t0 + NH)
                    vec.wait_ge(s_g.h, g0 + 16)
                    for fc in range(NE):
                        vec.scalar_tensor_tensor(y2[:, fc * 256:(fc + 1) * 256],
                                                 ps2[:, fc * 256:(fc + 1) * 256],
                                                 w2s_pp[:, fc:fc + 1],
                                                 R2b[:, ts_i * 256:(ts_i + 1) * 256],
                                                 op0=ALU.mult, op1=ALU.mult)
                        vec.tensor_add(y2[:, fc * 256:(fc + 1) * 256],
                                       y2[:, fc * 256:(fc + 1) * 256],
                                       yn_t[:, fc * 256:(fc + 1) * 256]).then_inc(s_v.h, 1)
                    for fc in range(NE):
                        if fc >= 2:
                            vec.wait_ge(s_t.h, t0 + NH + fc - 1)
                        vec.tensor_mul(sqt[:, (fc % 2) * 256:(fc % 2) * 256 + 256],
                                       y2[:, fc * 256:(fc + 1) * 256],
                                       y2[:, fc * 256:(fc + 1) * 256]).then_inc(s_v.h, 1)
                    s_v.v += 2 * NE
                    vec.wait_ge(s_t.h, t0 + NH + NE)
                    vec.tensor_scalar(rA[0:1, :], ps2[0:1, 0:256], 1.0 / E, None, op0=ALU.mult)
                    vec.tensor_scalar(rB[0:1, :], ps2[0:1, 512:768], 1.0 / E, None, op0=ALU.mult)
                    vec.tensor_mul(A2[0:1, 0:256], rA[0:1, :], rA[0:1, :])
                    vec.tensor_sub(rB[0:1, :], rB[0:1, :], A2[0:1, 0:256])
                    vec.tensor_scalar(rB[0:1, :], rB[0:1, :], 1e-5, None, op0=ALU.add
                                      ).then_inc(s_v.h, 1)
                    s_v.v += 1
                    vec.wait_ge(s_s.h, s0 + 1)
                    vec.reciprocal(A2[0:1, 0:256], B2[0:1, 0:256])
                    vec.tensor_mul(B2[0:1, 0:256], A2[0:1, 0:256], A2[0:1, 0:256])
                    vec.tensor_mul(B2[0:1, 0:256], B2[0:1, 0:256], rB[0:1, :])
                    vec.tensor_scalar(B2[0:1, 0:256], B2[0:1, 0:256], -0.5, 1.5,
                                      op0=ALU.mult, op1=ALU.add)
                    vec.tensor_mul(rB[0:1, :], A2[0:1, 0:256], B2[0:1, 0:256])
                    vec.tensor_mul(rA[0:1, :], rA[0:1, :], rB[0:1, :])
                    vec.tensor_scalar(rA[0:1, :], rA[0:1, :], -1.0, None, op0=ALU.mult
                                      ).then_inc(s_v.h, 1)
                    s_v.v += 1
                    vec.wait_ge(s_t.h, t0 + NH + NE + 1)
                    vec.tensor_copy(A2[:, :], ps2[:, 1024:1280]).then_inc(s_v.h, 1)
                    s_v.v += 1
                    vec.wait_ge(s_t.h, t0 + NH + NE + 2)
                    vec.tensor_copy(B2[:, :], ps2[:, 1536:1792]).then_inc(s_v.h, 1)
                    s_v.v += 1
                    for fc in range(NE):
                        vec.tensor_mul(y2[:, fc * 256:(fc + 1) * 256],
                                       y2[:, fc * 256:(fc + 1) * 256], A2[:, :])
                        vec.tensor_add(y2[:, fc * 256:(fc + 1) * 256],
                                       y2[:, fc * 256:(fc + 1) * 256], B2[:, :]
                                       ).then_inc(s_v.h, 1)
                    s_v.v += NE

                @blk.scalar
                def _(sc):
                    sc.wait_ge(s_v.h, v0 + 2 * NE + 1)
                    sc.activation(B2[0:1, 0:256], rB[0:1, :], AF.Sqrt).then_inc(s_s.h, 1)
                    s_s.v += 1

                @blk.sync
                def _(sync2):
                    for fc in range(NE):
                        sync2.wait_ge(s_v.h, v0 + 2 * NE + 4 + fc + 1)
                        sync2.dma_start(out=bass.AP(out_d, fc * 128 * TC + ts_i * 256,
                                                    [[TC, 128], [1, 256]]),
                                        in_=y2[:, fc * 256:(fc + 1) * 256]).then_inc(s_o.h, 16)
                        s_o.v += 16
                    sync2.wait_ge(s_o.h, s_o.v)
    e_stack.close()

    top.close()
    return nc


def _host_prep(x, wq, wk, wv, wo, w1, w2):
    ins = {}
    ins["wqT"] = np.ascontiguousarray(wq.T)
    ins["wkT"] = np.ascontiguousarray(wk.T)
    ins["wvT"] = np.ascontiguousarray(wv.T)
    ins["woT"] = np.ascontiguousarray(wo.T)
    ins["w1T"] = np.ascontiguousarray(w1.T)
    ins["w2T"] = np.ascontiguousarray(w2.T)

    import jax
    import jax.numpy as jnp
    cpu = jax.devices("cpu")[0]
    with jax.default_device(cpu):
        inv_j = 1.0 / (1000000.0 ** (jnp.arange(0, D, 2, dtype=jnp.float32) / D))
        fang = jnp.arange(T, dtype=jnp.float32)[:, None] * inv_j[None, :]
        cos32 = np.asarray(jnp.concatenate([jnp.cos(fang), jnp.cos(fang)], axis=-1))
        sin32 = np.asarray(jnp.concatenate([jnp.sin(fang), jnp.sin(fang)], axis=-1))
        sf = np.float64(np.asarray(jnp.sqrt(jnp.float32(D))))
    cos = cos32.astype(np.float64).T
    sin = sin32.astype(np.float64).T
    ss = sin.copy()
    ss[0:64] = -sin[0:64]
    sc = 1.0 / sf

    def hl(a):
        hi = a.astype(np.float32)
        lo = (a - hi.astype(np.float64)).astype(np.float32)
        return np.ascontiguousarray(hi), np.ascontiguousarray(lo)

    import ml_dtypes
    ins["identb"] = np.eye(128, dtype=np.float16)
    ins["identf"] = np.eye(128, dtype=np.float32)
    percore = []
    for c in range(8):
        b, half = c // 2, c % 2
        toks = np.concatenate([np.arange((2 * i + half) * 128, (2 * i + half + 1) * 128)
                               for i in range(8)])
        d = dict(ins)
        d["xT"] = np.ascontiguousarray(x[b].T)
        d["xTm"] = np.ascontiguousarray(x[b].T[:, toks])
        d["coskhi"], d["cosklo"] = hl(cos)
        d["sskhi"], d["ssklo"] = hl(ss)
        d["cosqhi"], d["cosqlo"] = hl(cos[:, toks] * sc)
        d["ssqhi"], d["ssqlo"] = hl(ss[:, toks] * sc)
        mk = np.zeros((128, NPAIR * 4 * 256), dtype=np.float32)
        for j in range(NPAIR):
            nk = 4 * j + 4
            for ci_, kc in enumerate(range(nk - 4, nk)):
                for pp in range(2):
                    pos = 2 * j + pp
                    qt = (2 * pos + half) * 128 + np.arange(128)
                    kt = kc * 128 + np.arange(128)
                    blkm = (qt[None, :] >= kt[:, None]).astype(np.float32)
                    mk[:, (j * 4 + ci_) * 256 + pp * 128:(j * 4 + ci_) * 256 + (pp + 1) * 128] = blkm
        d["maskp"] = mk.astype(np.float16)
        percore.append(d)
    return percore


def kernel(L0_attn_x, L0_attn_wq, L0_attn_wk, L0_attn_wv, L0_attn_wo, L0_ffn_w1, L0_ffn_w2):
    x = np.ascontiguousarray(np.asarray(L0_attn_x, dtype=np.float32))
    args = [np.asarray(a, dtype=np.float32) for a in
            (L0_attn_wq, L0_attn_wk, L0_attn_wv, L0_attn_wo, L0_ffn_w1, L0_ffn_w2)]
    in_maps = _host_prep(x, *args)
    nc = build_program()
    res = run_bass_kernel_spmd(nc, in_maps, core_ids=list(range(8)))
    out = np.empty((B, T, E), np.float32)
    for c in range(8):
        o = res.results[c]["out"]
        half = c % 2
        for i in range(8):
            t0_ = (2 * i + half) * 128
            out[c // 2, t0_:t0_ + 128, :] = o[:, i * 128:(i + 1) * 128].T
    return out


# version 15
# speedup vs baseline: 1.0901x; 1.0125x over previous
import contextlib
import math

import numpy as np

import concourse.bass as bass
import concourse.mybir as mybir
from concourse.bass_utils import run_bass_kernel_spmd

F32 = mybir.dt.float32
BF16 = mybir.dt.bfloat16
FP16 = mybir.dt.float16
FP8 = mybir.dt.float8e4
I32 = mybir.dt.int32
AF = mybir.ActivationFunctionType
ALU = mybir.AluOpType
AX = mybir.AxisListType

B, T, E = 4, 2048, 2048
H, D = 16, 128
HID = 8192
TC = 1024
NE = E // 128
NH = HID // 128
NPAIR = 4

_DEBUG = False
_PHASES = 10


class Sem:
    def __init__(self, stack, nc, name):
        self.h = stack.enter_context(nc.semaphore(name))
        self.v = 0


def build_program():
    nc = bass.Bass("TRN2", target_bir_lowering=False, detect_race_conditions=False)

    xT = nc.dram_tensor("xT", [E, T], F32, kind="ExternalInput")
    xTm = nc.dram_tensor("xTm", [E, TC], F32, kind="ExternalInput")
    wqT = nc.dram_tensor("wqT", [E, E], F32, kind="ExternalInput")
    wkT = nc.dram_tensor("wkT", [E, E], F32, kind="ExternalInput")
    wvT = nc.dram_tensor("wvT", [E, E], F32, kind="ExternalInput")
    woT = nc.dram_tensor("woT", [E, E], F32, kind="ExternalInput")
    w1T = nc.dram_tensor("w1T", [E, HID], F32, kind="ExternalInput")
    w2T = nc.dram_tensor("w2T", [HID, E], F32, kind="ExternalInput")
    tabs = {}
    for nm, cols in [("cosq", TC), ("ssq", TC), ("cosk", T), ("ssk", T)]:
        for p in ("hi", "lo"):
            tabs[nm + p] = nc.dram_tensor(nm + p, [D, cols], F32, kind="ExternalInput")
    maskp = nc.dram_tensor("maskp", [128, NPAIR * 4 * 256], FP16, kind="ExternalInput")
    identb = nc.dram_tensor("identb", [128, 128], FP16, kind="ExternalInput")
    identf = nc.dram_tensor("identf", [128, 128], F32, kind="ExternalInput")
    out_d = nc.dram_tensor("out", [E, TC], F32, kind="ExternalOutput")
    w2q_hbm = nc.dram_tensor("w2q_hbm", [NH * 128, E], FP8)
    ynorm_hbm = nc.dram_tensor("ynorm_hbm", [E, TC], F32)
    xq1_hbm = nc.dram_tensor("xq1_hbm", [E, TC], BF16)
    r1_hbm = nc.dram_tensor("r1_hbm", [128, TC], F32)

    dbg = {}
    if _DEBUG:
        for nm, sh, dt in [("d_k", [D, T], FP16), ("d_q", [D, TC], FP16),
                           ("d_vt", [128, 16 * 128], FP16), ("d_ctx", [128, H * TC], FP16),
                           ("d_y1", [128, NE * TC], F32), ("d_xq1", [128, NE * TC], BF16),
                           ("d_g", [128, 8 * TC], FP16), ("d_s1", [128, TC], F32),
                           ("d_rows", [1, 2 * TC], F32), ("d_mrun", [128, TC], F32)]:
            dbg[nm] = nc.dram_tensor(nm, sh, dt, kind="ExternalOutput")

    top = contextlib.ExitStack()
    s_in = Sem(top, nc, "s_in")
    s_g = Sem(top, nc, "s_g")
    s_t = Sem(top, nc, "s_t")
    s_v = Sem(top, nc, "s_v")
    s_s = Sem(top, nc, "s_s")
    s_o = Sem(top, nc, "s_o")
    s_p0 = Sem(top, nc, "s_p0")
    s_p1 = Sem(top, nc, "s_p1")
    s_q0 = Sem(top, nc, "s_q0")
    s_q1 = Sem(top, nc, "s_q1")

    ones_f = top.enter_context(nc.sbuf_tensor("ones_f", [128, 1], F32))
    ones_b = top.enter_context(nc.sbuf_tensor("ones_b", [128, 1], FP16))
    onesL_f = top.enter_context(nc.sbuf_tensor("onesL_f", [1, 128], F32))
    negb12 = top.enter_context(nc.sbuf_tensor("negb12", [128, 1], F32))
    idb = top.enter_context(nc.sbuf_tensor("idb", [128, 128], FP16))
    idf = top.enter_context(nc.sbuf_tensor("idf", [128, 128], F32))
    w1s_pp = top.enter_context(nc.sbuf_tensor("w1s_pp", [128, NH], F32))
    n1_pp = top.enter_context(nc.sbuf_tensor("n1_pp", [128, NH], F32))
    w2s_pp = top.enter_context(nc.sbuf_tensor("w2s_pp", [128, NE], F32))
    cstack0 = contextlib.ExitStack()
    ctx_bf = cstack0.enter_context(nc.sbuf_tensor("ctx_bf", [128, H * TC], FP16))

    heads_stack = contextlib.ExitStack()
    tabsb = {nm: heads_stack.enter_context(nc.sbuf_tensor("t_" + nm, [D, cols], F32))
             for nm, cols in [("cosq", TC), ("ssq", TC), ("cosk", T), ("ssk", T)]}
    x_bf = heads_stack.enter_context(nc.sbuf_tensor("x_bf", [128, NE * T], FP16))
    xm_bf = heads_stack.enter_context(nc.sbuf_tensor("xm_bf", [128, NE * TC], FP16))
    mask_sb = heads_stack.enter_context(nc.sbuf_tensor("mask_sb", [128, NPAIR * 4 * 256], FP16))


    def _early_out(tag):
        with contextlib.ExitStack() as p0:
            dummy = p0.enter_context(nc.sbuf_tensor("dummy" + tag, [128, 256], F32))
            with nc.Block() as blk:
                @blk.vector
                def _(vec):
                    vec.memset(dummy[:, :], 0.0).then_inc(s_v.h, 1)
                    s_v.v += 1

                @blk.sync
                def _(sync):
                    sync.wait_ge(s_v.h, s_v.v)
                    for fc in range(NE):
                        for tsx in range(4):
                            sync.dma_start(out=bass.AP(out_d, fc * 128 * TC + tsx * 256,
                                                       [[TC, 128], [1, 256]]),
                                           in_=dummy[:, :]).then_inc(s_o.h, 16)
                            s_o.v += 16
                    sync.wait_ge(s_o.h, s_o.v)
        return nc

    with contextlib.ExitStack() as ph:
        tlo = {nm: ph.enter_context(nc.sbuf_tensor("tl_" + nm, [D, cols], F32))
               for nm, cols in [("cosq", TC), ("ssq", TC), ("cosk", T), ("ssk", T)]}
        with nc.Block() as blk:
            @blk.sync
            def _(sync):
                for nm in ("cosq", "ssq", "cosk", "ssk"):
                    sync.dma_start(out=tabsb[nm][:], in_=tabs[nm + "hi"].ap()[:, :]).then_inc(s_in.h, 16)
                    sync.dma_start(out=tlo[nm][:], in_=tabs[nm + "lo"].ap()[:, :]).then_inc(s_in.h, 16)
                    s_in.v += 32
                sync.dma_start(out=mask_sb[:], in_=maskp.ap()[:, :]).then_inc(s_in.h, 16)
                sync.dma_start(out=idb[:], in_=identb.ap()[:, :]).then_inc(s_in.h, 16)
                sync.dma_start(out=idf[:], in_=identf.ap()[:, :]).then_inc(s_in.h, 16)
                s_in.v += 48
                sync.wait_ge(s_in.h, s_in.v)

            @blk.gpsimd
            def _(gp):
                gp.dma_start(out=x_bf[:, :],
                             in_=bass.AP(xT, 0, [[T, 128], [128 * T, NE], [1, T]])).then_inc(s_g.h, 16)
                gp.dma_start(out=xm_bf[:, :],
                             in_=bass.AP(xTm, 0, [[TC, 128], [128 * TC, NE], [1, TC]])).then_inc(s_g.h, 16)
                s_g.v += 32
                gp.wait_ge(s_g.h, s_g.v)

            @blk.vector
            def _(vec):
                vec.memset(ones_f[:], 1.0)
                vec.memset(ones_b[:], 1.0)
                vec.memset(onesL_f[:], 1.0)
                vec.memset(negb12[:], -12.0)
                vec.wait_ge(s_in.h, s_in.v)
                for nm in ("cosq", "ssq", "cosk", "ssk"):
                    vec.tensor_add(tabsb[nm][:], tabsb[nm][:], tlo[nm][:])

    if _PHASES <= 1:
        heads_stack.close()
        return _early_out("p1")
    cosq, ssq, cosk, ssk = (tabsb[n] for n in ("cosq", "ssq", "cosk", "ssk"))

    def rope_evac(vec, ps, dst, dc0, cos_t, ss_t, c0, width):
        vec.tensor_tensor(tmpA[:, 0:width], ps[:, 0:width], cos_t[:, c0:c0 + width], op=ALU.mult)
        vec.tensor_tensor(tmpB[0:64, 0:width], ps[64:128, 0:width], ss_t[0:64, c0:c0 + width], op=ALU.mult)
        vec.tensor_tensor(tmpB[64:128, 0:width], ps[0:64, 0:width], ss_t[64:128, c0:c0 + width], op=ALU.mult)
        return vec.tensor_add(dst[:, dc0:dc0 + width], tmpA[:, 0:width], tmpB[:, 0:width])

    for h in range(H):
        hp = contextlib.ExitStack()
        wq_bf = hp.enter_context(nc.sbuf_tensor(f"wq{h}", [128, NE * 128], FP16))
        wk_bf = hp.enter_context(nc.sbuf_tensor(f"wk{h}", [128, NE * 128], FP16))
        wv_bf = hp.enter_context(nc.sbuf_tensor(f"wv{h}", [128, NE * 128], FP16))
        k_bf = hp.enter_context(nc.sbuf_tensor(f"k{h}", [D, T], FP16))
        q_bf = hp.enter_context(nc.sbuf_tensor(f"q{h}", [D, TC], FP16))
        v_fm = hp.enter_context(nc.sbuf_tensor(f"vf{h}", [D, T], FP16))
        v_tm = hp.enter_context(nc.sbuf_tensor(f"vt{h}", [128, 16 * 128], FP16))
        tmpA = hp.enter_context(nc.sbuf_tensor(f"ta{h}", [128, 512], F32))
        tmpB = hp.enter_context(nc.sbuf_tensor(f"tb{h}", [128, 512], F32))
        pt_bf = hp.enter_context(nc.sbuf_tensor(f"pt{h}", [128, 2 * 256], FP16))
        rcpb = hp.enter_context(nc.sbuf_tensor(f"rc{h}", [128, TC], F32))

        with contextlib.ExitStack() as pp:
            ps_a = pp.enter_context(nc.psum_tensor(f"pa{h}", [128, 512], F32))
            ps_b = pp.enter_context(nc.psum_tensor(f"pb{h}", [128, 512], F32))
            ps_tr = pp.enter_context(nc.psum_tensor(f"ptr{h}", [128, 128], FP16))
            g0, t0, v0 = s_g.v, s_t.v, s_v.v
            with nc.Block() as blk:
                @blk.gpsimd
                def _(gp):
                    for wsb, wdr in ((wq_bf, wqT), (wk_bf, wkT), (wv_bf, wvT)):
                        gp.dma_start(out=wsb[:],
                                     in_=bass.AP(wdr, h * 128, [[E, 128], [128 * E, NE], [1, 128]])
                                     ).then_inc(s_g.h, 16)
                    s_g.v += 48

                @blk.tensor
                def _(tn):
                    tn.wait_ge(s_g.h, g0 + 48)
                    for i in range(10):
                        ps = ps_a if i % 2 == 0 else ps_b
                        if i >= 2:
                            tn.wait_ge(s_v.h, v0 + i - 1)
                        if i < 4:
                            w, rhs_t, cbase, wd = wk_bf, x_bf, i * 512, T
                        elif i < 8:
                            w, rhs_t, cbase, wd = wv_bf, x_bf, (i - 4) * 512, T
                        else:
                            w, rhs_t, cbase, wd = wq_bf, xm_bf, (i - 8) * 512, TC
                        for ec in range(NE):
                            ins = tn.matmul(ps[:, :], w[:, ec * 128:(ec + 1) * 128],
                                            rhs_t[:, ec * wd + cbase: ec * wd + cbase + 512],
                                            start=(ec == 0), stop=(ec == NE - 1))
                        ins.then_inc(s_t.h, 1)
                    s_t.v += 10
                    tn.wait_ge(s_v.h, v0 + 8)
                    for kc in range(16):
                        if kc >= 1:
                            tn.wait_ge(s_v.h, v0 + 10 + kc)
                        tn.transpose(ps_tr[:, :], v_fm[:, kc * 128:(kc + 1) * 128],
                                     idb[:, :]).then_inc(s_t.h, 1)
                    s_t.v += 16

                @blk.vector
                def _(vec):
                    for i in range(10):
                        ps = ps_a if i % 2 == 0 else ps_b
                        vec.wait_ge(s_t.h, t0 + i + 1)
                        if i < 4:
                            ins = rope_evac(vec, ps, k_bf, i * 512, cosk, ssk, i * 512, 512)
                        elif i < 8:
                            ins = vec.tensor_copy(v_fm[:, (i - 4) * 512:(i - 3) * 512], ps[:, :])
                        else:
                            ins = rope_evac(vec, ps, q_bf, (i - 8) * 512, cosq, ssq, (i - 8) * 512, 512)
                        ins.then_inc(s_v.h, 1)
                    for kc in range(16):
                        vec.wait_ge(s_t.h, t0 + 10 + kc + 1)
                        vec.tensor_copy(v_tm[:, kc * 128:(kc + 1) * 128], ps_tr[:, :]).then_inc(s_v.h, 1)
                    s_v.v += 26

        with contextlib.ExitStack() as pp:
            ps_s0 = pp.enter_context(nc.psum_tensor(f"pss0{h}", [128, 256], F32))
            ps_s1 = pp.enter_context(nc.psum_tensor(f"pss1{h}", [128, 256], F32))
            ps_ctx = pp.enter_context(nc.psum_tensor(f"pctx{h}", [128, TC], F32))
            ps_row = pp.enter_context(nc.psum_tensor(f"prow{h}", [1, TC], F32))
            ps_bc = pp.enter_context(nc.psum_tensor(f"pbc{h}", [128, TC], F32))
            t0, v0, s0 = s_t.v, s_v.v, s_s.v
            sched = []
            for j in range(NPAIR):
                nk = 4 * j + 4
                for kc in range(nk):
                    sched.append((j, kc, nk))
            masked_idx = {}
            for n, (j, kc, nk) in enumerate(sched):
                if kc >= nk - 4:
                    masked_idx[n] = len(masked_idx) + 1
            NM = len(masked_idx)

            with nc.Block() as blk:
                @blk.tensor
                def _(tn):
                    for n, (j, kc, nk) in enumerate(sched):
                        sl = (n % 2) * 256
                        pss = ps_s0 if n % 2 == 0 else ps_s1
                        if n >= 2:
                            tn.wait_ge(s_s.h, s0 + n - 1)
                        tn.matmul(pss[:, :], k_bf[:, kc * 128:(kc + 1) * 128],
                                  q_bf[:, j * 256:(j + 1) * 256], start=True, stop=True
                                  ).then_inc(s_t.h, 1)
                        if n in masked_idx:
                            tn.wait_ge(s_v.h, v0 + masked_idx[n])
                        else:
                            tn.wait_ge(s_s.h, s0 + n + 1)
                        tn.matmul(ps_ctx[:, j * 256:(j + 1) * 256],
                                  v_tm[:, kc * 128:(kc + 1) * 128],
                                  pt_bf[:, sl:sl + 256], start=(kc == 0), stop=(kc == nk - 1))
                        tn.matmul(ps_row[0:1, j * 256:(j + 1) * 256], ones_b[:, 0:1],
                                  pt_bf[:, sl:sl + 256], start=(kc == 0), stop=(kc == nk - 1)
                                  ).then_inc(s_t.h, 1)
                    s_t.v += 80
                    tn.wait_ge(s_v.h, v0 + NM + 1)
                    for tb in range(2):
                        ins = tn.matmul(ps_bc[:, tb * 512:(tb + 1) * 512], onesL_f[0:1, :],
                                        rcpb[0:1, tb * 512:(tb + 1) * 512], start=True, stop=True)
                    ins.then_inc(s_t.h, 1)
                    s_t.v += 1

                @blk.scalar
                def _(sc):
                    for n, (j, kc, nk) in enumerate(sched):
                        sl = (n % 2) * 256
                        pss = ps_s0 if n % 2 == 0 else ps_s1
                        sc.wait_ge(s_t.h, t0 + 2 * n + 1)
                        sc.activation(pt_bf[:, sl:sl + 256], pss[:, :], AF.Exp,
                                      bias=negb12[:, 0:1]).then_inc(s_s.h, 1)
                    s_s.v += 80 if False else len(sched)

                @blk.vector
                def _(vec):
                    for n, (j, kc, nk) in enumerate(sched):
                        if n not in masked_idx:
                            continue
                        sl = (n % 2) * 256
                        mo = (j * 4 + (kc - (nk - 4))) * 256
                        vec.wait_ge(s_s.h, s0 + n + 1)
                        vec.tensor_mul(pt_bf[:, sl:sl + 256], pt_bf[:, sl:sl + 256],
                                       mask_sb[:, mo:mo + 256]).then_inc(s_v.h, 1)
                    vec.wait_ge(s_t.h, t0 + 80)
                    vec.reciprocal(rcpb[0:1, :], ps_row[0:1, :]).then_inc(s_v.h, 1)
                    s_v.v += NM + 1
                    vec.wait_ge(s_t.h, t0 + 81)
                    vec.tensor_copy(rcpb[:, 0:512], ps_bc[:, 0:512])
                    vec.tensor_copy(rcpb[:, 512:1024], ps_bc[:, 512:1024])
                    for j in range(NPAIR):
                        vec.tensor_tensor(ctx_bf[:, h * TC + j * 256: h * TC + (j + 1) * 256],
                                          ps_ctx[:, j * 256:(j + 1) * 256],
                                          rcpb[:, j * 256:(j + 1) * 256], op=ALU.mult)
                    vec.engine_nop().then_inc(s_v.h, 1)
                    s_v.v += 1
        if _DEBUG and h == 0:
            with nc.Block() as blk:
                @blk.sync
                def _(sync):
                    sync.dma_start(out=dbg["d_k"].ap()[:, :], in_=k_bf[:]).then_inc(s_o.h, 16)
                    sync.dma_start(out=dbg["d_q"].ap()[:, :], in_=q_bf[:]).then_inc(s_o.h, 16)
                    sync.dma_start(out=dbg["d_vt"].ap()[:, :], in_=v_tm[:]).then_inc(s_o.h, 16)
                    s_o.v += 48
                    sync.wait_ge(s_o.h, s_o.v)
        hp.close()

    if _DEBUG:
        with nc.Block() as blk:
            @blk.sync
            def _(sync):
                sync.dma_start(out=dbg["d_ctx"].ap()[:, :], in_=ctx_bf[:]).then_inc(s_o.h, 16)
                s_o.v += 16
                sync.wait_ge(s_o.h, s_o.v)

    heads_stack.close()

    if _PHASES <= 2:
        return _early_out("p2")
    y1 = cstack0.enter_context(nc.sbuf_tensor("y1", [128, NE * TC], F32))
    mrun = cstack0.enter_context(nc.sbuf_tensor("mrun", [128, TC], F32))
    mrun_mn = cstack0.enter_context(nc.sbuf_tensor("mrun_mn", [128, TC], F32))
    rowsA = cstack0.enter_context(nc.sbuf_tensor("rowsA", [1, TC], F32))
    rowsB = cstack0.enter_context(nc.sbuf_tensor("rowsB", [1, TC], F32))
    Ab = cstack0.enter_context(nc.sbuf_tensor("Ab", [128, TC], F32))
    Bb = cstack0.enter_context(nc.sbuf_tensor("Bb", [128, TC], F32))

    with contextlib.ExitStack() as ph:
        wo_bf = ph.enter_context(nc.sbuf_tensor("wo_bf", [128, NE * E], FP16))
        xres = ph.enter_context(nc.sbuf_tensor("xres", [128, 2 * TC], F32))
        ps_a = ph.enter_context(nc.psum_tensor("pc_a", [128, 512], F32))
        ps_b = ph.enter_context(nc.psum_tensor("pc_b", [128, 512], F32))
        g0, t0, v0, i0 = s_g.v, s_t.v, s_v.v, s_in.v
        p00, p10 = s_p0.v, s_p1.v
        with nc.Block() as blk:
            @blk.gpsimd
            def _(gp):
                for ec in range(NE):
                    gp.dma_start(out=wo_bf[:, ec * E:(ec + 1) * E],
                                 in_=bass.AP(woT, ec * 128 * E, [[E, 128], [1, E]])
                                 ).then_inc(s_g.h, 16)
                s_g.v += 16 * NE

            @blk.sync
            def _(sync):
                for fc in range(NE):
                    sp = s_p0 if fc % 2 == 0 else s_p1
                    sync.dma_start(out=xres[:, (fc % 2) * TC:(fc % 2) * TC + TC],
                                   in_=bass.AP(xTm, fc * 128 * TC, [[TC, 128], [1, TC]])
                                   ).then_inc(sp.h, 16)
                    sp.v += 16
                    if fc < NE - 1:
                        sync.wait_ge(s_v.h, v0 + 2 * fc)

            @blk.tensor
            def _(tn):
                tn.wait_ge(s_g.h, g0 + 16 * NE)
                for fc in range(NE):
                    for th in range(2):
                        i = 2 * fc + th
                        ps = ps_a if i % 2 == 0 else ps_b
                        if i >= 2:
                            tn.wait_ge(s_v.h, v0 + i - 1)
                        for ec in range(NE):
                            ins = tn.matmul(ps[:, :], wo_bf[:, ec * E + fc * 128: ec * E + fc * 128 + 128],
                                            ctx_bf[:, ec * TC + th * 512: ec * TC + th * 512 + 512],
                                            start=(ec == 0), stop=(ec == NE - 1))
                        ins.then_inc(s_t.h, 1)
                s_t.v += 32

            @blk.vector
            def _(vec):
                for fc in range(NE):
                    for th in range(2):
                        i = 2 * fc + th
                        ps = ps_a if i % 2 == 0 else ps_b
                        vec.wait_ge(s_t.h, t0 + i + 1)
                        sp = s_p0 if fc % 2 == 0 else s_p1
                        vec.wait_ge(sp.h, (p00 if fc % 2 == 0 else p10) + 16 * (fc // 2 + 1))
                        vec.tensor_add(y1[:, fc * TC + th * 512: fc * TC + th * 512 + 512],
                                       ps[:, :], xres[:, (fc % 2) * TC + th * 512:(fc % 2) * TC + th * 512 + 512]
                                       ).then_inc(s_v.h, 1)
                s_v.v += 32

    if _PHASES <= 3:
        cstack0.close()
        return _early_out("p3")
    with contextlib.ExitStack() as ph:
        sqt = ph.enter_context(nc.sbuf_tensor("sqt", [128, 2 * TC], F32))
        ps_sum = ph.enter_context(nc.psum_tensor("pc_sum", [1, TC], F32))
        ps_sq = ph.enter_context(nc.psum_tensor("pc_sq", [1, TC], F32))
        ps_bc = ph.enter_context(nc.psum_tensor("pc_bc", [128, TC], F32))
        t0, v0, s0 = s_t.v, s_v.v, s_s.v
        with nc.Block() as blk:
            @blk.vector
            def _(vec):
                for fc in range(NE):
                    if fc >= 2:
                        vec.wait_ge(s_t.h, t0 + fc - 1)
                    vec.tensor_mul(sqt[:, (fc % 2) * TC:(fc % 2) * TC + TC],
                                   y1[:, fc * TC:(fc + 1) * TC],
                                   y1[:, fc * TC:(fc + 1) * TC]).then_inc(s_v.h, 1)
                s_v.v += NE
                vec.wait_ge(s_t.h, t0 + NE)
                vec.tensor_scalar(rowsA[0:1, :], ps_sum[0:1, :], 1.0 / E, None, op0=ALU.mult)
                vec.tensor_scalar(rowsB[0:1, :], ps_sq[0:1, :], 1.0 / E, None, op0=ALU.mult)
                vec.tensor_mul(Ab[0:1, 0:TC], rowsA[0:1, :], rowsA[0:1, :])
                vec.tensor_sub(rowsB[0:1, :], rowsB[0:1, :], Ab[0:1, 0:TC])
                vec.tensor_scalar(rowsB[0:1, :], rowsB[0:1, :], 1e-5, None, op0=ALU.add
                                  ).then_inc(s_v.h, 1)
                s_v.v += 1
                vec.wait_ge(s_s.h, s0 + 1)
                vec.reciprocal(Ab[0:1, 0:TC], Bb[0:1, 0:TC])
                vec.tensor_mul(Bb[0:1, 0:TC], Ab[0:1, 0:TC], Ab[0:1, 0:TC])
                vec.tensor_mul(Bb[0:1, 0:TC], Bb[0:1, 0:TC], rowsB[0:1, :])
                vec.tensor_scalar(Bb[0:1, 0:TC], Bb[0:1, 0:TC], -0.5, 1.5, op0=ALU.mult, op1=ALU.add)
                vec.tensor_mul(rowsB[0:1, :], Ab[0:1, 0:TC], Bb[0:1, 0:TC])
                vec.tensor_mul(rowsA[0:1, :], rowsA[0:1, :], rowsB[0:1, :])
                vec.tensor_scalar(rowsA[0:1, :], rowsA[0:1, :], -1.0, None, op0=ALU.mult
                                  ).then_inc(s_v.h, 1)
                s_v.v += 1
                vec.wait_ge(s_t.h, t0 + NE + 1)
                vec.tensor_copy(Ab[:, :], ps_bc[:, :]).then_inc(s_v.h, 1)
                s_v.v += 1
                vec.wait_ge(s_t.h, t0 + NE + 2)
                vec.tensor_copy(Bb[:, :], ps_bc[:, :]).then_inc(s_v.h, 1)
                s_v.v += 1
                vec.memset(mrun[:, :], 0.0)
                vec.memset(mrun_mn[:, :], 0.0)
                for fc in range(NE):
                    vec.tensor_mul(y1[:, fc * TC:(fc + 1) * TC], y1[:, fc * TC:(fc + 1) * TC], Ab[:, :])
                    vec.tensor_add(y1[:, fc * TC:(fc + 1) * TC], y1[:, fc * TC:(fc + 1) * TC], Bb[:, :])
                    vec.tensor_max(mrun[:, :], mrun[:, :], y1[:, fc * TC:(fc + 1) * TC])
                    vec.tensor_tensor(mrun_mn[:, :], mrun_mn[:, :], y1[:, fc * TC:(fc + 1) * TC],
                                      op=ALU.min)
                    vec.engine_nop().then_inc(s_v.h, 1)
                vec.tensor_scalar(mrun_mn[:, :], mrun_mn[:, :], -1.0, None, op0=ALU.mult)
                vec.tensor_max(mrun[:, :], mrun[:, :], mrun_mn[:, :])
                s_v.v += NE

            @blk.scalar
            def _(sc):
                sc.wait_ge(s_v.h, v0 + NE + 1)
                sc.activation(Bb[0:1, 0:TC], rowsB[0:1, :], AF.Sqrt).then_inc(s_s.h, 1)
                s_s.v += 1

            @blk.tensor
            def _(tn):
                for fc in range(NE):
                    tn.wait_ge(s_v.h, v0 + fc + 1)
                    for th in range(2):
                        tn.matmul(ps_sum[0:1, th * 512:(th + 1) * 512], ones_f[:, 0:1],
                                  y1[:, fc * TC + th * 512: fc * TC + th * 512 + 512],
                                  start=(fc == 0), stop=(fc == NE - 1))
                        ins = tn.matmul(ps_sq[0:1, th * 512:(th + 1) * 512], ones_f[:, 0:1],
                                        sqt[:, (fc % 2) * TC + th * 512:(fc % 2) * TC + th * 512 + 512],
                                        start=(fc == 0), stop=(fc == NE - 1))
                    ins.then_inc(s_t.h, 1)
                s_t.v += NE
                tn.wait_ge(s_v.h, v0 + NE + 2)
                for tb in range(2):
                    ins = tn.matmul(ps_bc[:, tb * 512:(tb + 1) * 512], onesL_f[0:1, :],
                                    rowsB[0:1, tb * 512:(tb + 1) * 512], start=True, stop=True)
                ins.then_inc(s_t.h, 1)
                s_t.v += 1
                tn.wait_ge(s_v.h, v0 + NE + 3)
                for tb in range(2):
                    ins = tn.matmul(ps_bc[:, tb * 512:(tb + 1) * 512], onesL_f[0:1, :],
                                    rowsA[0:1, tb * 512:(tb + 1) * 512], start=True, stop=True)
                ins.then_inc(s_t.h, 1)
                s_t.v += 1

            @blk.sync
            def _(sync):
                for fc in range(NE):
                    sync.wait_ge(s_v.h, v0 + NE + 4 + fc + 1)
                    sync.dma_start(out=bass.AP(ynorm_hbm, fc * 128 * TC, [[TC, 128], [1, TC]]),
                                   in_=y1[:, fc * TC:(fc + 1) * TC]).then_inc(s_o.h, 16)
                    s_o.v += 16
                sync.wait_ge(s_o.h, s_o.v)

    if _PHASES <= 4:
        cstack0.close()
        return _early_out("p4")
    with contextlib.ExitStack() as ph:
        xq1 = ph.enter_context(nc.sbuf_tensor("xq1c", [128, NE * TC], BF16))
        R1b = ph.enter_context(nc.sbuf_tensor("R1bc", [128, TC], F32))
        mca = ph.enter_context(nc.sbuf_tensor("mca", [128, 8], F32))
        v_t = ph.enter_context(nc.sbuf_tensor("qv", [128, TC], F32))
        ci_t = ph.enter_context(nc.sbuf_tensor("qc", [128, TC], I32))
        r_t = ph.enter_context(nc.sbuf_tensor("qr", [128, TC], F32))
        an_t = ph.enter_context(nc.sbuf_tensor("qa", [128, TC], F32))
        S1b = ph.enter_context(nc.sbuf_tensor("S1b", [128, TC], F32))
        ps_tr = ph.enter_context(nc.psum_tensor("pq_tr", [128, 128], F32))
        ps_tc0 = ph.enter_context(nc.psum_tensor("pq_tc0", [1, 128], F32))
        ps_tc1 = ph.enter_context(nc.psum_tensor("pq_tc1", [1, 128], F32))
        ps_bc = ph.enter_context(nc.psum_tensor("pq_bc", [128, TC], F32))
        t0, v0, s0 = s_t.v, s_v.v, s_s.v
        with nc.Block() as blk:
            @blk.tensor
            def _(tn):
                for jt in range(8):
                    if jt >= 1:
                        tn.wait_ge(s_v.h, v0 + jt)
                    tn.transpose(ps_tr[:, :], mrun[:, jt * 128:(jt + 1) * 128],
                                 idf[:, :]).then_inc(s_t.h, 1)
                s_t.v += 8
                for i in range(8):
                    tn.wait_ge(s_v.h, v0 + (i + 1 if i < 2 else 8 + i))
                    tn.transpose((ps_tc0 if i % 2 == 0 else ps_tc1)[0:1, :],
                                 mca[:, i:i + 1], idf[:, :]).then_inc(s_t.h, 1)
                s_t.v += 8
                tn.wait_ge(s_v.h, v0 + 17)
                for tb in range(2):
                    ins = tn.matmul(ps_bc[:, tb * 512:(tb + 1) * 512], onesL_f[0:1, :],
                                    rowsA[0:1, tb * 512:(tb + 1) * 512], start=True, stop=True)
                ins.then_inc(s_t.h, 1)
                s_t.v += 1
                tn.wait_ge(s_v.h, v0 + 18)
                for tb in range(2):
                    ins = tn.matmul(ps_bc[:, tb * 512:(tb + 1) * 512], onesL_f[0:1, :],
                                    rowsB[0:1, tb * 512:(tb + 1) * 512], start=True, stop=True)
                ins.then_inc(s_t.h, 1)
                s_t.v += 1

            @blk.vector
            def _(vec):
                for jt in range(8):
                    vec.wait_ge(s_t.h, t0 + jt + 1)
                    vec.reduce_max(mca[:, jt:jt + 1], ps_tr[:, :], axis=AX.X).then_inc(s_v.h, 1)
                for i in range(8):
                    vec.wait_ge(s_t.h, t0 + 8 + i + 1)
                    vec.tensor_copy(rowsA[0:1, i * 128:(i + 1) * 128],
                                    (ps_tc0 if i % 2 == 0 else ps_tc1)[0:1, :]).then_inc(s_v.h, 1)
                s_v.v += 16
                vec.tensor_scalar(rowsB[0:1, :], rowsA[0:1, :], 1e-5, 1.0 / 127.0,
                                  op0=ALU.max, op1=ALU.mult)
                vec.reciprocal(rowsA[0:1, :], rowsB[0:1, :])
                vec.engine_nop().then_inc(s_v.h, 1)
                s_v.v += 1
                vec.wait_ge(s_t.h, t0 + 17)
                vec.tensor_copy(S1b[:, :], ps_bc[:, :]).then_inc(s_v.h, 1)
                s_v.v += 1
                vec.wait_ge(s_t.h, t0 + 18)
                vec.tensor_copy(R1b[:, :], ps_bc[:, :]).then_inc(s_v.h, 1)
                s_v.v += 1
                for fc in range(NE):
                    vec.tensor_mul(v_t[:, :], y1[:, fc * TC:(fc + 1) * TC], S1b[:, :])
                    vec.engine_nop().then_inc(s_v.h, 1)
                    vec.wait_ge(s_s.h, s0 + fc + 1)
                    vec.tensor_sub(r_t[:, :], v_t[:, :], ci_t[:, :])
                    vec.tensor_scalar(an_t[:, :], r_t[:, :], 0.5, -1.0, op0=ALU.is_gt, op1=ALU.mult)
                    vec.scalar_tensor_tensor(r_t[:, :], r_t[:, :], -0.5, an_t[:, :],
                                             op0=ALU.is_lt, op1=ALU.add)
                    vec.tensor_sub(xq1[:, fc * TC:(fc + 1) * TC], ci_t[:, :], r_t[:, :])
                    vec.engine_nop().then_inc(s_v.h, 1)
                s_v.v += 2 * NE

            @blk.scalar
            def _(sc):
                for fc in range(NE):
                    sc.wait_ge(s_v.h, v0 + 19 + 2 * fc + 1)
                    sc.activation(ci_t[:, :], v_t[:, :], AF.Identity).then_inc(s_s.h, 1)
                s_s.v += NE

            @blk.sync
            def _(sync):
                for fc in range(NE):
                    sync.wait_ge(s_v.h, v0 + 19 + 2 * fc + 2)
                    sync.dma_start(out=bass.AP(xq1_hbm, fc * 128 * TC, [[TC, 128], [1, TC]]),
                                   in_=xq1[:, fc * TC:(fc + 1) * TC]).then_inc(s_o.h, 16)
                    s_o.v += 16
                sync.wait_ge(s_v.h, v0 + 19)
                sync.dma_start(out=r1_hbm.ap()[:, :], in_=R1b[:, :]).then_inc(s_o.h, 16)
                s_o.v += 16
                if _DEBUG:
                    sync.dma_start(out=dbg["d_y1"].ap()[:, :], in_=y1[:, :]).then_inc(s_o.h, 16)
                    sync.dma_start(out=dbg["d_xq1"].ap()[:, :], in_=xq1[:, :]).then_inc(s_o.h, 16)
                    sync.dma_start(out=dbg["d_s1"].ap()[:, :], in_=S1b[:, :]).then_inc(s_o.h, 16)
                    sync.dma_start(out=dbg["d_rows"].ap()[0:1, 0:TC], in_=rowsA[0:1, :]).then_inc(s_o.h, 16)
                    sync.dma_start(out=dbg["d_rows"].ap()[0:1, TC:2 * TC], in_=rowsB[0:1, :]).then_inc(s_o.h, 16)
                    sync.dma_start(out=dbg["d_mrun"].ap()[:, :], in_=mrun[:, :]).then_inc(s_o.h, 16)
                    s_o.v += 96
                sync.wait_ge(s_o.h, s_o.v)

    cstack0.close()

    if _PHASES <= 5:
        return _early_out("p5")

    with contextlib.ExitStack() as ph:
        stg = ph.enter_context(nc.sbuf_tensor("d0stg", [128, 2 * 8192], F32))
        absb = ph.enter_context(nc.sbuf_tensor("d0abs", [128, 8192], FP16))
        s1row = ph.enter_context(nc.sbuf_tensor("s1row", [1, HID], F32))
        n1row = ph.enter_context(nc.sbuf_tensor("n1row", [1, HID], F32))
        ps_r = ph.enter_context(nc.psum_tensor("d0pr", [1, 512], F32))
        g0 = s_in.v
        p00, p10 = s_p0.v, s_p1.v
        t0, v0, s0 = s_t.v, s_v.v, s_s.v
        with nc.Block() as blk:
            @blk.sync
            def _(sync):
                for hb in range(16):
                    if hb >= 2:
                        sync.wait_ge(s_s.h, s0 + hb - 1)
                    sp = s_p0 if hb % 2 == 0 else s_p1
                    sync.dma_start(out=stg[:, (hb % 2) * 8192:(hb % 2) * 8192 + 8192],
                                   in_=bass.AP(w1T, hb * 512, [[HID, 128], [128 * HID, NE], [1, 512]])
                                   ).then_inc(sp.h, 16)
                    sp.v += 16

            @blk.scalar
            def _(sc):
                for hb in range(16):
                    sp0 = p00 if hb % 2 == 0 else p10
                    sc.wait_ge((s_p0 if hb % 2 == 0 else s_p1).h, sp0 + 16 * (hb // 2 + 1))
                    if hb >= 1:
                        sc.wait_ge(s_t.h, t0 + hb)
                    sc.activation(absb[:, :], stg[:, (hb % 2) * 8192:(hb % 2) * 8192 + 8192],
                                  AF.Abs).then_inc(s_s.h, 1)
                s_s.v += 16

            @blk.tensor
            def _(tn):
                for hb in range(16):
                    tn.wait_ge(s_s.h, s0 + hb + 1)
                    if hb >= 1:
                        tn.wait_ge(s_v.h, v0 + hb)
                    for ec in range(NE):
                        ins = tn.matmul(ps_r[0:1, :], ones_b[:, 0:1],
                                        absb[:, ec * 512:(ec + 1) * 512],
                                        start=(ec == 0), stop=(ec == NE - 1))
                    ins.then_inc(s_t.h, 1)
                s_t.v += 16

            @blk.vector
            def _(vec):
                for hb in range(16):
                    vec.wait_ge(s_t.h, t0 + hb + 1)
                    vec.tensor_scalar(s1row[0:1, hb * 512:(hb + 1) * 512], ps_r[0:1, :],
                                      1.0 / E, 1e-5, op0=ALU.mult, op1=ALU.max).then_inc(s_v.h, 1)
                s_v.v += 16
                vec.reciprocal(n1row[0:1, :], s1row[0:1, :])
                vec.tensor_scalar(n1row[0:1, :], n1row[0:1, :], -2.0, None, op0=ALU.mult)

        with contextlib.ExitStack() as p2:
            ps_tp = p2.enter_context(nc.psum_tensor("d0tp", [128, 1], F32))
            t0, v0 = s_t.v, s_v.v
            with nc.Block() as blk:
                @blk.tensor
                def _(tn):
                    for i in range(2 * NH):
                        row = s1row if i < NH else n1row
                        off = i % NH
                        if i >= 1:
                            tn.wait_ge(s_v.h, v0 + i)
                        tn.transpose(ps_tp[:, :], row[0:1, off * 128:(off + 1) * 128],
                                     idf[0:1, 0:1]).then_inc(s_t.h, 1)
                    s_t.v += 2 * NH

                @blk.vector
                def _(vec):
                    for i in range(2 * NH):
                        dst = w1s_pp if i < NH else n1_pp
                        off = i % NH
                        vec.wait_ge(s_t.h, t0 + i + 1)
                        vec.tensor_copy(dst[:, off:off + 1], ps_tp[:, :]).then_inc(s_v.h, 1)
                    s_v.v += 2 * NH

    with contextlib.ExitStack() as ph2:
        stg = ph2.enter_context(nc.sbuf_tensor("d0s2", [128, 2 * 2048], F32))
        absb = ph2.enter_context(nc.sbuf_tensor("d0a2", [128, 2048], F32))
        s2row = ph2.enter_context(nc.sbuf_tensor("s2row", [1, E], F32))
        n2row = ph2.enter_context(nc.sbuf_tensor("n2row", [1, E], F32))
        ps_r = ph2.enter_context(nc.psum_tensor("d0p2", [1, 2048], F32))
        g0 = s_in.v
        p00, p10 = s_p0.v, s_p1.v
        t0, v0, s0 = s_t.v, s_v.v, s_s.v
        with nc.Block() as blk:
            @blk.sync
            def _(sync):
                for hc in range(NH):
                    if hc >= 2:
                        sync.wait_ge(s_s.h, s0 + hc - 1)
                    sp = s_p0 if hc % 2 == 0 else s_p1
                    sync.dma_start(out=stg[:, (hc % 2) * 2048:(hc % 2) * 2048 + 2048],
                                   in_=bass.AP(w2T, hc * 128 * E, [[E, 128], [1, E]])
                                   ).then_inc(sp.h, 16)
                    sp.v += 16

            @blk.scalar
            def _(sc):
                for hc in range(NH):
                    sp0 = p00 if hc % 2 == 0 else p10
                    sc.wait_ge((s_p0 if hc % 2 == 0 else s_p1).h, sp0 + 16 * (hc // 2 + 1))
                    if hc >= 1:
                        sc.wait_ge(s_t.h, t0 + hc)
                    sc.activation(absb[:, :], stg[:, (hc % 2) * 2048:(hc % 2) * 2048 + 2048],
                                  AF.Abs).then_inc(s_s.h, 1)
                s_s.v += NH

            @blk.tensor
            def _(tn):
                for hc in range(NH):
                    tn.wait_ge(s_s.h, s0 + hc + 1)
                    for fg in range(4):
                        ins = tn.matmul(ps_r[0:1, fg * 512:(fg + 1) * 512], ones_f[:, 0:1],
                                        absb[:, fg * 512:(fg + 1) * 512],
                                        start=(hc == 0), stop=(hc == NH - 1))
                    ins.then_inc(s_t.h, 1)
                s_t.v += NH

            @blk.vector
            def _(vec):
                vec.wait_ge(s_t.h, t0 + NH)
                vec.tensor_scalar(s2row[0:1, :], ps_r[0:1, :], 1.0 / HID, 1e-5,
                                  op0=ALU.mult, op1=ALU.max)
                vec.reciprocal(n2row[0:1, :], s2row[0:1, :])
                vec.tensor_scalar(n2row[0:1, :], n2row[0:1, :], -2.0, None, op0=ALU.mult)

        with contextlib.ExitStack() as p2:
            ps_tp = p2.enter_context(nc.psum_tensor("dspp", [128, 1], F32))
            t0, v0 = s_t.v, s_v.v
            with nc.Block() as blk:
                @blk.tensor
                def _(tn):
                    for i in range(NE):
                        if i >= 1:
                            tn.wait_ge(s_v.h, v0 + i)
                        tn.transpose(ps_tp[:, :], s2row[0:1, i * 128:(i + 1) * 128],
                                     idf[0:1, 0:1]).then_inc(s_t.h, 1)
                    s_t.v += NE

                @blk.vector
                def _(vec):
                    for i in range(NE):
                        vec.wait_ge(s_t.h, t0 + i + 1)
                        vec.tensor_copy(w2s_pp[:, i:i + 1], ps_tp[:, :]).then_inc(s_v.h, 1)
                    s_v.v += NE

        ph = ph2
        stg = ph.enter_context(nc.sbuf_tensor("d1s", [128, 2 * 2048], F32))
        ub = ph.enter_context(nc.sbuf_tensor("d1u", [128, 2048], F32))
        bn = ph.enter_context(nc.sbuf_tensor("d1b", [128, 2048], BF16))
        n2b = ph.enter_context(nc.sbuf_tensor("n2b", [128, 2048], F32))
        w2q_t = ph.enter_context(nc.sbuf_tensor("w2q_t", [128, 2 * 2048], FP8))
        ps_bc = ph.enter_context(nc.psum_tensor("d1p", [128, 2048], F32))
        g0 = s_in.v
        t0, v0, o0 = s_t.v, s_v.v, s_o.v
        p00, p10, q00, q10 = s_p0.v, s_p1.v, s_q0.v, s_q1.v
        with nc.Block() as blk:
            @blk.tensor
            def _(tn):
                for fg in range(4):
                    ins = tn.matmul(ps_bc[:, fg * 512:(fg + 1) * 512], onesL_f[0:1, :],
                                    n2row[0:1, fg * 512:(fg + 1) * 512], start=True, stop=True)
                ins.then_inc(s_t.h, 1)
                s_t.v += 1

            @blk.sync
            def _(sync):
                for hc in range(NH):
                    if hc >= 2:
                        sync.wait_ge(s_v.h, v0 + hc - 1)
                    sp = s_p0 if hc % 2 == 0 else s_p1
                    sync.dma_start(out=stg[:, (hc % 2) * 2048:(hc % 2) * 2048 + 2048],
                                   in_=bass.AP(w2T, hc * 128 * E, [[E, 128], [1, E]])
                                   ).then_inc(sp.h, 16)
                    sp.v += 16

            @blk.vector
            def _(vec):
                vec.wait_ge(s_t.h, t0 + 1)
                vec.tensor_copy(n2b[:, :], ps_bc[:, :])
                for hc in range(NH):
                    vec.wait_ge((s_p0 if hc % 2 == 0 else s_p1).h,
                                (p00 if hc % 2 == 0 else p10) + 16 * (hc // 2 + 1))
                    if hc >= 2:
                        vec.wait_ge((s_q0 if hc % 2 == 0 else s_q1).h,
                                    (q00 if hc % 2 == 0 else q10) + 16 * ((hc - 2) // 2 + 1))
                    sl = (hc % 2) * 2048
                    vec.tensor_mul(ub[:, :], stg[:, sl:sl + 2048], n2b[:, :])
                    vec.tensor_scalar(bn[:, :], ub[:, :], 1.0, -1.0, op0=ALU.is_gt, op1=ALU.mult)
                    vec.scalar_tensor_tensor(w2q_t[:, sl:sl + 2048], ub[:, :], -1.0, bn[:, :],
                                             op0=ALU.is_lt, op1=ALU.add).then_inc(s_v.h, 1)
                s_v.v += NH

            @blk.gpsimd
            def _(gp):
                for hc in range(NH):
                    gp.wait_ge(s_v.h, v0 + hc + 1)
                    sl = (hc % 2) * 2048
                    sq = s_q0 if hc % 2 == 0 else s_q1
                    gp.dma_start(out=bass.AP(w2q_hbm, hc * 128 * E, [[E, 128], [1, E]]),
                                 in_=w2q_t[:, sl:sl + 2048]).then_inc(sq.h, 16)
                    sq.v += 16
                gp.wait_ge(s_q0.h, s_q0.v)
                gp.wait_ge(s_q1.h, s_q1.v)

    if _PHASES <= 6:
        return _early_out("p6")
    e_stack = contextlib.ExitStack()
    g_sb = e_stack.enter_context(nc.sbuf_tensor("g_sb", [128, NH * TC], FP16))
    mg = e_stack.enter_context(nc.sbuf_tensor("mg", [128, TC], F32))
    mg_mn = e_stack.enter_context(nc.sbuf_tensor("mg_mn", [128, TC], F32))

    with contextlib.ExitStack() as ph:
        stg = ph.enter_context(nc.sbuf_tensor("d2s", [128, 2 * 2048], F32))
        ub = ph.enter_context(nc.sbuf_tensor("d2u", [128, 1024], F32))
        bn = ph.enter_context(nc.sbuf_tensor("d2b", [128, 1024], BF16))
        n1bt = ph.enter_context(nc.sbuf_tensor("n1bt", [128, 2 * 128], F32))
        w1q_t = ph.enter_context(nc.sbuf_tensor("w1q_t", [128, 2 * 2048], FP8))
        t0t = ph.enter_context(nc.sbuf_tensor("d2t0", [128, 4 * 512], FP16))
        n1r_sb = ph.enter_context(nc.sbuf_tensor("n1r_sb", [1, 2 * 128], F32))
        xq1 = ph.enter_context(nc.sbuf_tensor("xq1d", [128, NE * TC], BF16))
        R1b = ph.enter_context(nc.sbuf_tensor("R1bd", [128, TC], F32))
        ps_m = ph.enter_context(nc.psum_tensor("d2pm", [128, 2048], F32))
        ps_bc0 = ph.enter_context(nc.psum_tensor("d2pb0", [128, 128], F32))
        ps_bc1 = ph.enter_context(nc.psum_tensor("d2pb1", [128, 128], F32))
        ps_n10 = ph.enter_context(nc.psum_tensor("d2pn0", [1, 128], F32))
        ps_n11 = ph.enter_context(nc.psum_tensor("d2pn1", [1, 128], F32))

        T0, V0, S0 = s_t.v, s_v.v, s_s.v
        P00, P10 = s_p0.v, s_p1.v
        I0 = s_in.v
        with nc.Block() as blk:
            @blk.sync
            def _(sync):
                sync.dma_start(out=xq1[:, :],
                               in_=bass.AP(xq1_hbm, 0, [[TC, 128], [128 * TC, NE], [1, TC]])
                               ).then_inc(s_in.h, 16)
                sync.dma_start(out=R1b[:, :], in_=r1_hbm.ap()[:, :]).then_inc(s_in.h, 16)
                s_in.v += 32
                for hc in range(NH):
                    if hc >= 2:
                        sync.wait_ge(s_v.h, V0 + 7 * (hc - 2) + 3)
                    sp = s_p0 if hc % 2 == 0 else s_p1
                    sync.dma_start(out=stg[:, (hc % 2) * 2048:(hc % 2) * 2048 + 2048],
                                   in_=bass.AP(w1T, hc * 128, [[HID, 128], [128 * HID, NE], [1, 128]])
                                   ).then_inc(sp.h, 16)
                    sp.v += 16

            @blk.tensor
            def _(tn):
                tn.wait_ge(s_in.h, I0 + 32)
                for hc in range(NH):
                    psn = ps_n10 if hc % 2 == 0 else ps_n11
                    psb = ps_bc0 if hc % 2 == 0 else ps_bc1
                    if hc >= 2:
                        tn.wait_ge(s_v.h, V0 + 7 * (hc - 2) + 1)
                    tn.transpose(psn[0:1, :], n1_pp[:, hc:hc + 1], idf[:, :]).then_inc(s_t.h, 1)
                    tn.wait_ge(s_v.h, V0 + 7 * hc + 1)
                    tn.matmul(psb[:, :], onesL_f[0:1, :],
                              n1r_sb[0:1, (hc % 2) * 128:(hc % 2) * 128 + 128],
                              start=True, stop=True).then_inc(s_t.h, 1)
                    tn.wait_ge(s_v.h, V0 + 7 * hc + 3)
                    for th in range(2):
                        sl = (hc % 2) * 1024
                        for ec in range(NE):
                            ins = tn.matmul(ps_m[:, sl + th * 512: sl + th * 512 + 512],
                                            w1q_t[:, (hc % 2) * 2048 + ec * 128:(hc % 2) * 2048 + ec * 128 + 128],
                                            xq1[:, ec * TC + th * 512: ec * TC + th * 512 + 512],
                                            start=(ec == 0), stop=(ec == NE - 1))
                        ins.then_inc(s_t.h, 1)
                s_t.v += 4 * NH

            @blk.vector
            def _(vec):
                vec.memset(mg[:, :], 0.0)
                vec.memset(mg_mn[:, :], 0.0)
                for hc in range(NH):
                    psn = ps_n10 if hc % 2 == 0 else ps_n11
                    psb = ps_bc0 if hc % 2 == 0 else ps_bc1
                    nsl = (hc % 2) * 128
                    vec.wait_ge(s_t.h, T0 + 4 * hc + 1)
                    vec.tensor_copy(n1r_sb[0:1, nsl:nsl + 128], psn[0:1, :]).then_inc(s_v.h, 1)
                    vec.wait_ge(s_t.h, T0 + 4 * hc + 2)
                    vec.tensor_copy(n1bt[:, nsl:nsl + 128], psb[:, :]).then_inc(s_v.h, 1)
                    sp = s_p0 if hc % 2 == 0 else s_p1
                    vec.wait_ge(sp.h, (P00 if hc % 2 == 0 else P10) + 16 * (hc // 2 + 1))
                    sl = (hc % 2) * 2048
                    for hh in range(2):
                        vec.tensor_tensor(ub[:, :], stg[:, sl + hh * 1024: sl + hh * 1024 + 1024],
                                          bass.AP(n1bt, nsl, [[256, 128], [0, NE // 2], [1, 128]]),
                                          op=ALU.mult)
                        vec.tensor_scalar(bn[:, :], ub[:, :], 1.0, -1.0, op0=ALU.is_gt, op1=ALU.mult)
                        vec.scalar_tensor_tensor(w1q_t[:, sl + hh * 1024: sl + hh * 1024 + 1024],
                                                 ub[:, :], -1.0, bn[:, :],
                                                 op0=ALU.is_lt, op1=ALU.add)
                    vec.engine_nop().then_inc(s_v.h, 1)
                    tsl = (hc % 2) * 1024
                    for th in range(2):
                        vec.wait_ge(s_t.h, T0 + 4 * hc + 3 + th)
                        if hc >= 2:
                            vec.wait_ge(s_s.h, S0 + 2 * (hc - 2) + th + 1)
                        vec.tensor_mul(t0t[:, tsl + th * 512: tsl + th * 512 + 512],
                                       ps_m[:, tsl + th * 512: tsl + th * 512 + 512],
                                       R1b[:, th * 512:(th + 1) * 512])
                        vec.engine_nop().then_inc(s_v.h, 1)
                    for th in range(2):
                        vec.wait_ge(s_s.h, S0 + 2 * hc + th + 1)
                        vec.tensor_max(mg[:, th * 512:(th + 1) * 512],
                                       mg[:, th * 512:(th + 1) * 512],
                                       g_sb[:, hc * TC + th * 512: hc * TC + th * 512 + 512])
                        vec.tensor_tensor(mg_mn[:, th * 512:(th + 1) * 512],
                                          mg_mn[:, th * 512:(th + 1) * 512],
                                          g_sb[:, hc * TC + th * 512: hc * TC + th * 512 + 512],
                                          op=ALU.min).then_inc(s_v.h, 1)
                s_v.v += 7 * NH

            @blk.scalar
            def _(sc):
                for hc in range(NH):
                    tsl = (hc % 2) * 1024
                    for th in range(2):
                        sc.wait_ge(s_v.h, V0 + 7 * hc + 4 + th)
                        sc.activation(g_sb[:, hc * TC + th * 512: hc * TC + th * 512 + 512],
                                      t0t[:, tsl + th * 512: tsl + th * 512 + 512],
                                      AF.Gelu_apprx_tanh,
                                      scale=w1s_pp[:, hc:hc + 1]).then_inc(s_s.h, 1)
                s_s.v += 2 * NH

    R2b = e_stack.enter_context(nc.sbuf_tensor("R2b", [128, TC], F32))
    with contextlib.ExitStack() as ph:
        mca = ph.enter_context(nc.sbuf_tensor("e0ma", [128, 8], F32))
        v_t = ph.enter_context(nc.sbuf_tensor("e0v", [128, TC], F32))
        ci_t = ph.enter_context(nc.sbuf_tensor("e0c", [128, TC], I32))
        r_t = ph.enter_context(nc.sbuf_tensor("e0r", [128, TC], F32))
        an_t = ph.enter_context(nc.sbuf_tensor("e0a", [128, TC], F32))
        S2b = ph.enter_context(nc.sbuf_tensor("S2b", [128, TC], F32))
        rA = ph.enter_context(nc.sbuf_tensor("eq_ra", [1, TC], F32))
        rB = ph.enter_context(nc.sbuf_tensor("eq_rb", [1, TC], F32))
        ps_tr = ph.enter_context(nc.psum_tensor("e0pt", [128, 128], F32))
        ps_tc0 = ph.enter_context(nc.psum_tensor("e0pc0", [1, 128], F32))
        ps_tc1 = ph.enter_context(nc.psum_tensor("e0pc1", [1, 128], F32))
        ps_bc = ph.enter_context(nc.psum_tensor("e0pb", [128, TC], F32))
        t0, v0, s0 = s_t.v, s_v.v, s_s.v
        with nc.Block() as blk:
            @blk.tensor
            def _(tn):
                for jt in range(8):
                    tn.wait_ge(s_v.h, v0 + 1 + max(0, jt))
                    tn.transpose(ps_tr[:, :], mg[:, jt * 128:(jt + 1) * 128],
                                 idf[:, :]).then_inc(s_t.h, 1)
                s_t.v += 8
                for i in range(8):
                    tn.wait_ge(s_v.h, v0 + 1 + (i + 1 if i < 2 else 8 + i))
                    tn.transpose((ps_tc0 if i % 2 == 0 else ps_tc1)[0:1, :],
                                 mca[:, i:i + 1], idf[:, :]).then_inc(s_t.h, 1)
                s_t.v += 8
                tn.wait_ge(s_v.h, v0 + 18)
                for tb in range(2):
                    ins = tn.matmul(ps_bc[:, tb * 512:(tb + 1) * 512], onesL_f[0:1, :],
                                    rA[0:1, tb * 512:(tb + 1) * 512], start=True, stop=True)
                ins.then_inc(s_t.h, 1)
                s_t.v += 1
                tn.wait_ge(s_v.h, v0 + 19)
                for tb in range(2):
                    ins = tn.matmul(ps_bc[:, tb * 512:(tb + 1) * 512], onesL_f[0:1, :],
                                    rB[0:1, tb * 512:(tb + 1) * 512], start=True, stop=True)
                ins.then_inc(s_t.h, 1)
                s_t.v += 1

            @blk.vector
            def _(vec):
                vec.tensor_scalar(mg_mn[:, :], mg_mn[:, :], -1.0, None, op0=ALU.mult)
                vec.tensor_max(mg[:, :], mg[:, :], mg_mn[:, :]).then_inc(s_v.h, 1)
                s_v.v += 1
                for jt in range(8):
                    vec.wait_ge(s_t.h, t0 + jt + 1)
                    vec.reduce_max(mca[:, jt:jt + 1], ps_tr[:, :], axis=AX.X).then_inc(s_v.h, 1)
                for i in range(8):
                    vec.wait_ge(s_t.h, t0 + 8 + i + 1)
                    vec.tensor_copy(rA[0:1, i * 128:(i + 1) * 128],
                                    (ps_tc0 if i % 2 == 0 else ps_tc1)[0:1, :]).then_inc(s_v.h, 1)
                s_v.v += 16
                vec.tensor_scalar(rB[0:1, :], rA[0:1, :], 1e-5, 1.0 / 127.0,
                                  op0=ALU.max, op1=ALU.mult)
                vec.reciprocal(rA[0:1, :], rB[0:1, :])
                vec.engine_nop().then_inc(s_v.h, 1)
                s_v.v += 1
                vec.wait_ge(s_t.h, t0 + 17)
                vec.tensor_copy(S2b[:, :], ps_bc[:, :]).then_inc(s_v.h, 1)
                s_v.v += 1
                vec.wait_ge(s_t.h, t0 + 18)
                vec.tensor_copy(R2b[:, :], ps_bc[:, :]).then_inc(s_v.h, 1)
                s_v.v += 1
                for hc in range(NH):
                    vec.tensor_mul(v_t[:, :], g_sb[:, hc * TC:(hc + 1) * TC], S2b[:, :])
                    vec.engine_nop().then_inc(s_v.h, 1)
                    vec.wait_ge(s_s.h, s0 + hc + 1)
                    vec.tensor_sub(r_t[:, :], v_t[:, :], ci_t[:, :])
                    vec.tensor_scalar(an_t[:, :], r_t[:, :], 0.5, -1.0, op0=ALU.is_gt, op1=ALU.mult)
                    vec.scalar_tensor_tensor(r_t[:, :], r_t[:, :], -0.5, an_t[:, :],
                                             op0=ALU.is_lt, op1=ALU.add)
                    vec.tensor_sub(g_sb[:, hc * TC:(hc + 1) * TC], ci_t[:, :], r_t[:, :])
                    vec.engine_nop().then_inc(s_v.h, 1)
                s_v.v += 2 * NH

            @blk.scalar
            def _(sc):
                for hc in range(NH):
                    sc.wait_ge(s_v.h, v0 + 20 + 2 * hc + 1)
                    sc.activation(ci_t[:, :], v_t[:, :], AF.Identity).then_inc(s_s.h, 1)
                s_s.v += NH

    if _PHASES <= 8:
        e_stack.close()
        return _early_out("p8")
    for ts_i in range(4):
        with contextlib.ExitStack() as ph:
            w2q_t = ph.enter_context(nc.sbuf_tensor(f"e{ts_i}w", [128, 2 * 2048], FP8))
            yn_t = ph.enter_context(nc.sbuf_tensor(f"e{ts_i}y", [128, NE * 256], F32))
            y2 = ph.enter_context(nc.sbuf_tensor(f"e{ts_i}z", [128, NE * 256], F32))
            sqt = ph.enter_context(nc.sbuf_tensor(f"e{ts_i}q", [128, 2 * 256], F32))
            rA = ph.enter_context(nc.sbuf_tensor(f"e{ts_i}ra", [1, 256], F32))
            rB = ph.enter_context(nc.sbuf_tensor(f"e{ts_i}rb", [1, 256], F32))
            A2 = ph.enter_context(nc.sbuf_tensor(f"e{ts_i}A", [128, 256], F32))
            B2 = ph.enter_context(nc.sbuf_tensor(f"e{ts_i}B", [128, 256], F32))
            g0, i0 = s_g.v, s_in.v
            t0, v0, s0 = s_t.v, s_v.v, s_s.v
            p00, p10 = s_p0.v, s_p1.v
            with nc.Block() as blk:
                ps2 = ph.enter_context(nc.psum_tensor(f"e{ts_i}p", [128, 4096], F32))

                @blk.sync
                def _(sync):
                    for hc in range(NH):
                        if hc >= 2:
                            sync.wait_ge(s_t.h, t0 + hc - 1)
                        sp = s_p0 if hc % 2 == 0 else s_p1
                        sync.dma_start(out=w2q_t[:, (hc % 2) * 2048:(hc % 2) * 2048 + 2048],
                                       in_=bass.AP(w2q_hbm, hc * 128 * E, [[E, 128], [1, E]])
                                       ).then_inc(sp.h, 16)
                        sp.v += 16

                @blk.gpsimd
                def _(gp):
                    gp.dma_start(out=yn_t[:, :],
                                 in_=bass.AP(ynorm_hbm, ts_i * 256,
                                             [[TC, 128], [128 * TC, NE], [1, 256]])
                                 ).then_inc(s_g.h, 16)
                    s_g.v += 16

                @blk.tensor
                def _(tn):
                    for hc in range(NH):
                        tn.wait_ge((s_p0 if hc % 2 == 0 else s_p1).h,
                                   (p00 if hc % 2 == 0 else p10) + 16 * (hc // 2 + 1))
                        sl = (hc % 2) * 2048
                        for fc in range(NE):
                            ins = tn.matmul(ps2[:, fc * 256:(fc + 1) * 256],
                                            w2q_t[:, sl + fc * 128: sl + fc * 128 + 128],
                                            g_sb[:, hc * TC + ts_i * 256: hc * TC + ts_i * 256 + 256],
                                            start=(hc == 0 and fc % 2 == 0),
                                            stop=(hc == NH - 1 and fc % 2 == 1))
                        ins.then_inc(s_t.h, 1)
                    s_t.v += NH
                    for fc in range(NE):
                        tn.wait_ge(s_v.h, v0 + NE + fc + 1)
                        tn.matmul(ps2[0:1, 0:256], ones_f[:, 0:1], y2[:, fc * 256:(fc + 1) * 256],
                                  start=(fc == 0), stop=(fc == NE - 1))
                        tn.matmul(ps2[0:1, 512:768], ones_f[:, 0:1],
                                  sqt[:, (fc % 2) * 256:(fc % 2) * 256 + 256],
                                  start=(fc == 0), stop=(fc == NE - 1)).then_inc(s_t.h, 1)
                    s_t.v += NE
                    tn.wait_ge(s_v.h, v0 + 2 * NE + 2)
                    tn.matmul(ps2[:, 1024:1280], onesL_f[0:1, :], rB[0:1, :],
                              start=True, stop=True).then_inc(s_t.h, 1)
                    tn.wait_ge(s_v.h, v0 + 2 * NE + 3)
                    tn.matmul(ps2[:, 1536:1792], onesL_f[0:1, :], rA[0:1, :],
                              start=True, stop=True).then_inc(s_t.h, 1)
                    s_t.v += 2

                @blk.vector
                def _(vec):
                    vec.wait_ge(s_t.h, t0 + NH)
                    vec.wait_ge(s_g.h, g0 + 16)
                    for fc in range(NE):
                        vec.scalar_tensor_tensor(y2[:, fc * 256:(fc + 1) * 256],
                                                 ps2[:, fc * 256:(fc + 1) * 256],
                                                 w2s_pp[:, fc:fc + 1],
                                                 R2b[:, ts_i * 256:(ts_i + 1) * 256],
                                                 op0=ALU.mult, op1=ALU.mult)
                        vec.tensor_add(y2[:, fc * 256:(fc + 1) * 256],
                                       y2[:, fc * 256:(fc + 1) * 256],
                                       yn_t[:, fc * 256:(fc + 1) * 256]).then_inc(s_v.h, 1)
                    for fc in range(NE):
                        if fc >= 2:
                            vec.wait_ge(s_t.h, t0 + NH + fc - 1)
                        vec.tensor_mul(sqt[:, (fc % 2) * 256:(fc % 2) * 256 + 256],
                                       y2[:, fc * 256:(fc + 1) * 256],
                                       y2[:, fc * 256:(fc + 1) * 256]).then_inc(s_v.h, 1)
                    s_v.v += 2 * NE
                    vec.wait_ge(s_t.h, t0 + NH + NE)
                    vec.tensor_scalar(rA[0:1, :], ps2[0:1, 0:256], 1.0 / E, None, op0=ALU.mult)
                    vec.tensor_scalar(rB[0:1, :], ps2[0:1, 512:768], 1.0 / E, None, op0=ALU.mult)
                    vec.tensor_mul(A2[0:1, 0:256], rA[0:1, :], rA[0:1, :])
                    vec.tensor_sub(rB[0:1, :], rB[0:1, :], A2[0:1, 0:256])
                    vec.tensor_scalar(rB[0:1, :], rB[0:1, :], 1e-5, None, op0=ALU.add
                                      ).then_inc(s_v.h, 1)
                    s_v.v += 1
                    vec.wait_ge(s_s.h, s0 + 1)
                    vec.reciprocal(A2[0:1, 0:256], B2[0:1, 0:256])
                    vec.tensor_mul(B2[0:1, 0:256], A2[0:1, 0:256], A2[0:1, 0:256])
                    vec.tensor_mul(B2[0:1, 0:256], B2[0:1, 0:256], rB[0:1, :])
                    vec.tensor_scalar(B2[0:1, 0:256], B2[0:1, 0:256], -0.5, 1.5,
                                      op0=ALU.mult, op1=ALU.add)
                    vec.tensor_mul(rB[0:1, :], A2[0:1, 0:256], B2[0:1, 0:256])
                    vec.tensor_mul(rA[0:1, :], rA[0:1, :], rB[0:1, :])
                    vec.tensor_scalar(rA[0:1, :], rA[0:1, :], -1.0, None, op0=ALU.mult
                                      ).then_inc(s_v.h, 1)
                    s_v.v += 1
                    vec.wait_ge(s_t.h, t0 + NH + NE + 1)
                    vec.tensor_copy(A2[:, :], ps2[:, 1024:1280]).then_inc(s_v.h, 1)
                    s_v.v += 1
                    vec.wait_ge(s_t.h, t0 + NH + NE + 2)
                    vec.tensor_copy(B2[:, :], ps2[:, 1536:1792]).then_inc(s_v.h, 1)
                    s_v.v += 1
                    for fc in range(NE):
                        vec.tensor_mul(y2[:, fc * 256:(fc + 1) * 256],
                                       y2[:, fc * 256:(fc + 1) * 256], A2[:, :])
                        vec.tensor_add(y2[:, fc * 256:(fc + 1) * 256],
                                       y2[:, fc * 256:(fc + 1) * 256], B2[:, :]
                                       ).then_inc(s_v.h, 1)
                    s_v.v += NE

                @blk.scalar
                def _(sc):
                    sc.wait_ge(s_v.h, v0 + 2 * NE + 1)
                    sc.activation(B2[0:1, 0:256], rB[0:1, :], AF.Sqrt).then_inc(s_s.h, 1)
                    s_s.v += 1

                @blk.sync
                def _(sync2):
                    for fc in range(NE):
                        sync2.wait_ge(s_v.h, v0 + 2 * NE + 4 + fc + 1)
                        sync2.dma_start(out=bass.AP(out_d, fc * 128 * TC + ts_i * 256,
                                                    [[TC, 128], [1, 256]]),
                                        in_=y2[:, fc * 256:(fc + 1) * 256]).then_inc(s_o.h, 16)
                        s_o.v += 16
                    sync2.wait_ge(s_o.h, s_o.v)
    e_stack.close()

    top.close()
    return nc


def _host_prep(x, wq, wk, wv, wo, w1, w2):
    ins = {}
    ins["wqT"] = np.ascontiguousarray(wq.T)
    ins["wkT"] = np.ascontiguousarray(wk.T)
    ins["wvT"] = np.ascontiguousarray(wv.T)
    ins["woT"] = np.ascontiguousarray(wo.T)
    ins["w1T"] = np.ascontiguousarray(w1.T)
    ins["w2T"] = np.ascontiguousarray(w2.T)

    import jax
    import jax.numpy as jnp
    cpu = jax.devices("cpu")[0]
    with jax.default_device(cpu):
        inv_j = 1.0 / (1000000.0 ** (jnp.arange(0, D, 2, dtype=jnp.float32) / D))
        fang = jnp.arange(T, dtype=jnp.float32)[:, None] * inv_j[None, :]
        cos32 = np.asarray(jnp.concatenate([jnp.cos(fang), jnp.cos(fang)], axis=-1))
        sin32 = np.asarray(jnp.concatenate([jnp.sin(fang), jnp.sin(fang)], axis=-1))
        sf = np.float64(np.asarray(jnp.sqrt(jnp.float32(D))))
    cos = cos32.astype(np.float64).T
    sin = sin32.astype(np.float64).T
    ss = sin.copy()
    ss[0:64] = -sin[0:64]
    sc = 1.0 / sf

    def hl(a):
        hi = a.astype(np.float32)
        lo = (a - hi.astype(np.float64)).astype(np.float32)
        return np.ascontiguousarray(hi), np.ascontiguousarray(lo)

    import ml_dtypes
    ins["identb"] = np.eye(128, dtype=np.float16)
    ins["identf"] = np.eye(128, dtype=np.float32)
    percore = []
    for c in range(8):
        b, half = c // 2, c % 2
        toks = np.concatenate([np.arange((2 * i + half) * 128, (2 * i + half + 1) * 128)
                               for i in range(8)])
        d = dict(ins)
        d["xT"] = np.ascontiguousarray(x[b].T)
        d["xTm"] = np.ascontiguousarray(x[b].T[:, toks])
        d["coskhi"], d["cosklo"] = hl(cos)
        d["sskhi"], d["ssklo"] = hl(ss)
        d["cosqhi"], d["cosqlo"] = hl(cos[:, toks] * sc)
        d["ssqhi"], d["ssqlo"] = hl(ss[:, toks] * sc)
        mk = np.zeros((128, NPAIR * 4 * 256), dtype=np.float32)
        for j in range(NPAIR):
            nk = 4 * j + 4
            for ci_, kc in enumerate(range(nk - 4, nk)):
                for pp in range(2):
                    pos = 2 * j + pp
                    qt = (2 * pos + half) * 128 + np.arange(128)
                    kt = kc * 128 + np.arange(128)
                    blkm = (qt[None, :] >= kt[:, None]).astype(np.float32)
                    mk[:, (j * 4 + ci_) * 256 + pp * 128:(j * 4 + ci_) * 256 + (pp + 1) * 128] = blkm
        d["maskp"] = mk.astype(np.float16)
        percore.append(d)
    return percore


def kernel(L0_attn_x, L0_attn_wq, L0_attn_wk, L0_attn_wv, L0_attn_wo, L0_ffn_w1, L0_ffn_w2):
    x = np.ascontiguousarray(np.asarray(L0_attn_x, dtype=np.float32))
    args = [np.asarray(a, dtype=np.float32) for a in
            (L0_attn_wq, L0_attn_wk, L0_attn_wv, L0_attn_wo, L0_ffn_w1, L0_ffn_w2)]
    in_maps = _host_prep(x, *args)
    nc = build_program()
    res = run_bass_kernel_spmd(nc, in_maps, core_ids=list(range(8)))
    out = np.empty((B, T, E), np.float32)
    for c in range(8):
        o = res.results[c]["out"]
        half = c % 2
        for i in range(8):
            t0_ = (2 * i + half) * 128
            out[c // 2, t0_:t0_ + 128, :] = o[:, i * 128:(i + 1) * 128].T
    return out


# version 16 (final)
# speedup vs baseline: 1.1134x; 1.0214x over previous
import contextlib
import math

import numpy as np

import concourse.bass as bass
import concourse.mybir as mybir
from concourse.bass_utils import run_bass_kernel_spmd

F32 = mybir.dt.float32
BF16 = mybir.dt.bfloat16
FP16 = mybir.dt.float16
FP8 = mybir.dt.float8e4
I32 = mybir.dt.int32
AF = mybir.ActivationFunctionType
ALU = mybir.AluOpType
AX = mybir.AxisListType

B, T, E = 4, 2048, 2048
H, D = 16, 128
HID = 8192
TC = 1024
NE = E // 128
NH = HID // 128
NPAIR = 4

_DEBUG = False
_PHASES = 10


class Sem:
    def __init__(self, stack, nc, name):
        self.h = stack.enter_context(nc.semaphore(name))
        self.v = 0


def build_program():
    nc = bass.Bass("TRN2", target_bir_lowering=False, detect_race_conditions=False)

    xT = nc.dram_tensor("xT", [E, T], F32, kind="ExternalInput")
    xTm = nc.dram_tensor("xTm", [E, TC], F32, kind="ExternalInput")
    wqT = nc.dram_tensor("wqT", [E, E], F32, kind="ExternalInput")
    wkT = nc.dram_tensor("wkT", [E, E], F32, kind="ExternalInput")
    wvT = nc.dram_tensor("wvT", [E, E], F32, kind="ExternalInput")
    woT = nc.dram_tensor("woT", [E, E], F32, kind="ExternalInput")
    w1T = nc.dram_tensor("w1T", [E, HID], F32, kind="ExternalInput")
    w2T = nc.dram_tensor("w2T", [HID, E], F32, kind="ExternalInput")
    tabs = {}
    for nm, cols in [("cosq", TC), ("ssq", TC), ("cosk", T), ("ssk", T)]:
        for p in ("hi", "lo"):
            tabs[nm + p] = nc.dram_tensor(nm + p, [D, cols], F32, kind="ExternalInput")
    maskp = nc.dram_tensor("maskp", [128, NPAIR * 4 * 256], FP16, kind="ExternalInput")
    identb = nc.dram_tensor("identb", [128, 128], FP16, kind="ExternalInput")
    identf = nc.dram_tensor("identf", [128, 128], F32, kind="ExternalInput")
    out_d = nc.dram_tensor("out", [TC, E], F32, kind="ExternalOutput")
    w2q_hbm = nc.dram_tensor("w2q_hbm", [NH * 128, E], FP8)
    ynorm_hbm = nc.dram_tensor("ynorm_hbm", [E, TC], FP16)
    xq1_hbm = nc.dram_tensor("xq1_hbm", [E, TC], BF16)
    r1_hbm = nc.dram_tensor("r1_hbm", [128, TC], F32)

    dbg = {}
    if _DEBUG:
        for nm, sh, dt in [("d_k", [D, T], FP16), ("d_q", [D, TC], FP16),
                           ("d_vt", [128, 16 * 128], FP16), ("d_ctx", [128, H * TC], FP16),
                           ("d_y1", [128, NE * TC], F32), ("d_xq1", [128, NE * TC], BF16),
                           ("d_g", [128, 8 * TC], FP16), ("d_s1", [128, TC], F32),
                           ("d_rows", [1, 2 * TC], F32), ("d_mrun", [128, TC], F32)]:
            dbg[nm] = nc.dram_tensor(nm, sh, dt, kind="ExternalOutput")

    top = contextlib.ExitStack()
    s_in = Sem(top, nc, "s_in")
    s_g = Sem(top, nc, "s_g")
    s_t = Sem(top, nc, "s_t")
    s_v = Sem(top, nc, "s_v")
    s_s = Sem(top, nc, "s_s")
    s_o = Sem(top, nc, "s_o")
    s_p0 = Sem(top, nc, "s_p0")
    s_p1 = Sem(top, nc, "s_p1")
    s_q0 = Sem(top, nc, "s_q0")
    s_q1 = Sem(top, nc, "s_q1")

    ones_f = top.enter_context(nc.sbuf_tensor("ones_f", [128, 1], F32))
    ones_b = top.enter_context(nc.sbuf_tensor("ones_b", [128, 1], FP16))
    onesL_f = top.enter_context(nc.sbuf_tensor("onesL_f", [1, 128], F32))
    negb12 = top.enter_context(nc.sbuf_tensor("negb12", [128, 1], F32))
    idb = top.enter_context(nc.sbuf_tensor("idb", [128, 128], FP16))
    idf = top.enter_context(nc.sbuf_tensor("idf", [128, 128], F32))
    w1s_pp = top.enter_context(nc.sbuf_tensor("w1s_pp", [128, NH], F32))
    n1_pp = top.enter_context(nc.sbuf_tensor("n1_pp", [128, NH], F32))
    w2s_pp = top.enter_context(nc.sbuf_tensor("w2s_pp", [128, NE], F32))
    cstack0 = contextlib.ExitStack()
    ctx_bf = cstack0.enter_context(nc.sbuf_tensor("ctx_bf", [128, H * TC], FP16))

    heads_stack = contextlib.ExitStack()
    tabsb = {nm: heads_stack.enter_context(nc.sbuf_tensor("t_" + nm, [D, cols], F32))
             for nm, cols in [("cosq", TC), ("ssq", TC), ("cosk", T), ("ssk", T)]}
    x_bf = heads_stack.enter_context(nc.sbuf_tensor("x_bf", [128, NE * T], FP16))
    xm_bf = heads_stack.enter_context(nc.sbuf_tensor("xm_bf", [128, NE * TC], FP16))
    mask_sb = heads_stack.enter_context(nc.sbuf_tensor("mask_sb", [128, NPAIR * 4 * 256], FP16))


    def _early_out(tag):
        with contextlib.ExitStack() as p0:
            dummy = p0.enter_context(nc.sbuf_tensor("dummy" + tag, [128, 256], F32))
            with nc.Block() as blk:
                @blk.vector
                def _(vec):
                    vec.memset(dummy[:, :], 0.0).then_inc(s_v.h, 1)
                    s_v.v += 1

                @blk.sync
                def _(sync):
                    sync.wait_ge(s_v.h, s_v.v)
                    for fc in range(NE):
                        for tsx in range(4):
                            sync.dma_start(out=bass.AP(out_d, fc * 128 * TC + tsx * 256,
                                                       [[TC, 128], [1, 256]]),
                                           in_=dummy[:, :]).then_inc(s_o.h, 16)
                            s_o.v += 16
                    sync.wait_ge(s_o.h, s_o.v)
        return nc

    with contextlib.ExitStack() as ph:
        tlo = {nm: ph.enter_context(nc.sbuf_tensor("tl_" + nm, [D, cols], F32))
               for nm, cols in [("cosq", TC), ("ssq", TC), ("cosk", T), ("ssk", T)]}
        with nc.Block() as blk:
            @blk.sync
            def _(sync):
                for nm in ("cosq", "ssq", "cosk", "ssk"):
                    sync.dma_start(out=tabsb[nm][:], in_=tabs[nm + "hi"].ap()[:, :]).then_inc(s_in.h, 16)
                    sync.dma_start(out=tlo[nm][:], in_=tabs[nm + "lo"].ap()[:, :]).then_inc(s_in.h, 16)
                    s_in.v += 32
                sync.dma_start(out=mask_sb[:], in_=maskp.ap()[:, :]).then_inc(s_in.h, 16)
                sync.dma_start(out=idb[:], in_=identb.ap()[:, :]).then_inc(s_in.h, 16)
                sync.dma_start(out=idf[:], in_=identf.ap()[:, :]).then_inc(s_in.h, 16)
                s_in.v += 48
                sync.wait_ge(s_in.h, s_in.v)

            @blk.gpsimd
            def _(gp):
                gp.dma_start(out=x_bf[:, :],
                             in_=bass.AP(xT, 0, [[T, 128], [128 * T, NE], [1, T]])).then_inc(s_g.h, 16)
                gp.dma_start(out=xm_bf[:, :],
                             in_=bass.AP(xTm, 0, [[TC, 128], [128 * TC, NE], [1, TC]])).then_inc(s_g.h, 16)
                s_g.v += 32
                gp.wait_ge(s_g.h, s_g.v)

            @blk.vector
            def _(vec):
                vec.memset(ones_f[:], 1.0)
                vec.memset(ones_b[:], 1.0)
                vec.memset(onesL_f[:], 1.0)
                vec.memset(negb12[:], -12.0)
                vec.wait_ge(s_in.h, s_in.v)
                for nm in ("cosq", "ssq", "cosk", "ssk"):
                    vec.tensor_add(tabsb[nm][:], tabsb[nm][:], tlo[nm][:])

    if _PHASES <= 1:
        heads_stack.close()
        return _early_out("p1")
    cosq, ssq, cosk, ssk = (tabsb[n] for n in ("cosq", "ssq", "cosk", "ssk"))

    def rope_evac(vec, ps, dst, dc0, cos_t, ss_t, c0, width):
        vec.tensor_tensor(tmpA[:, 0:width], ps[:, 0:width], cos_t[:, c0:c0 + width], op=ALU.mult)
        vec.tensor_tensor(tmpB[0:64, 0:width], ps[64:128, 0:width], ss_t[0:64, c0:c0 + width], op=ALU.mult)
        vec.tensor_tensor(tmpB[64:128, 0:width], ps[0:64, 0:width], ss_t[64:128, c0:c0 + width], op=ALU.mult)
        return vec.tensor_add(dst[:, dc0:dc0 + width], tmpA[:, 0:width], tmpB[:, 0:width])

    for h in range(H):
        hp = contextlib.ExitStack()
        wq_bf = hp.enter_context(nc.sbuf_tensor(f"wq{h}", [128, NE * 128], FP16))
        wk_bf = hp.enter_context(nc.sbuf_tensor(f"wk{h}", [128, NE * 128], FP16))
        wv_bf = hp.enter_context(nc.sbuf_tensor(f"wv{h}", [128, NE * 128], FP16))
        k_bf = hp.enter_context(nc.sbuf_tensor(f"k{h}", [D, T], FP16))
        q_bf = hp.enter_context(nc.sbuf_tensor(f"q{h}", [D, TC], FP16))
        v_fm = hp.enter_context(nc.sbuf_tensor(f"vf{h}", [D, T], FP16))
        v_tm = hp.enter_context(nc.sbuf_tensor(f"vt{h}", [128, 16 * 128], FP16))
        tmpA = hp.enter_context(nc.sbuf_tensor(f"ta{h}", [128, 512], F32))
        tmpB = hp.enter_context(nc.sbuf_tensor(f"tb{h}", [128, 512], F32))
        pt_bf = hp.enter_context(nc.sbuf_tensor(f"pt{h}", [128, 2 * 256], FP16))
        rcpb = hp.enter_context(nc.sbuf_tensor(f"rc{h}", [128, TC], F32))

        with contextlib.ExitStack() as pp:
            ps_a = pp.enter_context(nc.psum_tensor(f"pa{h}", [128, 512], F32))
            ps_b = pp.enter_context(nc.psum_tensor(f"pb{h}", [128, 512], F32))
            ps_tr = pp.enter_context(nc.psum_tensor(f"ptr{h}", [128, 128], FP16))
            g0, t0, v0 = s_g.v, s_t.v, s_v.v
            with nc.Block() as blk:
                @blk.gpsimd
                def _(gp):
                    for wsb, wdr in ((wq_bf, wqT), (wk_bf, wkT), (wv_bf, wvT)):
                        gp.dma_start(out=wsb[:],
                                     in_=bass.AP(wdr, h * 128, [[E, 128], [128 * E, NE], [1, 128]])
                                     ).then_inc(s_g.h, 16)
                    s_g.v += 48

                @blk.tensor
                def _(tn):
                    tn.wait_ge(s_g.h, g0 + 48)
                    for i in range(10):
                        ps = ps_a if i % 2 == 0 else ps_b
                        if i >= 2:
                            tn.wait_ge(s_v.h, v0 + i - 1)
                        if i < 4:
                            w, rhs_t, cbase, wd = wk_bf, x_bf, i * 512, T
                        elif i < 8:
                            w, rhs_t, cbase, wd = wv_bf, x_bf, (i - 4) * 512, T
                        else:
                            w, rhs_t, cbase, wd = wq_bf, xm_bf, (i - 8) * 512, TC
                        for ec in range(NE):
                            ins = tn.matmul(ps[:, :], w[:, ec * 128:(ec + 1) * 128],
                                            rhs_t[:, ec * wd + cbase: ec * wd + cbase + 512],
                                            start=(ec == 0), stop=(ec == NE - 1))
                        ins.then_inc(s_t.h, 1)
                    s_t.v += 10
                    tn.wait_ge(s_v.h, v0 + 8)
                    for kc in range(16):
                        if kc >= 1:
                            tn.wait_ge(s_v.h, v0 + 10 + kc)
                        tn.transpose(ps_tr[:, :], v_fm[:, kc * 128:(kc + 1) * 128],
                                     idb[:, :]).then_inc(s_t.h, 1)
                    s_t.v += 16

                @blk.vector
                def _(vec):
                    for i in range(10):
                        ps = ps_a if i % 2 == 0 else ps_b
                        vec.wait_ge(s_t.h, t0 + i + 1)
                        if i < 4:
                            ins = rope_evac(vec, ps, k_bf, i * 512, cosk, ssk, i * 512, 512)
                        elif i < 8:
                            ins = vec.tensor_copy(v_fm[:, (i - 4) * 512:(i - 3) * 512], ps[:, :])
                        else:
                            ins = rope_evac(vec, ps, q_bf, (i - 8) * 512, cosq, ssq, (i - 8) * 512, 512)
                        ins.then_inc(s_v.h, 1)
                    for kc in range(16):
                        vec.wait_ge(s_t.h, t0 + 10 + kc + 1)
                        vec.tensor_copy(v_tm[:, kc * 128:(kc + 1) * 128], ps_tr[:, :]).then_inc(s_v.h, 1)
                    s_v.v += 26

        with contextlib.ExitStack() as pp:
            ps_s0 = pp.enter_context(nc.psum_tensor(f"pss0{h}", [128, 256], F32))
            ps_s1 = pp.enter_context(nc.psum_tensor(f"pss1{h}", [128, 256], F32))
            ps_ctx = pp.enter_context(nc.psum_tensor(f"pctx{h}", [128, TC], F32))
            ps_row = pp.enter_context(nc.psum_tensor(f"prow{h}", [1, TC], F32))
            ps_bc = pp.enter_context(nc.psum_tensor(f"pbc{h}", [128, TC], F32))
            t0, v0, s0 = s_t.v, s_v.v, s_s.v
            sched = []
            for j in range(NPAIR):
                nk = 4 * j + 4
                for kc in range(nk):
                    sched.append((j, kc, nk))
            masked_idx = {}
            for n, (j, kc, nk) in enumerate(sched):
                if kc >= nk - 4:
                    masked_idx[n] = len(masked_idx) + 1
            NM = len(masked_idx)

            with nc.Block() as blk:
                @blk.tensor
                def _(tn):
                    for n, (j, kc, nk) in enumerate(sched):
                        sl = (n % 2) * 256
                        pss = ps_s0 if n % 2 == 0 else ps_s1
                        if n >= 2:
                            tn.wait_ge(s_s.h, s0 + n - 1)
                        tn.matmul(pss[:, :], k_bf[:, kc * 128:(kc + 1) * 128],
                                  q_bf[:, j * 256:(j + 1) * 256], start=True, stop=True
                                  ).then_inc(s_t.h, 1)
                        if n in masked_idx:
                            tn.wait_ge(s_v.h, v0 + masked_idx[n])
                        else:
                            tn.wait_ge(s_s.h, s0 + n + 1)
                        tn.matmul(ps_ctx[:, j * 256:(j + 1) * 256],
                                  v_tm[:, kc * 128:(kc + 1) * 128],
                                  pt_bf[:, sl:sl + 256], start=(kc == 0), stop=(kc == nk - 1))
                        tn.matmul(ps_row[0:1, j * 256:(j + 1) * 256], ones_b[:, 0:1],
                                  pt_bf[:, sl:sl + 256], start=(kc == 0), stop=(kc == nk - 1)
                                  ).then_inc(s_t.h, 1)
                    s_t.v += 80
                    tn.wait_ge(s_v.h, v0 + NM + 1)
                    for tb in range(2):
                        ins = tn.matmul(ps_bc[:, tb * 512:(tb + 1) * 512], onesL_f[0:1, :],
                                        rcpb[0:1, tb * 512:(tb + 1) * 512], start=True, stop=True)
                    ins.then_inc(s_t.h, 1)
                    s_t.v += 1

                @blk.scalar
                def _(sc):
                    for n, (j, kc, nk) in enumerate(sched):
                        sl = (n % 2) * 256
                        pss = ps_s0 if n % 2 == 0 else ps_s1
                        sc.wait_ge(s_t.h, t0 + 2 * n + 1)
                        sc.activation(pt_bf[:, sl:sl + 256], pss[:, :], AF.Exp,
                                      bias=negb12[:, 0:1]).then_inc(s_s.h, 1)
                    s_s.v += 80 if False else len(sched)

                @blk.vector
                def _(vec):
                    for n, (j, kc, nk) in enumerate(sched):
                        if n not in masked_idx:
                            continue
                        sl = (n % 2) * 256
                        mo = (j * 4 + (kc - (nk - 4))) * 256
                        vec.wait_ge(s_s.h, s0 + n + 1)
                        vec.tensor_mul(pt_bf[:, sl:sl + 256], pt_bf[:, sl:sl + 256],
                                       mask_sb[:, mo:mo + 256]).then_inc(s_v.h, 1)
                    vec.wait_ge(s_t.h, t0 + 80)
                    vec.reciprocal(rcpb[0:1, :], ps_row[0:1, :]).then_inc(s_v.h, 1)
                    s_v.v += NM + 1
                    vec.wait_ge(s_t.h, t0 + 81)
                    vec.tensor_copy(rcpb[:, 0:512], ps_bc[:, 0:512])
                    vec.tensor_copy(rcpb[:, 512:1024], ps_bc[:, 512:1024])
                    for j in range(NPAIR):
                        vec.tensor_tensor(ctx_bf[:, h * TC + j * 256: h * TC + (j + 1) * 256],
                                          ps_ctx[:, j * 256:(j + 1) * 256],
                                          rcpb[:, j * 256:(j + 1) * 256], op=ALU.mult)
                    vec.engine_nop().then_inc(s_v.h, 1)
                    s_v.v += 1
        if _DEBUG and h == 0:
            with nc.Block() as blk:
                @blk.sync
                def _(sync):
                    sync.dma_start(out=dbg["d_k"].ap()[:, :], in_=k_bf[:]).then_inc(s_o.h, 16)
                    sync.dma_start(out=dbg["d_q"].ap()[:, :], in_=q_bf[:]).then_inc(s_o.h, 16)
                    sync.dma_start(out=dbg["d_vt"].ap()[:, :], in_=v_tm[:]).then_inc(s_o.h, 16)
                    s_o.v += 48
                    sync.wait_ge(s_o.h, s_o.v)
        hp.close()

    if _DEBUG:
        with nc.Block() as blk:
            @blk.sync
            def _(sync):
                sync.dma_start(out=dbg["d_ctx"].ap()[:, :], in_=ctx_bf[:]).then_inc(s_o.h, 16)
                s_o.v += 16
                sync.wait_ge(s_o.h, s_o.v)

    heads_stack.close()

    if _PHASES <= 2:
        return _early_out("p2")
    y1 = cstack0.enter_context(nc.sbuf_tensor("y1", [128, NE * TC], F32))
    mrun = cstack0.enter_context(nc.sbuf_tensor("mrun", [128, TC], F32))
    mrun_mn = cstack0.enter_context(nc.sbuf_tensor("mrun_mn", [128, TC], F32))
    rowsA = cstack0.enter_context(nc.sbuf_tensor("rowsA", [1, TC], F32))
    rowsB = cstack0.enter_context(nc.sbuf_tensor("rowsB", [1, TC], F32))
    Ab = cstack0.enter_context(nc.sbuf_tensor("Ab", [128, TC], F32))
    Bb = cstack0.enter_context(nc.sbuf_tensor("Bb", [128, TC], F32))

    with contextlib.ExitStack() as ph:
        wo_bf = ph.enter_context(nc.sbuf_tensor("wo_bf", [128, NE * E], FP16))
        xres = ph.enter_context(nc.sbuf_tensor("xres", [128, 2 * TC], F32))
        ps_a = ph.enter_context(nc.psum_tensor("pc_a", [128, 512], F32))
        ps_b = ph.enter_context(nc.psum_tensor("pc_b", [128, 512], F32))
        g0, t0, v0, i0 = s_g.v, s_t.v, s_v.v, s_in.v
        p00, p10 = s_p0.v, s_p1.v
        with nc.Block() as blk:
            @blk.gpsimd
            def _(gp):
                for ec in range(NE):
                    gp.dma_start(out=wo_bf[:, ec * E:(ec + 1) * E],
                                 in_=bass.AP(woT, ec * 128 * E, [[E, 128], [1, E]])
                                 ).then_inc(s_g.h, 16)
                s_g.v += 16 * NE

            @blk.sync
            def _(sync):
                for fc in range(NE):
                    sp = s_p0 if fc % 2 == 0 else s_p1
                    sync.dma_start(out=xres[:, (fc % 2) * TC:(fc % 2) * TC + TC],
                                   in_=bass.AP(xTm, fc * 128 * TC, [[TC, 128], [1, TC]])
                                   ).then_inc(sp.h, 16)
                    sp.v += 16
                    if fc < NE - 1:
                        sync.wait_ge(s_v.h, v0 + 2 * fc)

            @blk.tensor
            def _(tn):
                tn.wait_ge(s_g.h, g0 + 16 * NE)
                for fc in range(NE):
                    for th in range(2):
                        i = 2 * fc + th
                        ps = ps_a if i % 2 == 0 else ps_b
                        if i >= 2:
                            tn.wait_ge(s_v.h, v0 + i - 1)
                        for ec in range(NE):
                            ins = tn.matmul(ps[:, :], wo_bf[:, ec * E + fc * 128: ec * E + fc * 128 + 128],
                                            ctx_bf[:, ec * TC + th * 512: ec * TC + th * 512 + 512],
                                            start=(ec == 0), stop=(ec == NE - 1))
                        ins.then_inc(s_t.h, 1)
                s_t.v += 32

            @blk.vector
            def _(vec):
                for fc in range(NE):
                    for th in range(2):
                        i = 2 * fc + th
                        ps = ps_a if i % 2 == 0 else ps_b
                        vec.wait_ge(s_t.h, t0 + i + 1)
                        sp = s_p0 if fc % 2 == 0 else s_p1
                        vec.wait_ge(sp.h, (p00 if fc % 2 == 0 else p10) + 16 * (fc // 2 + 1))
                        vec.tensor_add(y1[:, fc * TC + th * 512: fc * TC + th * 512 + 512],
                                       ps[:, :], xres[:, (fc % 2) * TC + th * 512:(fc % 2) * TC + th * 512 + 512]
                                       ).then_inc(s_v.h, 1)
                s_v.v += 32

    if _PHASES <= 3:
        cstack0.close()
        return _early_out("p3")
    with contextlib.ExitStack() as ph:
        sqt = ph.enter_context(nc.sbuf_tensor("sqt", [128, 2 * TC], F32))
        ps_sum = ph.enter_context(nc.psum_tensor("pc_sum", [1, TC], F32))
        ps_sq = ph.enter_context(nc.psum_tensor("pc_sq", [1, TC], F32))
        ps_bc = ph.enter_context(nc.psum_tensor("pc_bc", [128, TC], F32))
        t0, v0, s0 = s_t.v, s_v.v, s_s.v
        with nc.Block() as blk:
            @blk.vector
            def _(vec):
                for fc in range(NE):
                    if fc >= 2:
                        vec.wait_ge(s_t.h, t0 + fc - 1)
                    vec.tensor_mul(sqt[:, (fc % 2) * TC:(fc % 2) * TC + TC],
                                   y1[:, fc * TC:(fc + 1) * TC],
                                   y1[:, fc * TC:(fc + 1) * TC]).then_inc(s_v.h, 1)
                s_v.v += NE
                vec.wait_ge(s_t.h, t0 + NE)
                vec.tensor_scalar(rowsA[0:1, :], ps_sum[0:1, :], 1.0 / E, None, op0=ALU.mult)
                vec.tensor_scalar(rowsB[0:1, :], ps_sq[0:1, :], 1.0 / E, None, op0=ALU.mult)
                vec.tensor_mul(Ab[0:1, 0:TC], rowsA[0:1, :], rowsA[0:1, :])
                vec.tensor_sub(rowsB[0:1, :], rowsB[0:1, :], Ab[0:1, 0:TC])
                vec.tensor_scalar(rowsB[0:1, :], rowsB[0:1, :], 1e-5, None, op0=ALU.add
                                  ).then_inc(s_v.h, 1)
                s_v.v += 1
                vec.wait_ge(s_s.h, s0 + 1)
                vec.reciprocal(Ab[0:1, 0:TC], Bb[0:1, 0:TC])
                vec.tensor_mul(Bb[0:1, 0:TC], Ab[0:1, 0:TC], Ab[0:1, 0:TC])
                vec.tensor_mul(Bb[0:1, 0:TC], Bb[0:1, 0:TC], rowsB[0:1, :])
                vec.tensor_scalar(Bb[0:1, 0:TC], Bb[0:1, 0:TC], -0.5, 1.5, op0=ALU.mult, op1=ALU.add)
                vec.tensor_mul(rowsB[0:1, :], Ab[0:1, 0:TC], Bb[0:1, 0:TC])
                vec.tensor_mul(rowsA[0:1, :], rowsA[0:1, :], rowsB[0:1, :])
                vec.tensor_scalar(rowsA[0:1, :], rowsA[0:1, :], -1.0, None, op0=ALU.mult
                                  ).then_inc(s_v.h, 1)
                s_v.v += 1
                vec.wait_ge(s_t.h, t0 + NE + 1)
                vec.tensor_copy(Ab[:, :], ps_bc[:, :]).then_inc(s_v.h, 1)
                s_v.v += 1
                vec.wait_ge(s_t.h, t0 + NE + 2)
                vec.tensor_copy(Bb[:, :], ps_bc[:, :]).then_inc(s_v.h, 1)
                s_v.v += 1
                vec.memset(mrun[:, :], 0.0)
                vec.memset(mrun_mn[:, :], 0.0)
                for fc in range(NE):
                    vec.tensor_mul(y1[:, fc * TC:(fc + 1) * TC], y1[:, fc * TC:(fc + 1) * TC], Ab[:, :])
                    vec.tensor_add(y1[:, fc * TC:(fc + 1) * TC], y1[:, fc * TC:(fc + 1) * TC], Bb[:, :])
                    vec.tensor_max(mrun[:, :], mrun[:, :], y1[:, fc * TC:(fc + 1) * TC])
                    vec.tensor_tensor(mrun_mn[:, :], mrun_mn[:, :], y1[:, fc * TC:(fc + 1) * TC],
                                      op=ALU.min)
                    vec.engine_nop().then_inc(s_v.h, 1)
                vec.tensor_scalar(mrun_mn[:, :], mrun_mn[:, :], -1.0, None, op0=ALU.mult)
                vec.tensor_max(mrun[:, :], mrun[:, :], mrun_mn[:, :])
                s_v.v += NE

            @blk.scalar
            def _(sc):
                sc.wait_ge(s_v.h, v0 + NE + 1)
                sc.activation(Bb[0:1, 0:TC], rowsB[0:1, :], AF.Sqrt).then_inc(s_s.h, 1)
                s_s.v += 1

            @blk.tensor
            def _(tn):
                for fc in range(NE):
                    tn.wait_ge(s_v.h, v0 + fc + 1)
                    for th in range(2):
                        tn.matmul(ps_sum[0:1, th * 512:(th + 1) * 512], ones_f[:, 0:1],
                                  y1[:, fc * TC + th * 512: fc * TC + th * 512 + 512],
                                  start=(fc == 0), stop=(fc == NE - 1))
                        ins = tn.matmul(ps_sq[0:1, th * 512:(th + 1) * 512], ones_f[:, 0:1],
                                        sqt[:, (fc % 2) * TC + th * 512:(fc % 2) * TC + th * 512 + 512],
                                        start=(fc == 0), stop=(fc == NE - 1))
                    ins.then_inc(s_t.h, 1)
                s_t.v += NE
                tn.wait_ge(s_v.h, v0 + NE + 2)
                for tb in range(2):
                    ins = tn.matmul(ps_bc[:, tb * 512:(tb + 1) * 512], onesL_f[0:1, :],
                                    rowsB[0:1, tb * 512:(tb + 1) * 512], start=True, stop=True)
                ins.then_inc(s_t.h, 1)
                s_t.v += 1
                tn.wait_ge(s_v.h, v0 + NE + 3)
                for tb in range(2):
                    ins = tn.matmul(ps_bc[:, tb * 512:(tb + 1) * 512], onesL_f[0:1, :],
                                    rowsA[0:1, tb * 512:(tb + 1) * 512], start=True, stop=True)
                ins.then_inc(s_t.h, 1)
                s_t.v += 1

            @blk.gpsimd
            def _(gp):
                for fc in range(NE):
                    gp.wait_ge(s_v.h, v0 + NE + 4 + fc + 1)
                    gp.dma_start(out=bass.AP(ynorm_hbm, fc * 128 * TC, [[TC, 128], [1, TC]]),
                                 in_=y1[:, fc * TC:(fc + 1) * TC]).then_inc(s_o.h, 16)
                    s_o.v += 16
                gp.wait_ge(s_o.h, s_o.v)

    if _PHASES <= 4:
        cstack0.close()
        return _early_out("p4")
    with contextlib.ExitStack() as ph:
        xq1 = ph.enter_context(nc.sbuf_tensor("xq1c", [128, NE * TC], BF16))
        R1b = ph.enter_context(nc.sbuf_tensor("R1bc", [128, TC], F32))
        mca = ph.enter_context(nc.sbuf_tensor("mca", [128, 8], F32))
        v_t = ph.enter_context(nc.sbuf_tensor("qv", [128, TC], F32))
        ci_t = ph.enter_context(nc.sbuf_tensor("qc", [128, TC], I32))
        r_t = ph.enter_context(nc.sbuf_tensor("qr", [128, TC], F32))
        an_t = ph.enter_context(nc.sbuf_tensor("qa", [128, TC], F32))
        S1b = ph.enter_context(nc.sbuf_tensor("S1b", [128, TC], F32))
        ps_tr = ph.enter_context(nc.psum_tensor("pq_tr", [128, 128], F32))
        ps_tc0 = ph.enter_context(nc.psum_tensor("pq_tc0", [1, 128], F32))
        ps_tc1 = ph.enter_context(nc.psum_tensor("pq_tc1", [1, 128], F32))
        ps_bc = ph.enter_context(nc.psum_tensor("pq_bc", [128, TC], F32))
        t0, v0, s0 = s_t.v, s_v.v, s_s.v
        with nc.Block() as blk:
            @blk.tensor
            def _(tn):
                for jt in range(8):
                    if jt >= 1:
                        tn.wait_ge(s_v.h, v0 + jt)
                    tn.transpose(ps_tr[:, :], mrun[:, jt * 128:(jt + 1) * 128],
                                 idf[:, :]).then_inc(s_t.h, 1)
                s_t.v += 8
                for i in range(8):
                    tn.wait_ge(s_v.h, v0 + (i + 1 if i < 2 else 8 + i))
                    tn.transpose((ps_tc0 if i % 2 == 0 else ps_tc1)[0:1, :],
                                 mca[:, i:i + 1], idf[:, :]).then_inc(s_t.h, 1)
                s_t.v += 8
                tn.wait_ge(s_v.h, v0 + 17)
                for tb in range(2):
                    ins = tn.matmul(ps_bc[:, tb * 512:(tb + 1) * 512], onesL_f[0:1, :],
                                    rowsA[0:1, tb * 512:(tb + 1) * 512], start=True, stop=True)
                ins.then_inc(s_t.h, 1)
                s_t.v += 1
                tn.wait_ge(s_v.h, v0 + 18)
                for tb in range(2):
                    ins = tn.matmul(ps_bc[:, tb * 512:(tb + 1) * 512], onesL_f[0:1, :],
                                    rowsB[0:1, tb * 512:(tb + 1) * 512], start=True, stop=True)
                ins.then_inc(s_t.h, 1)
                s_t.v += 1

            @blk.vector
            def _(vec):
                for jt in range(8):
                    vec.wait_ge(s_t.h, t0 + jt + 1)
                    vec.reduce_max(mca[:, jt:jt + 1], ps_tr[:, :], axis=AX.X).then_inc(s_v.h, 1)
                for i in range(8):
                    vec.wait_ge(s_t.h, t0 + 8 + i + 1)
                    vec.tensor_copy(rowsA[0:1, i * 128:(i + 1) * 128],
                                    (ps_tc0 if i % 2 == 0 else ps_tc1)[0:1, :]).then_inc(s_v.h, 1)
                s_v.v += 16
                vec.tensor_scalar(rowsB[0:1, :], rowsA[0:1, :], 1e-5, 1.0 / 127.0,
                                  op0=ALU.max, op1=ALU.mult)
                vec.reciprocal(rowsA[0:1, :], rowsB[0:1, :])
                vec.engine_nop().then_inc(s_v.h, 1)
                s_v.v += 1
                vec.wait_ge(s_t.h, t0 + 17)
                vec.tensor_copy(S1b[:, :], ps_bc[:, :]).then_inc(s_v.h, 1)
                s_v.v += 1
                vec.wait_ge(s_t.h, t0 + 18)
                vec.tensor_copy(R1b[:, :], ps_bc[:, :]).then_inc(s_v.h, 1)
                s_v.v += 1
                for fc in range(NE):
                    vec.tensor_mul(v_t[:, :], y1[:, fc * TC:(fc + 1) * TC], S1b[:, :])
                    vec.engine_nop().then_inc(s_v.h, 1)
                    vec.wait_ge(s_s.h, s0 + fc + 1)
                    vec.tensor_sub(r_t[:, :], v_t[:, :], ci_t[:, :])
                    vec.tensor_scalar(an_t[:, :], r_t[:, :], 0.5, -1.0, op0=ALU.is_gt, op1=ALU.mult)
                    vec.scalar_tensor_tensor(r_t[:, :], r_t[:, :], -0.5, an_t[:, :],
                                             op0=ALU.is_lt, op1=ALU.add)
                    vec.tensor_sub(xq1[:, fc * TC:(fc + 1) * TC], ci_t[:, :], r_t[:, :])
                    vec.engine_nop().then_inc(s_v.h, 1)
                s_v.v += 2 * NE

            @blk.scalar
            def _(sc):
                for fc in range(NE):
                    sc.wait_ge(s_v.h, v0 + 19 + 2 * fc + 1)
                    sc.activation(ci_t[:, :], v_t[:, :], AF.Identity).then_inc(s_s.h, 1)
                s_s.v += NE

            @blk.sync
            def _(sync):
                for fc in range(NE):
                    sync.wait_ge(s_v.h, v0 + 19 + 2 * fc + 2)
                    sync.dma_start(out=bass.AP(xq1_hbm, fc * 128 * TC, [[TC, 128], [1, TC]]),
                                   in_=xq1[:, fc * TC:(fc + 1) * TC]).then_inc(s_o.h, 16)
                    s_o.v += 16
                sync.wait_ge(s_v.h, v0 + 19)
                sync.dma_start(out=r1_hbm.ap()[:, :], in_=R1b[:, :]).then_inc(s_o.h, 16)
                s_o.v += 16
                if _DEBUG:
                    sync.dma_start(out=dbg["d_y1"].ap()[:, :], in_=y1[:, :]).then_inc(s_o.h, 16)
                    sync.dma_start(out=dbg["d_xq1"].ap()[:, :], in_=xq1[:, :]).then_inc(s_o.h, 16)
                    sync.dma_start(out=dbg["d_s1"].ap()[:, :], in_=S1b[:, :]).then_inc(s_o.h, 16)
                    sync.dma_start(out=dbg["d_rows"].ap()[0:1, 0:TC], in_=rowsA[0:1, :]).then_inc(s_o.h, 16)
                    sync.dma_start(out=dbg["d_rows"].ap()[0:1, TC:2 * TC], in_=rowsB[0:1, :]).then_inc(s_o.h, 16)
                    sync.dma_start(out=dbg["d_mrun"].ap()[:, :], in_=mrun[:, :]).then_inc(s_o.h, 16)
                    s_o.v += 96
                sync.wait_ge(s_o.h, s_o.v)

    cstack0.close()

    if _PHASES <= 5:
        return _early_out("p5")

    with contextlib.ExitStack() as ph:
        stg = ph.enter_context(nc.sbuf_tensor("d0stg", [128, 2 * 8192], F32))
        absb = ph.enter_context(nc.sbuf_tensor("d0abs", [128, 8192], FP16))
        s1row = ph.enter_context(nc.sbuf_tensor("s1row", [1, HID], F32))
        n1row = ph.enter_context(nc.sbuf_tensor("n1row", [1, HID], F32))
        ps_r = ph.enter_context(nc.psum_tensor("d0pr", [1, 512], F32))
        g0 = s_in.v
        p00, p10 = s_p0.v, s_p1.v
        t0, v0, s0 = s_t.v, s_v.v, s_s.v
        with nc.Block() as blk:
            @blk.sync
            def _(sync):
                for hb in range(16):
                    if hb >= 2:
                        sync.wait_ge(s_s.h, s0 + hb - 1)
                    sp = s_p0 if hb % 2 == 0 else s_p1
                    sync.dma_start(out=stg[:, (hb % 2) * 8192:(hb % 2) * 8192 + 8192],
                                   in_=bass.AP(w1T, hb * 512, [[HID, 128], [128 * HID, NE], [1, 512]])
                                   ).then_inc(sp.h, 16)
                    sp.v += 16

            @blk.scalar
            def _(sc):
                for hb in range(16):
                    sp0 = p00 if hb % 2 == 0 else p10
                    sc.wait_ge((s_p0 if hb % 2 == 0 else s_p1).h, sp0 + 16 * (hb // 2 + 1))
                    if hb >= 1:
                        sc.wait_ge(s_t.h, t0 + hb)
                    sc.activation(absb[:, :], stg[:, (hb % 2) * 8192:(hb % 2) * 8192 + 8192],
                                  AF.Abs).then_inc(s_s.h, 1)
                s_s.v += 16

            @blk.tensor
            def _(tn):
                for hb in range(16):
                    tn.wait_ge(s_s.h, s0 + hb + 1)
                    if hb >= 1:
                        tn.wait_ge(s_v.h, v0 + hb)
                    for ec in range(NE):
                        ins = tn.matmul(ps_r[0:1, :], ones_b[:, 0:1],
                                        absb[:, ec * 512:(ec + 1) * 512],
                                        start=(ec == 0), stop=(ec == NE - 1))
                    ins.then_inc(s_t.h, 1)
                s_t.v += 16

            @blk.vector
            def _(vec):
                for hb in range(16):
                    vec.wait_ge(s_t.h, t0 + hb + 1)
                    vec.tensor_scalar(s1row[0:1, hb * 512:(hb + 1) * 512], ps_r[0:1, :],
                                      1.0 / E, 1e-5, op0=ALU.mult, op1=ALU.max).then_inc(s_v.h, 1)
                s_v.v += 16
                vec.reciprocal(n1row[0:1, :], s1row[0:1, :])
                vec.tensor_scalar(n1row[0:1, :], n1row[0:1, :], -2.0, None, op0=ALU.mult)

        with contextlib.ExitStack() as p2:
            ps_tp = p2.enter_context(nc.psum_tensor("d0tp", [128, 1], F32))
            t0, v0 = s_t.v, s_v.v
            with nc.Block() as blk:
                @blk.tensor
                def _(tn):
                    for i in range(2 * NH):
                        row = s1row if i < NH else n1row
                        off = i % NH
                        if i >= 1:
                            tn.wait_ge(s_v.h, v0 + i)
                        tn.transpose(ps_tp[:, :], row[0:1, off * 128:(off + 1) * 128],
                                     idf[0:1, 0:1]).then_inc(s_t.h, 1)
                    s_t.v += 2 * NH

                @blk.vector
                def _(vec):
                    for i in range(2 * NH):
                        dst = w1s_pp if i < NH else n1_pp
                        off = i % NH
                        vec.wait_ge(s_t.h, t0 + i + 1)
                        vec.tensor_copy(dst[:, off:off + 1], ps_tp[:, :]).then_inc(s_v.h, 1)
                    s_v.v += 2 * NH

    with contextlib.ExitStack() as ph2:
        stg = ph2.enter_context(nc.sbuf_tensor("d0s2", [128, 2 * 2048], F32))
        absb = ph2.enter_context(nc.sbuf_tensor("d0a2", [128, 2048], F32))
        s2row = ph2.enter_context(nc.sbuf_tensor("s2row", [1, E], F32))
        n2row = ph2.enter_context(nc.sbuf_tensor("n2row", [1, E], F32))
        ps_r = ph2.enter_context(nc.psum_tensor("d0p2", [1, 2048], F32))
        g0 = s_in.v
        p00, p10 = s_p0.v, s_p1.v
        t0, v0, s0 = s_t.v, s_v.v, s_s.v
        with nc.Block() as blk:
            @blk.sync
            def _(sync):
                for hc in range(NH):
                    if hc >= 2:
                        sync.wait_ge(s_s.h, s0 + hc - 1)
                    sp = s_p0 if hc % 2 == 0 else s_p1
                    sync.dma_start(out=stg[:, (hc % 2) * 2048:(hc % 2) * 2048 + 2048],
                                   in_=bass.AP(w2T, hc * 128 * E, [[E, 128], [1, E]])
                                   ).then_inc(sp.h, 16)
                    sp.v += 16

            @blk.scalar
            def _(sc):
                for hc in range(NH):
                    sp0 = p00 if hc % 2 == 0 else p10
                    sc.wait_ge((s_p0 if hc % 2 == 0 else s_p1).h, sp0 + 16 * (hc // 2 + 1))
                    if hc >= 1:
                        sc.wait_ge(s_t.h, t0 + hc)
                    sc.activation(absb[:, :], stg[:, (hc % 2) * 2048:(hc % 2) * 2048 + 2048],
                                  AF.Abs).then_inc(s_s.h, 1)
                s_s.v += NH

            @blk.tensor
            def _(tn):
                for hc in range(NH):
                    tn.wait_ge(s_s.h, s0 + hc + 1)
                    for fg in range(4):
                        ins = tn.matmul(ps_r[0:1, fg * 512:(fg + 1) * 512], ones_f[:, 0:1],
                                        absb[:, fg * 512:(fg + 1) * 512],
                                        start=(hc == 0), stop=(hc == NH - 1))
                    ins.then_inc(s_t.h, 1)
                s_t.v += NH

            @blk.vector
            def _(vec):
                vec.wait_ge(s_t.h, t0 + NH)
                vec.tensor_scalar(s2row[0:1, :], ps_r[0:1, :], 1.0 / HID, 1e-5,
                                  op0=ALU.mult, op1=ALU.max)
                vec.reciprocal(n2row[0:1, :], s2row[0:1, :])
                vec.tensor_scalar(n2row[0:1, :], n2row[0:1, :], -2.0, None, op0=ALU.mult)

        with contextlib.ExitStack() as p2:
            ps_tp = p2.enter_context(nc.psum_tensor("dspp", [128, 1], F32))
            t0, v0 = s_t.v, s_v.v
            with nc.Block() as blk:
                @blk.tensor
                def _(tn):
                    for i in range(NE):
                        if i >= 1:
                            tn.wait_ge(s_v.h, v0 + i)
                        tn.transpose(ps_tp[:, :], s2row[0:1, i * 128:(i + 1) * 128],
                                     idf[0:1, 0:1]).then_inc(s_t.h, 1)
                    s_t.v += NE

                @blk.vector
                def _(vec):
                    for i in range(NE):
                        vec.wait_ge(s_t.h, t0 + i + 1)
                        vec.tensor_copy(w2s_pp[:, i:i + 1], ps_tp[:, :]).then_inc(s_v.h, 1)
                    s_v.v += NE

        ph = ph2
        stg = ph.enter_context(nc.sbuf_tensor("d1s", [128, 2 * 2048], F32))
        ub = ph.enter_context(nc.sbuf_tensor("d1u", [128, 2048], F32))
        bn = ph.enter_context(nc.sbuf_tensor("d1b", [128, 2048], BF16))
        n2b = ph.enter_context(nc.sbuf_tensor("n2b", [128, 2048], F32))
        w2q_t = ph.enter_context(nc.sbuf_tensor("w2q_t", [128, 2 * 2048], FP8))
        ps_bc = ph.enter_context(nc.psum_tensor("d1p", [128, 2048], F32))
        g0 = s_in.v
        t0, v0, o0 = s_t.v, s_v.v, s_o.v
        p00, p10, q00, q10 = s_p0.v, s_p1.v, s_q0.v, s_q1.v
        with nc.Block() as blk:
            @blk.tensor
            def _(tn):
                for fg in range(4):
                    ins = tn.matmul(ps_bc[:, fg * 512:(fg + 1) * 512], onesL_f[0:1, :],
                                    n2row[0:1, fg * 512:(fg + 1) * 512], start=True, stop=True)
                ins.then_inc(s_t.h, 1)
                s_t.v += 1

            @blk.sync
            def _(sync):
                for hc in range(NH):
                    if hc >= 2:
                        sync.wait_ge(s_v.h, v0 + hc - 1)
                    sp = s_p0 if hc % 2 == 0 else s_p1
                    sync.dma_start(out=stg[:, (hc % 2) * 2048:(hc % 2) * 2048 + 2048],
                                   in_=bass.AP(w2T, hc * 128 * E, [[E, 128], [1, E]])
                                   ).then_inc(sp.h, 16)
                    sp.v += 16

            @blk.vector
            def _(vec):
                vec.wait_ge(s_t.h, t0 + 1)
                vec.tensor_copy(n2b[:, :], ps_bc[:, :])
                for hc in range(NH):
                    vec.wait_ge((s_p0 if hc % 2 == 0 else s_p1).h,
                                (p00 if hc % 2 == 0 else p10) + 16 * (hc // 2 + 1))
                    if hc >= 2:
                        vec.wait_ge((s_q0 if hc % 2 == 0 else s_q1).h,
                                    (q00 if hc % 2 == 0 else q10) + 16 * ((hc - 2) // 2 + 1))
                    sl = (hc % 2) * 2048
                    vec.tensor_mul(ub[:, :], stg[:, sl:sl + 2048], n2b[:, :])
                    vec.tensor_scalar(bn[:, :], ub[:, :], 1.0, -1.0, op0=ALU.is_gt, op1=ALU.mult)
                    vec.scalar_tensor_tensor(w2q_t[:, sl:sl + 2048], ub[:, :], -1.0, bn[:, :],
                                             op0=ALU.is_lt, op1=ALU.add).then_inc(s_v.h, 1)
                s_v.v += NH

            @blk.gpsimd
            def _(gp):
                for hc in range(NH):
                    gp.wait_ge(s_v.h, v0 + hc + 1)
                    sl = (hc % 2) * 2048
                    sq = s_q0 if hc % 2 == 0 else s_q1
                    gp.dma_start(out=bass.AP(w2q_hbm, hc * 128 * E, [[E, 128], [1, E]]),
                                 in_=w2q_t[:, sl:sl + 2048]).then_inc(sq.h, 16)
                    sq.v += 16
                gp.wait_ge(s_q0.h, s_q0.v)
                gp.wait_ge(s_q1.h, s_q1.v)

    if _PHASES <= 6:
        return _early_out("p6")
    e_stack = contextlib.ExitStack()
    g_sb = e_stack.enter_context(nc.sbuf_tensor("g_sb", [128, NH * TC], FP16))
    mg = e_stack.enter_context(nc.sbuf_tensor("mg", [128, TC], F32))
    mg_mn = e_stack.enter_context(nc.sbuf_tensor("mg_mn", [128, TC], F32))

    with contextlib.ExitStack() as ph:
        stg = ph.enter_context(nc.sbuf_tensor("d2s", [128, 2 * 2048], F32))
        ub = ph.enter_context(nc.sbuf_tensor("d2u", [128, 1024], F32))
        bn = ph.enter_context(nc.sbuf_tensor("d2b", [128, 1024], BF16))
        n1bt = ph.enter_context(nc.sbuf_tensor("n1bt", [128, 2 * 128], F32))
        w1q_t = ph.enter_context(nc.sbuf_tensor("w1q_t", [128, 2 * 2048], FP8))
        t0t = ph.enter_context(nc.sbuf_tensor("d2t0", [128, 4 * 512], FP16))
        n1r_sb = ph.enter_context(nc.sbuf_tensor("n1r_sb", [1, 2 * 128], F32))
        xq1 = ph.enter_context(nc.sbuf_tensor("xq1d", [128, NE * TC], BF16))
        R1b = ph.enter_context(nc.sbuf_tensor("R1bd", [128, TC], F32))
        ps_m = ph.enter_context(nc.psum_tensor("d2pm", [128, 2048], F32))
        ps_bc0 = ph.enter_context(nc.psum_tensor("d2pb0", [128, 128], F32))
        ps_bc1 = ph.enter_context(nc.psum_tensor("d2pb1", [128, 128], F32))
        ps_n10 = ph.enter_context(nc.psum_tensor("d2pn0", [1, 128], F32))
        ps_n11 = ph.enter_context(nc.psum_tensor("d2pn1", [1, 128], F32))

        T0, V0, S0 = s_t.v, s_v.v, s_s.v
        P00, P10 = s_p0.v, s_p1.v
        I0 = s_in.v
        with nc.Block() as blk:
            @blk.sync
            def _(sync):
                sync.dma_start(out=xq1[:, :],
                               in_=bass.AP(xq1_hbm, 0, [[TC, 128], [128 * TC, NE], [1, TC]])
                               ).then_inc(s_in.h, 16)
                sync.dma_start(out=R1b[:, :], in_=r1_hbm.ap()[:, :]).then_inc(s_in.h, 16)
                s_in.v += 32
                for hc in range(NH):
                    if hc >= 2:
                        sync.wait_ge(s_v.h, V0 + 7 * (hc - 2) + 3)
                    sp = s_p0 if hc % 2 == 0 else s_p1
                    sync.dma_start(out=stg[:, (hc % 2) * 2048:(hc % 2) * 2048 + 2048],
                                   in_=bass.AP(w1T, hc * 128, [[HID, 128], [128 * HID, NE], [1, 128]])
                                   ).then_inc(sp.h, 16)
                    sp.v += 16

            @blk.tensor
            def _(tn):
                tn.wait_ge(s_in.h, I0 + 32)
                for hc in range(NH):
                    psn = ps_n10 if hc % 2 == 0 else ps_n11
                    psb = ps_bc0 if hc % 2 == 0 else ps_bc1
                    if hc >= 2:
                        tn.wait_ge(s_v.h, V0 + 7 * (hc - 2) + 1)
                    tn.transpose(psn[0:1, :], n1_pp[:, hc:hc + 1], idf[:, :]).then_inc(s_t.h, 1)
                    tn.wait_ge(s_v.h, V0 + 7 * hc + 1)
                    tn.matmul(psb[:, :], onesL_f[0:1, :],
                              n1r_sb[0:1, (hc % 2) * 128:(hc % 2) * 128 + 128],
                              start=True, stop=True).then_inc(s_t.h, 1)
                    tn.wait_ge(s_v.h, V0 + 7 * hc + 3)
                    for th in range(2):
                        sl = (hc % 2) * 1024
                        for ec in range(NE):
                            ins = tn.matmul(ps_m[:, sl + th * 512: sl + th * 512 + 512],
                                            w1q_t[:, (hc % 2) * 2048 + ec * 128:(hc % 2) * 2048 + ec * 128 + 128],
                                            xq1[:, ec * TC + th * 512: ec * TC + th * 512 + 512],
                                            start=(ec == 0), stop=(ec == NE - 1))
                        ins.then_inc(s_t.h, 1)
                s_t.v += 4 * NH

            @blk.vector
            def _(vec):
                vec.memset(mg[:, :], 0.0)
                vec.memset(mg_mn[:, :], 0.0)
                for hc in range(NH):
                    psn = ps_n10 if hc % 2 == 0 else ps_n11
                    psb = ps_bc0 if hc % 2 == 0 else ps_bc1
                    nsl = (hc % 2) * 128
                    vec.wait_ge(s_t.h, T0 + 4 * hc + 1)
                    vec.tensor_copy(n1r_sb[0:1, nsl:nsl + 128], psn[0:1, :]).then_inc(s_v.h, 1)
                    vec.wait_ge(s_t.h, T0 + 4 * hc + 2)
                    vec.tensor_copy(n1bt[:, nsl:nsl + 128], psb[:, :]).then_inc(s_v.h, 1)
                    sp = s_p0 if hc % 2 == 0 else s_p1
                    vec.wait_ge(sp.h, (P00 if hc % 2 == 0 else P10) + 16 * (hc // 2 + 1))
                    sl = (hc % 2) * 2048
                    for hh in range(2):
                        vec.tensor_tensor(ub[:, :], stg[:, sl + hh * 1024: sl + hh * 1024 + 1024],
                                          bass.AP(n1bt, nsl, [[256, 128], [0, NE // 2], [1, 128]]),
                                          op=ALU.mult)
                        vec.tensor_scalar(bn[:, :], ub[:, :], 1.0, -1.0, op0=ALU.is_gt, op1=ALU.mult)
                        vec.scalar_tensor_tensor(w1q_t[:, sl + hh * 1024: sl + hh * 1024 + 1024],
                                                 ub[:, :], -1.0, bn[:, :],
                                                 op0=ALU.is_lt, op1=ALU.add)
                    vec.engine_nop().then_inc(s_v.h, 1)
                    tsl = (hc % 2) * 1024
                    for th in range(2):
                        vec.wait_ge(s_t.h, T0 + 4 * hc + 3 + th)
                        if hc >= 2:
                            vec.wait_ge(s_s.h, S0 + 2 * (hc - 2) + th + 1)
                        vec.tensor_mul(t0t[:, tsl + th * 512: tsl + th * 512 + 512],
                                       ps_m[:, tsl + th * 512: tsl + th * 512 + 512],
                                       R1b[:, th * 512:(th + 1) * 512])
                        vec.engine_nop().then_inc(s_v.h, 1)
                    for th in range(2):
                        vec.wait_ge(s_s.h, S0 + 2 * hc + th + 1)
                        vec.tensor_max(mg[:, th * 512:(th + 1) * 512],
                                       mg[:, th * 512:(th + 1) * 512],
                                       g_sb[:, hc * TC + th * 512: hc * TC + th * 512 + 512])
                        vec.tensor_tensor(mg_mn[:, th * 512:(th + 1) * 512],
                                          mg_mn[:, th * 512:(th + 1) * 512],
                                          g_sb[:, hc * TC + th * 512: hc * TC + th * 512 + 512],
                                          op=ALU.min).then_inc(s_v.h, 1)
                s_v.v += 7 * NH

            @blk.scalar
            def _(sc):
                for hc in range(NH):
                    tsl = (hc % 2) * 1024
                    for th in range(2):
                        sc.wait_ge(s_v.h, V0 + 7 * hc + 4 + th)
                        sc.activation(g_sb[:, hc * TC + th * 512: hc * TC + th * 512 + 512],
                                      t0t[:, tsl + th * 512: tsl + th * 512 + 512],
                                      AF.Gelu_apprx_tanh,
                                      scale=w1s_pp[:, hc:hc + 1]).then_inc(s_s.h, 1)
                s_s.v += 2 * NH

    R2b = e_stack.enter_context(nc.sbuf_tensor("R2b", [128, TC], F32))
    w2s_bc = e_stack.enter_context(nc.sbuf_tensor("w2s_bc", [128, E], F32))
    r2_tm = e_stack.enter_context(nc.sbuf_tensor("r2_tm", [128, 8], F32))
    with contextlib.ExitStack() as ph:
        mca = ph.enter_context(nc.sbuf_tensor("e0ma", [128, 8], F32))
        v_t = ph.enter_context(nc.sbuf_tensor("e0v", [128, TC], F32))
        ci_t = ph.enter_context(nc.sbuf_tensor("e0c", [128, TC], I32))
        r_t = ph.enter_context(nc.sbuf_tensor("e0r", [128, TC], F32))
        an_t = ph.enter_context(nc.sbuf_tensor("e0a", [128, TC], F32))
        S2b = ph.enter_context(nc.sbuf_tensor("S2b", [128, TC], F32))
        rA = ph.enter_context(nc.sbuf_tensor("eq_ra", [1, TC], F32))
        rB = ph.enter_context(nc.sbuf_tensor("eq_rb", [1, TC], F32))
        ps_tr = ph.enter_context(nc.psum_tensor("e0pt", [128, 128], F32))
        ps_tc0 = ph.enter_context(nc.psum_tensor("e0pc0", [1, 128], F32))
        ps_tc1 = ph.enter_context(nc.psum_tensor("e0pc1", [1, 128], F32))
        ps_bc = ph.enter_context(nc.psum_tensor("e0pb", [128, TC], F32))
        t0, v0, s0 = s_t.v, s_v.v, s_s.v
        with nc.Block() as blk:
            @blk.tensor
            def _(tn):
                for jt in range(8):
                    tn.wait_ge(s_v.h, v0 + 1 + max(0, jt))
                    tn.transpose(ps_tr[:, :], mg[:, jt * 128:(jt + 1) * 128],
                                 idf[:, :]).then_inc(s_t.h, 1)
                s_t.v += 8
                for i in range(8):
                    tn.wait_ge(s_v.h, v0 + 1 + (i + 1 if i < 2 else 8 + i))
                    tn.transpose((ps_tc0 if i % 2 == 0 else ps_tc1)[0:1, :],
                                 mca[:, i:i + 1], idf[:, :]).then_inc(s_t.h, 1)
                s_t.v += 8
                tn.wait_ge(s_v.h, v0 + 18)
                for tb in range(2):
                    ins = tn.matmul(ps_bc[:, tb * 512:(tb + 1) * 512], onesL_f[0:1, :],
                                    rA[0:1, tb * 512:(tb + 1) * 512], start=True, stop=True)
                ins.then_inc(s_t.h, 1)
                s_t.v += 1
                tn.wait_ge(s_v.h, v0 + 19)
                for tb in range(2):
                    ins = tn.matmul(ps_bc[:, tb * 512:(tb + 1) * 512], onesL_f[0:1, :],
                                    rB[0:1, tb * 512:(tb + 1) * 512], start=True, stop=True)
                ins.then_inc(s_t.h, 1)
                s_t.v += 1

            @blk.vector
            def _(vec):
                vec.tensor_scalar(mg_mn[:, :], mg_mn[:, :], -1.0, None, op0=ALU.mult)
                vec.tensor_max(mg[:, :], mg[:, :], mg_mn[:, :]).then_inc(s_v.h, 1)
                s_v.v += 1
                for jt in range(8):
                    vec.wait_ge(s_t.h, t0 + jt + 1)
                    vec.reduce_max(mca[:, jt:jt + 1], ps_tr[:, :], axis=AX.X).then_inc(s_v.h, 1)
                for i in range(8):
                    vec.wait_ge(s_t.h, t0 + 8 + i + 1)
                    vec.tensor_copy(rA[0:1, i * 128:(i + 1) * 128],
                                    (ps_tc0 if i % 2 == 0 else ps_tc1)[0:1, :]).then_inc(s_v.h, 1)
                s_v.v += 16
                vec.tensor_scalar(rB[0:1, :], rA[0:1, :], 1e-5, 1.0 / 127.0,
                                  op0=ALU.max, op1=ALU.mult)
                vec.reciprocal(rA[0:1, :], rB[0:1, :])
                vec.engine_nop().then_inc(s_v.h, 1)
                s_v.v += 1
                vec.wait_ge(s_t.h, t0 + 17)
                vec.tensor_copy(S2b[:, :], ps_bc[:, :]).then_inc(s_v.h, 1)
                s_v.v += 1
                vec.wait_ge(s_t.h, t0 + 18)
                vec.tensor_copy(R2b[:, :], ps_bc[:, :]).then_inc(s_v.h, 1)
                s_v.v += 1
                for hc in range(NH):
                    vec.tensor_mul(v_t[:, :], g_sb[:, hc * TC:(hc + 1) * TC], S2b[:, :])
                    vec.engine_nop().then_inc(s_v.h, 1)
                    vec.wait_ge(s_s.h, s0 + hc + 1)
                    vec.tensor_sub(r_t[:, :], v_t[:, :], ci_t[:, :])
                    vec.tensor_scalar(an_t[:, :], r_t[:, :], 0.5, -1.0, op0=ALU.is_gt, op1=ALU.mult)
                    vec.scalar_tensor_tensor(r_t[:, :], r_t[:, :], -0.5, an_t[:, :],
                                             op0=ALU.is_lt, op1=ALU.add)
                    vec.tensor_sub(g_sb[:, hc * TC:(hc + 1) * TC], ci_t[:, :], r_t[:, :])
                    vec.engine_nop().then_inc(s_v.h, 1)
                s_v.v += 2 * NH

            @blk.scalar
            def _(sc):
                for hc in range(NH):
                    sc.wait_ge(s_v.h, v0 + 20 + 2 * hc + 1)
                    sc.activation(ci_t[:, :], v_t[:, :], AF.Identity).then_inc(s_s.h, 1)
                s_s.v += NH

    if _PHASES <= 8:
        e_stack.close()
        return _early_out("p8")
    with contextlib.ExitStack() as ph:
        w2row = ph.enter_context(nc.sbuf_tensor("w2row", [1, E], F32))
        r2row = ph.enter_context(nc.sbuf_tensor("r2row", [1, TC], F32))
        ps_t0 = ph.enter_context(nc.psum_tensor("ep_t0", [1, 128], F32))
        ps_t1 = ph.enter_context(nc.psum_tensor("ep_t1", [1, 128], F32))
        ps_c0 = ph.enter_context(nc.psum_tensor("ep_c0", [128, 1], F32))
        ps_c1 = ph.enter_context(nc.psum_tensor("ep_c1", [128, 1], F32))
        ps_b = ph.enter_context(nc.psum_tensor("ep_b", [128, 512], F32))
        t0, v0 = s_t.v, s_v.v
        with nc.Block() as blk:
            @blk.vector
            def _(vec):
                vec.tensor_copy(r2row[0:1, :], R2b[0:1, :]).then_inc(s_v.h, 1)
                for i in range(16):
                    vec.wait_ge(s_t.h, t0 + i + 1)
                    vec.tensor_copy(w2row[0:1, i * 128:(i + 1) * 128],
                                    (ps_t0 if i % 2 == 0 else ps_t1)[0:1, :]).then_inc(s_v.h, 1)
                for i in range(8):
                    vec.wait_ge(s_t.h, t0 + 16 + i + 1)
                    vec.tensor_copy(r2_tm[:, i:i + 1],
                                    (ps_c0 if i % 2 == 0 else ps_c1)[:, :]).then_inc(s_v.h, 1)
                for fg in range(4):
                    vec.wait_ge(s_t.h, t0 + 24 + fg + 1)
                    vec.tensor_copy(w2s_bc[:, fg * 512:(fg + 1) * 512], ps_b[:, :]).then_inc(s_v.h, 1)
                s_v.v += 1 + 16 + 8 + 4

            @blk.tensor
            def _(tn):
                for i in range(16):
                    if i >= 2:
                        tn.wait_ge(s_v.h, v0 + 1 + i - 1)
                    tn.transpose((ps_t0 if i % 2 == 0 else ps_t1)[0:1, :],
                                 w2s_pp[:, i:i + 1], idf[:, :]).then_inc(s_t.h, 1)
                tn.wait_ge(s_v.h, v0 + 1)
                for i in range(8):
                    if i >= 2:
                        tn.wait_ge(s_v.h, v0 + 17 + i - 1)
                    tn.transpose((ps_c0 if i % 2 == 0 else ps_c1)[:, :],
                                 r2row[0:1, i * 128:(i + 1) * 128], idf[0:1, 0:1]).then_inc(s_t.h, 1)
                tn.wait_ge(s_v.h, v0 + 17)
                for fg in range(4):
                    if fg >= 1:
                        tn.wait_ge(s_v.h, v0 + 25 + fg)
                    tn.matmul(ps_b[:, :], onesL_f[0:1, :], w2row[0:1, fg * 512:(fg + 1) * 512],
                              start=True, stop=True).then_inc(s_t.h, 1)
                s_t.v += 16 + 8 + 4

    for tp in range(4):
        with contextlib.ExitStack() as ph:
            w2q_t = ph.enter_context(nc.sbuf_tensor(f"e{tp}w", [128, 2 * 2048], FP8))
            yn_t = ph.enter_context(nc.sbuf_tensor(f"e{tp}y", [128, 2 * 2048], FP16))
            y2 = ph.enter_context(nc.sbuf_tensor(f"e{tp}z", [128, 2 * 2048], F32))
            scr = ph.enter_context(nc.sbuf_tensor(f"e{tp}sc", [128, 2048], FP16))
            rowt = {}
            for nm in ("rm", "ms", "mn", "mq", "m2", "vr", "ve", "sd", "r0", "na", "nb", "nc_", "rs"):
                rowt[nm] = ph.enter_context(nc.sbuf_tensor(f"e{tp}_{nm}", [128, 2], F32))
            ps_o0 = ph.enter_context(nc.psum_tensor(f"e{tp}p0", [128, 2048], F32))
            ps_o1 = ph.enter_context(nc.psum_tensor(f"e{tp}p1", [128, 2048], F32))
            T0, V0, S0 = s_t.v, s_v.v, s_s.v
            P00, P10 = s_p0.v, s_p1.v
            I0, O0 = s_in.v, s_o.v
            with nc.Block() as blk:
                @blk.sync
                def _(sync):
                    for tl in range(2):
                        tix = tp * 2 + tl
                        sync.dma_start_transpose(
                            yn_t[:, tl * 2048:(tl + 1) * 2048],
                            bass.AP(ynorm_hbm, tix * 128, [[TC, E], [1, 128]])
                        ).then_inc(s_in.h, 16)
                        s_in.v += 16
                    for hc in range(NH):
                        if hc >= 2:
                            sync.wait_ge(s_t.h, T0 + 2 * (hc - 2) + 2)
                        sp = s_p0 if hc % 2 == 0 else s_p1
                        sync.dma_start(out=w2q_t[:, (hc % 2) * 2048:(hc % 2) * 2048 + 2048],
                                       in_=bass.AP(w2q_hbm, hc * 128 * E, [[E, 128], [1, E]])
                                       ).then_inc(sp.h, 16)
                        sp.v += 16
                    for tl in range(2):
                        tix = tp * 2 + tl
                        sync.wait_ge(s_v.h, V0 + 6 + tl + 1)
                        sync.dma_start(out=bass.AP(out_d, tix * 128 * E, [[E, 128], [1, E]]),
                                       in_=y2[:, tl * 2048:(tl + 1) * 2048]).then_inc(s_o.h, 16)
                        s_o.v += 16
                    sync.wait_ge(s_o.h, s_o.v)

                @blk.tensor
                def _(tn):
                    for hc in range(NH):
                        sp0 = P00 if hc % 2 == 0 else P10
                        tn.wait_ge((s_p0 if hc % 2 == 0 else s_p1).h, sp0 + 16 * (hc // 2 + 1))
                        sl = (hc % 2) * 2048
                        for tl in range(2):
                            tix = tp * 2 + tl
                            pso = ps_o0 if tl == 0 else ps_o1
                            for fg in range(4):
                                ins = tn.matmul(pso[:, fg * 512:(fg + 1) * 512],
                                                g_sb[:, hc * TC + tix * 128: hc * TC + tix * 128 + 128],
                                                w2q_t[:, sl + fg * 512: sl + fg * 512 + 512],
                                                start=(hc == 0), stop=(hc == NH - 1))
                            ins.then_inc(s_t.h, 1)
                    s_t.v += 2 * NH

                @blk.vector
                def _(vec):
                    vec.wait_ge(s_t.h, T0 + 2 * NH)
                    vec.wait_ge(s_in.h, I0 + 32)
                    for tl in range(2):
                        tix = tp * 2 + tl
                        pso = ps_o0 if tl == 0 else ps_o1
                        vec.scalar_tensor_tensor(y2[:, tl * 2048:(tl + 1) * 2048],
                                                 pso[:, :], r2_tm[:, tix:tix + 1], w2s_bc[:, :],
                                                 op0=ALU.mult, op1=ALU.mult)
                        vec.tensor_add(y2[:, tl * 2048:(tl + 1) * 2048],
                                       y2[:, tl * 2048:(tl + 1) * 2048],
                                       yn_t[:, tl * 2048:(tl + 1) * 2048]).then_inc(s_v.h, 1)
                    for tl in range(2):
                        vec.reduce_sum(rowt["rm"][:, tl:tl + 1], y2[:, tl * 2048:(tl + 1) * 2048],
                                       axis=AX.X).then_inc(s_v.h, 1)
                    for tl in range(2):
                        vec.wait_ge(s_s.h, S0 + tl + 1)
                        vec.tensor_scalar(rowt["mn"][:, tl:tl + 1], rowt["rm"][:, tl:tl + 1],
                                          1.0 / E, None, op0=ALU.mult)
                    for tl in range(2):
                        vec.tensor_scalar(rowt["mq"][:, tl:tl + 1], rowt["ms"][:, tl:tl + 1],
                                          1.0 / E, None, op0=ALU.mult)
                    for tl in range(2):
                        vec.tensor_mul(rowt["m2"][:, tl:tl + 1], rowt["mn"][:, tl:tl + 1],
                                       rowt["mn"][:, tl:tl + 1])
                    for tl in range(2):
                        vec.tensor_sub(rowt["vr"][:, tl:tl + 1], rowt["mq"][:, tl:tl + 1],
                                       rowt["m2"][:, tl:tl + 1])
                    for tl in range(2):
                        ins = vec.tensor_scalar(rowt["ve"][:, tl:tl + 1], rowt["vr"][:, tl:tl + 1],
                                                1e-5, None, op0=ALU.add)
                    ins.then_inc(s_v.h, 1)
                    s_v.v += 2 + 2 + 1
                    vec.wait_ge(s_s.h, S0 + 4)
                    for tl in range(2):
                        vec.reciprocal(rowt["r0"][:, tl:tl + 1], rowt["sd"][:, tl:tl + 1])
                    for tl in range(2):
                        vec.tensor_mul(rowt["na"][:, tl:tl + 1], rowt["r0"][:, tl:tl + 1],
                                       rowt["r0"][:, tl:tl + 1])
                    for tl in range(2):
                        vec.tensor_mul(rowt["nb"][:, tl:tl + 1], rowt["na"][:, tl:tl + 1],
                                       rowt["ve"][:, tl:tl + 1])
                    for tl in range(2):
                        vec.tensor_scalar(rowt["nc_"][:, tl:tl + 1], rowt["nb"][:, tl:tl + 1],
                                          -0.5, 1.5, op0=ALU.mult, op1=ALU.add)
                    for tl in range(2):
                        vec.tensor_mul(rowt["rs"][:, tl:tl + 1], rowt["r0"][:, tl:tl + 1],
                                       rowt["nc_"][:, tl:tl + 1])
                    vec.engine_nop().then_inc(s_v.h, 1)
                    s_v.v += 1
                    for tl in range(2):
                        vec.tensor_scalar(y2[:, tl * 2048:(tl + 1) * 2048],
                                          y2[:, tl * 2048:(tl + 1) * 2048],
                                          rowt["mn"][:, tl:tl + 1], rowt["rs"][:, tl:tl + 1],
                                          op0=ALU.subtract, op1=ALU.mult).then_inc(s_v.h, 1)
                    s_v.v += 2

                @blk.scalar
                def _(sc):
                    for tl in range(2):
                        sc.wait_ge(s_v.h, V0 + tl + 1)
                        sc.activation(scr[:, :], y2[:, tl * 2048:(tl + 1) * 2048], AF.Square,
                                      accum_out=rowt["ms"][:, tl:tl + 1]).then_inc(s_s.h, 1)
                    for tl in range(2):
                        sc.wait_ge(s_v.h, V0 + 4 + 1)
                        sc.activation(rowt["sd"][:, tl:tl + 1], rowt["ve"][:, tl:tl + 1],
                                      AF.Sqrt).then_inc(s_s.h, 1)
                    s_s.v += 4
    e_stack.close()

    top.close()
    return nc


def _host_prep(x, wq, wk, wv, wo, w1, w2):
    ins = {}
    ins["wqT"] = np.ascontiguousarray(wq.T)
    ins["wkT"] = np.ascontiguousarray(wk.T)
    ins["wvT"] = np.ascontiguousarray(wv.T)
    ins["woT"] = np.ascontiguousarray(wo.T)
    ins["w1T"] = np.ascontiguousarray(w1.T)
    ins["w2T"] = np.ascontiguousarray(w2.T)

    import jax
    import jax.numpy as jnp
    cpu = jax.devices("cpu")[0]
    with jax.default_device(cpu):
        inv_j = 1.0 / (1000000.0 ** (jnp.arange(0, D, 2, dtype=jnp.float32) / D))
        fang = jnp.arange(T, dtype=jnp.float32)[:, None] * inv_j[None, :]
        cos32 = np.asarray(jnp.concatenate([jnp.cos(fang), jnp.cos(fang)], axis=-1))
        sin32 = np.asarray(jnp.concatenate([jnp.sin(fang), jnp.sin(fang)], axis=-1))
        sf = np.float64(np.asarray(jnp.sqrt(jnp.float32(D))))
    cos = cos32.astype(np.float64).T
    sin = sin32.astype(np.float64).T
    ss = sin.copy()
    ss[0:64] = -sin[0:64]
    sc = 1.0 / sf

    def hl(a):
        hi = a.astype(np.float32)
        lo = (a - hi.astype(np.float64)).astype(np.float32)
        return np.ascontiguousarray(hi), np.ascontiguousarray(lo)

    import ml_dtypes
    ins["identb"] = np.eye(128, dtype=np.float16)
    ins["identf"] = np.eye(128, dtype=np.float32)
    percore = []
    for c in range(8):
        b, half = c // 2, c % 2
        toks = np.concatenate([np.arange((2 * i + half) * 128, (2 * i + half + 1) * 128)
                               for i in range(8)])
        d = dict(ins)
        d["xT"] = np.ascontiguousarray(x[b].T)
        d["xTm"] = np.ascontiguousarray(x[b].T[:, toks])
        d["coskhi"], d["cosklo"] = hl(cos)
        d["sskhi"], d["ssklo"] = hl(ss)
        d["cosqhi"], d["cosqlo"] = hl(cos[:, toks] * sc)
        d["ssqhi"], d["ssqlo"] = hl(ss[:, toks] * sc)
        mk = np.zeros((128, NPAIR * 4 * 256), dtype=np.float32)
        for j in range(NPAIR):
            nk = 4 * j + 4
            for ci_, kc in enumerate(range(nk - 4, nk)):
                for pp in range(2):
                    pos = 2 * j + pp
                    qt = (2 * pos + half) * 128 + np.arange(128)
                    kt = kc * 128 + np.arange(128)
                    blkm = (qt[None, :] >= kt[:, None]).astype(np.float32)
                    mk[:, (j * 4 + ci_) * 256 + pp * 128:(j * 4 + ci_) * 256 + (pp + 1) * 128] = blkm
        d["maskp"] = mk.astype(np.float16)
        percore.append(d)
    return percore


def kernel(L0_attn_x, L0_attn_wq, L0_attn_wk, L0_attn_wv, L0_attn_wo, L0_ffn_w1, L0_ffn_w2):
    x = np.ascontiguousarray(np.asarray(L0_attn_x, dtype=np.float32))
    args = [np.asarray(a, dtype=np.float32) for a in
            (L0_attn_wq, L0_attn_wk, L0_attn_wv, L0_attn_wo, L0_ffn_w1, L0_ffn_w2)]
    in_maps = _host_prep(x, *args)
    nc = build_program()
    res = run_bass_kernel_spmd(nc, in_maps, core_ids=list(range(8)))
    out = np.empty((B, T, E), np.float32)
    for c in range(8):
        o = res.results[c]["out"]
        half = c % 2
        for i in range(8):
            t0_ = (2 * i + half) * 128
            out[c // 2, t0_:t0_ + 128, :] = o[i * 128:(i + 1) * 128, :]
    return out
